# version 29
# baseline (speedup 1.0000x reference)
import numpy as np
import ml_dtypes
import concourse.bass as bass
import concourse.mybir as mybir
from concourse.bass_utils import run_bass_kernel_spmd

F32 = mybir.dt.float32
BF16 = mybir.dt.bfloat16
I32 = mybir.dt.int32
AF = mybir.ActivationFunctionType
OP = mybir.AluOpType
f32 = np.float32
bf16 = ml_dtypes.bfloat16

BATCH, S, DM, NH, HD = 4, 8192, 1024, 16, 64
FH = 3072
HALF = S // 2
NT = 512
EPS = 1e-5
C_A = 8.0
C_B = 16.0
DSC = 1.0 / 127.0
VSLOT = 130

ENG = ("sync", "tensor", "vector", "scalar", "gpsimd")
KB = 1024


class Plan:
    def __init__(self):
        self.ops = {e: [] for e in ENG}
        self.cnt = {}

    def sem(self, name):
        self.cnt[name] = 0

    def op(self, eng, fn, wait=(), inc=()):
        by = {}
        for s, v in wait:
            by[s] = max(v, by.get(s, 0))
        wait = tuple(sorted(by.items()))
        for s, v in wait:
            assert v <= self.cnt[s], f"wait on future: {s} {v} > {self.cnt[s]}"
        toks = []
        for s, n in inc:
            self.cnt[s] += n
            toks.append((s, self.cnt[s]))
        self.ops[eng].append((fn, wait, tuple(inc)))
        return tuple(toks)

    def mark(self, eng, inc):
        fn, waits, incs = self.ops[eng][-1]
        toks = []
        for s, n in inc:
            self.cnt[s] += n
            toks.append((s, self.cnt[s]))
        self.ops[eng][-1] = (fn, waits, incs + tuple(inc))
        return tuple(toks)

    def dma(self, dst, src, sem="ld", wait=(), eng="sync"):
        return self.op(eng, lambda e, d=dst, s=src: e.dma_start(out=d, in_=s),
                       wait=wait, inc=[(sem, 16)])


def _replay(block, plan, sems):
    def mk(ops):
        def body(eng):
            for fn, waits, incs in ops:
                for s, v in waits:
                    eng.wait_ge(sems[s], v)
        return body

    for e in ENG:
        ops = plan.ops[e]

        def body(eng, ops=ops):
            for fn, waits, incs in ops:
                for s, v in waits:
                    eng.wait_ge(sems[s], v)
                inst = fn(eng)
                if incs:
                    if inst is None:
                        inst = eng.nop()
                    for s, n in incs:
                        inst = inst.then_inc(sems[s], n)
        getattr(block, e)(body)


def _build(taps=False, phases="ABCD", na=16, nb=16, nd=8, asub=7):
    nc = bass.Bass("TRN2", target_bir_lowering=False, debug=False,
                   detect_race_conditions=False)

    def ein(name, shape, dtype):
        return nc.dram_tensor(name, list(shape), dtype, kind="ExternalInput").ap()

    x0T = ein("x0T", (DM, S), F32)
    waqT = ein("waqT", (DM, DM), F32)
    wakT = ein("wakT", (DM, DM), F32)
    wavT = ein("wavT", (DM, DM), F32)
    waoT = ein("waoT", (DM, DM), F32)
    acols = ein("acols", (128, 48), F32)
    bvrow = ein("bvrow", (1, DM), F32)
    r128 = ein("r128", (128, 128), BF16)
    cosT = ein("cosT", (128, S), F32)
    sinT = ein("sinT", (128, S), F32)
    maskA = ein("maskA", (128, 2048), BF16)
    maskA0 = ein("maskA0", (128, 2048), BF16)
    wbq = ein("wbq", (DM, DM), F32)
    wbk = ein("wbk", (DM, DM), F32)
    wbv = ein("wbv", (DM, DM), F32)
    wbo = ein("wbo", (DM, DM), F32)
    wgT = ein("wgT", (DM, FH), I32)
    wuT = ein("wuT", (DM, FH), I32)
    wdT = ein("wdT", (FH, DM), I32)
    flags = ein("flags", (128, 4), F32)
    flagsb = ein("flagsb", (128, 4), BF16)
    shmat = ein("shmat", (128, 128), BF16)

    outT = nc.dram_tensor("outT", [DM, HALF], F32, kind="ExternalOutput").ap()
    ikind = "ExternalOutput" if taps else "Internal"
    x1d = nc.dram_tensor("x1d", [DM, S], F32, kind=ikind).ap()
    x2d = nc.dram_tensor("x2d", [DM, S], F32, kind=ikind).ap()
    kd = nc.dram_tensor("kd", [DM, S], BF16, kind=ikind).ap()
    vd = nc.dram_tensor("vd", [S, DM], BF16, kind=ikind).ap()
    qd = nc.dram_tensor("qd", [DM, S], BF16, kind=ikind).ap()
    od = nc.dram_tensor("od", [DM, HALF], BF16,
                        kind="ExternalOutput" if taps else "Internal").ap()
    wgbf = nc.dram_tensor("wgbf", [24, 128, 1024], BF16, kind="Internal").ap()
    wubf = nc.dram_tensor("wubf", [24, 128, 1024], BF16, kind="Internal").ap()
    wdbf = nc.dram_tensor("wdbf", [8, 128, 3072], BF16, kind="Internal").ap()

    LIMIT = 212832
    SB_BASE = 16512

    def sb(name, shape, dtype, off):
        nb = int(np.prod(shape[1:])) * mybir.dt.size(dtype)
        assert off >= 0 and off + nb <= LIMIT, (name, off, nb)
        assert SB_BASE + off + nb <= nc.sbuf_top, (name, off, nb)
        return nc.alloc_sbuf_tensor_at(name, list(shape), dtype,
                                       offset=SB_BASE + off)

    t_cols = sb("cols", [128, 48], F32, 0)
    t_onesb = sb("onesb", [128, 16], BF16, 192)
    t_onesr = sb("onesr", [1, 128], F32, 224)
    t_r128 = sb("r128s", [128, 128], BF16, 736)
    t_flags = sb("flagss", [128, 4], F32, 992)
    t_flagsb = sb("flagsbs", [128, 4], BF16, 1024)
    t_rows = sb("rowsP", [1, 2048], F32, 1056)
    t_maskA = sb("maskAs", [128, 2048], BF16, 9248)
    t_maskA0 = sb("maskA0s", [128, 2048], BF16, 13344)
    t_bvbc = sb("bvbc", [128, DM], F32, 17440)
    t_shift = sb("shifts", [128, 128], BF16, 21536)
    PH = 22 * KB

    o = PH
    a_w = sb("a_w", [128, 4 * 8192], BF16, o); o += 64 * KB
    a_x0 = sb("a_x0", [128, 8 * NT], F32, o); o += 16 * KB
    a_xb = sb("a_xb", [128, 2 * NT], BF16, o); o += 2 * KB
    a_sq = sb("a_sq", [128, 2 * NT], BF16, o); o += 2 * KB
    a_xn = sb("a_xn", [128, 8 * NT], BF16, o); o += 8 * KB
    a_abc = sb("a_abc", [128, NT], F32, o); o += 2 * KB
    a_bbc = sb("a_bbc", [128, NT], F32, o); o += 2 * KB
    a_tmp = sb("a_tmp", [128, NT], F32, o); o += 2 * KB
    a_qf = sb("a_qf", [128, NT], F32, o); o += 2 * KB
    a_qb = sb("a_qb", [128, NT], BF16, o); o += 1 * KB
    a_qpf = sb("a_qpf", [128, NT], F32, o); o += 2 * KB
    a_t1 = sb("a_t1", [128, NT], F32, o); o += 2 * KB
    a_t2 = sb("a_t2", [128, NT], F32, o); o += 2 * KB
    a_qr = sb("a_qr", [128, 8 * NT], BF16, o); o += 8 * KB
    a_cos = sb("a_cos", [128, NT], F32, o); o += 2 * KB
    a_sin = sb("a_sin", [128, NT], F32, o); o += 2 * KB
    a_kring = sb("a_kring", [128, 8 * 8 * 128], BF16, o); o += 16 * KB
    a_vring = sb("a_vring", [128, 8 * 8 * VSLOT], BF16, o); o += 8 * 8 * VSLOT * 2
    a_p = sb("a_p", [128, 2048], BF16, o); o += 4 * KB
    a_pm = sb("a_pm", [128, 2 * 2048], BF16, o); o += 8 * KB
    a_oa = sb("a_oa", [128, 8 * NT], BF16, o); o += 8 * KB
    a_rb = sb("a_rb", [128, NT], F32, o); o += 2 * KB
    a_x1 = sb("a_x1", [128, 2 * NT], F32, o); o += 4 * KB
    a_lnx = sb("a_lnx", [128, NT], F32, o); o += 2 * KB
    a_xr0 = sb("a_xr0", [128, NT], F32, o); o += 2 * KB
    assert o <= LIMIT, o

    i_wstage = sb("i_wstage", [128, 12288], I32, LIMIT - 48 * KB)
    i_wcvt = sb("i_wcvt", [128, 12288], BF16, LIMIT - 72 * KB)
    i_astage = sb("i_astage", [128, 8192], F32, LIMIT - 104 * KB)

    o = PH
    b_w = sb("b_w", [128, 3 * 8192], BF16, o); o += 48 * KB
    b_x1 = sb("b_x1", [128, 2 * 8 * NT], F32, o); o += 32 * KB
    b_xb = sb("b_xb", [128, 2 * NT], BF16, o); o += 2 * KB
    b_sq = sb("b_sq", [128, 2 * NT], BF16, o); o += 2 * KB
    b_xn = sb("b_xn", [128, 2 * 8 * NT], BF16, o); o += 16 * KB
    b_abc = sb("b_abc", [128, NT], F32, o); o += 2 * KB
    b_bbc = sb("b_bbc", [128, NT], F32, o); o += 2 * KB
    b_tmp = sb("b_tmp", [128, NT], F32, o); o += 2 * KB
    b_wg = sb("b_wg", [128, 2 * 1024], BF16, o); o += 4 * KB
    b_wu = sb("b_wu", [128, 2 * 1024], BF16, o); o += 4 * KB
    b_sg = sb("b_sg", [128, 2 * NT], BF16, o); o += 2 * KB
    b_h = sb("b_h", [128, 24 * NT], BF16, o); o += 24 * KB
    b_wd = sb("b_wd", [128, 2 * 3072], BF16, o); o += 12 * KB
    b_x2 = sb("b_x2", [128, 8 * NT], F32, o); o += 16 * KB
    b_xn3 = sb("b_xn3", [128, 8 * NT], BF16, o); o += 8 * KB
    b_xn3b = sb("b_xn3b", [128, 8 * NT], BF16, o); o += 8 * KB
    assert o <= LIMIT, o
    b_rows2 = sb("b_rows2", [1, 2048], F32, 9248)
    b_kv = sb("b_kv", [128, 4 * NT], BF16, 17440)
    b_fstage = sb("b_fstage", [128, 8192], F32,
                  PH + 48 * KB + 16 * KB + 2 * KB + 2 * KB + 8 * KB
                  + 2 * KB + 2 * KB + 2 * KB + 4 * KB + 4 * KB + 1 * KB)

    o = PH
    c_q = sb("c_q", [128, 8 * HALF], BF16, o); o += 64 * KB
    C_O_OFF = o
    c_o = sb("c_o", [128, 8 * HALF], BF16, o); o += 64 * KB
    c_wo = sb("c_wo", [128, 8192], BF16, o); o += 16 * KB
    C_K_OFF = o
    c_st0 = sb("c_st0", [128, 8 * NT], BF16, C_K_OFF)
    c_st1 = sb("c_st1", [128, 8 * NT], BF16, C_K_OFF + 8 * KB)
    c_k = sb("c_k", [128, S], BF16, C_K_OFF); o += 16 * KB
    c_ve = sb("c_ve", [128, 64 * 65], BF16, o); o += 64 * 65 * 2
    c_vo = sb("c_vo", [128, 64 * 65], BF16, o); o += 64 * 65 * 2
    c_p = sb("c_p", [128, 2 * 2048], BF16, o); o += 8 * KB
    assert o <= LIMIT, o
    c_rb = sb("c_rb", [128, NT], F32, 17440)
    c_lnx = sb("c_lnx", [128, NT], F32, 13344)
    c_ob = sb("c_ob", [128, NT], BF16, 15392)
    c_fstage = sb("c_fstage", [128, 8192], F32, C_O_OFF)

    o = PH
    d_x3 = sb("d_x3", [128, 8 * NT], F32, o); o += 16 * KB
    d_xs0 = sb("d_xs0", [128, 8 * NT], F32, o); o += 16 * KB
    d_xn = sb("d_xn", [128, 8 * NT], BF16, o); o += 8 * KB
    d_xb = sb("d_xb", [128, 2 * NT], BF16, o); o += 2 * KB
    d_sq = sb("d_sq", [128, 2 * NT], BF16, o); o += 2 * KB
    d_abc = sb("d_abc", [128, NT], F32, o); o += 2 * KB
    d_bbc = sb("d_bbc", [128, NT], F32, o); o += 2 * KB
    d_tmp = sb("d_tmp", [128, NT], F32, o); o += 2 * KB
    d_sg = sb("d_sg", [128, 2 * NT], BF16, o); o += 2 * KB
    d_out = sb("d_out", [128, 2 * NT], F32, o); o += 4 * KB
    d_wg = sb("d_wg", [128, 2 * 1024], BF16, o); o += 4 * KB
    d_wu = sb("d_wu", [128, 2 * 1024], BF16, o); o += 4 * KB
    assert o <= PH + 64 * KB, o
    o2 = C_K_OFF
    d_h = sb("d_h", [128, 24 * NT], BF16, o2); o2 += 24 * KB
    d_wd = sb("d_wd", [128, 2 * 3072], BF16, o2); o2 += 12 * KB
    assert o2 <= LIMIT, o2
    d_xs1a = sb("d_xs1a", [128, 6 * NT], F32, 9248)
    d_xs1b = sb("d_xs1b", [128, 2 * NT], F32, C_K_OFF + 36 * KB)

    def dxs1(dd):
        if dd < 6:
            return d_xs1a[:, dd * NT:(dd + 1) * NT]
        return d_xs1b[:, (dd - 6) * NT:(dd - 5) * NT]

    sc4 = nc.alloc_psum_tensor("sc4", [128, 2048], F32)
    pvE = nc.alloc_psum_tensor("pvE", [128, 512], F32)
    pvO = nc.alloc_psum_tensor("pvO", [128, 512], F32)
    pAb = nc.alloc_psum_tensor("pAb", [128, 512], F32)
    pBb = nc.alloc_psum_tensor("pBb", [128, 512], F32)

    p = Plan()
    for s in ("ld", "wv", "mm", "ve", "ac", "gp", "s2", "s3", "fin"):
        p.sem(s)
    for s in ("x1w0", "x1w1"):
        p.sem(s)
    for i in range(8):
        p.sem(f"x2w{i}")
    for i in range(4):
        p.sem(f"kvw{i}")
    p.sem("ow0"); p.sem("ow1")
    p.sem("xr0"); p.sem("xr1")
    for s in ("wvg0", "wvg1", "wvd0", "wvd1"):
        p.sem(s)
    for s in ("wva0", "wva1", "wvf0", "wvf1", "s30", "s31"):
        p.sem(s)

    G = {"sc": (), "sc0": (), "sc1": (), "pvE": (), "pvO": (),
         "pA": (), "pB": (), "lnb": (), "rows": (), "rows2": (), "abc": ()}

    p.dma(t_cols[:], acols)
    p.dma(t_r128[:], r128)
    p.dma(t_maskA[:], maskA)
    p.dma(t_maskA0[:], maskA0)
    p.dma(t_flags[:], flags)
    tok_cfl = p.dma(t_flagsb[:], flagsb)
    p.dma(t_shift[:], shmat)
    p.op("vector", lambda e: e.memset(t_onesb[:], 1.0))
    p.op("vector", lambda e: e.memset(t_onesr[:], 1.0))
    tok_ones = p.mark("vector", [("ve", 1)])

    tok_bvr = p.dma(i_astage[0:1, 0:DM], bvrow)
    for j in range(2):
        p.op("tensor", lambda e, j=j: e.matmul(
            out=sc4[:, j * 512:(j + 1) * 512], lhsT=t_onesr[0:1, :],
            rhs=i_astage[0:1, j * 512:(j + 1) * 512], start=True, stop=True),
            wait=tok_bvr + tok_ones if j == 0 else ())
    tok_bvm = p.mark("tensor", [("mm", 1)])
    tok_bvc = p.op("vector", lambda e: e.tensor_copy(
        out=t_bvbc[:], in_=sc4[:, 0:1024]), wait=tok_bvm, inc=[("ve", 1)])
    G["sc"] = tok_bvc

    cvt_done = {}
    for u in range(8):
        wi, h = u // 2, u % 2
        wsrc = (waqT, wakT, wavT, waoT)[wi]
        sl = (u % 2) * 4096
        tok_l = p.dma(i_astage[:, sl:sl + 4096]
                      .rearrange("p (dc f) -> p dc f", dc=4),
                      wsrc.rearrange("(dc p) f -> p dc f", p=128)
                      [:, h * 4:h * 4 + 4, :],
                      sem=f"wva{u % 2}", wait=(tok_bvc if u == 0 else ())
                      + cvt_done.get(u - 2, ()),
                      eng=("gpsimd", "scalar")[u % 2])
        cvt_done[u] = p.op("vector", lambda e, wi=wi, h=h, sl=sl: e.tensor_copy(
            out=a_w[:, wi * 8192 + h * 4096:wi * 8192 + h * 4096 + 4096],
            in_=i_astage[:, sl:sl + 4096]),
            wait=tok_l, inc=[("ve", 1)])
    tok_aw = cvt_done[6] + cvt_done[7]

    units = []
    for wsrc, wdst, n_in, n_out in ((wgT, wgbf, 8, 24), (wuT, wubf, 8, 24),
                                    (wdT, wdbf, 24, 8)):
        for ic in range(n_in):
            units.append((wsrc, wdst, ic, n_out))
    f_cvt = {}
    f_store = {}
    for u, (wsrc, wdst, ic, n_out) in enumerate(units):
        sl = (u % 2) * 6144
        nh = n_out * 128 // 2
        tok_l = ()
        for h in range(2):
            tok_l = tok_l + p.dma(
                i_wstage[:, sl + h * nh:sl + (h + 1) * nh],
                wsrc[ic * 128:(ic + 1) * 128, h * nh:(h + 1) * nh],
                sem=f"wvf{u % 2}", wait=f_cvt.get(u - 2, ()),
                eng=("gpsimd", "scalar")[h])
        tok_c = p.op("vector", lambda e, sl=sl, n=n_out: e.tensor_copy(
            out=i_wcvt[:, sl:sl + n * 128], in_=i_wstage[:, sl:sl + n * 128]),
            wait=tok_l + f_store.get(u - 2, ()), inc=[("ve", 1)])
        f_cvt[u] = tok_c
        tok_s = ()
        stp = 8
        for o0 in range(0, n_out, stp):
            tok_s = tok_s + p.dma(
                wdst[o0:o0 + stp, :, ic * 128:(ic + 1) * 128]
                .rearrange("oc p f -> p oc f"),
                i_wcvt[:, sl + o0 * 128:sl + (o0 + stp) * 128]
                .rearrange("p (oc f) -> p oc f", f=128),
                sem=f"s3{u % 2}", wait=tok_c if o0 == 0 else ())
        f_store[u] = tok_s
    tok_wcache = f_store[len(units) - 1] + f_store[len(units) - 2]

    p.op("vector", lambda e: e.memset(a_kring[:], 0.0),
         wait=tok_wcache + tok_aw)
    p.op("vector", lambda e: e.memset(a_vring[:], 0.0))
    p.op("vector", lambda e: e.memset(
        a_vring[:].rearrange("p (s f) -> p s f", f=VSLOT)[:, :, 64:65], 1.0))
    p.op("vector", lambda e: e.memset(
        a_vring[:].rearrange("p (s f) -> p s f", f=VSLOT)[:, :, 129:130], 1.0))
    tok_ring = p.mark("vector", [("ve", 1)])

    def ln_stats(xF, xbT, sqT, wait_first, bf_in=False):
        s1 = sc4[:, 512:1024]
        s2 = sc4[:, 1024:1536]
        mm_toks = {}
        for dc in range(8):
            sl = (dc % 2) * NT
            w = wait_first if dc == 0 else ()
            w = w + mm_toks.get(dc - 2, ())
            if bf_in:
                tsq = p.op("gpsimd", lambda e, d=dc, s=sl: e.tensor_mul(
                    out=sqT[:, s:s + NT], in0=xF(d), in1=xF(d)),
                    wait=w, inc=[("gp", 1)])
                srcap = xF(dc)
                tdat = tsq
            else:
                tdat = p.op("vector", lambda e, d=dc, s=sl: e.tensor_copy(
                    out=xbT[:, s:s + NT], in_=xF(d)), wait=w, inc=[("ve", 1)])
                tsq = p.op("gpsimd", lambda e, d=dc, s=sl: e.tensor_mul(
                    out=sqT[:, s:s + NT], in0=xbT[:, s:s + NT],
                    in1=xbT[:, s:s + NT]), wait=tdat, inc=[("gp", 1)])
                srcap = xbT[:, sl:sl + NT]
            wmm = (wait_first + G["lnb"] + G["sc"]) if dc == 0 else ()
            p.op("tensor", lambda e, d=dc, s=srcap: e.matmul(
                out=s1[0:1, 0:NT], lhsT=t_onesb[:, 0:1], rhs=s,
                start=(d == 0), stop=(d == 7)), wait=wmm + tdat)
            tok2 = p.op("tensor", lambda e, d=dc, s=(dc % 2) * NT: e.matmul(
                out=s2[0:1, 0:NT], lhsT=t_onesb[:, 0:1], rhs=sqT[:, s:s + NT],
                start=(d == 0), stop=(d == 7)), wait=tsq, inc=[("mm", 1)])
            mm_toks[dc] = tok2
        return mm_toks[7]

    def ln_rows(tok_stats, rows=None, rkey="rows"):
        rows = t_rows if rows is None else rows
        s1 = sc4[:, 512:1024]
        s2 = sc4[:, 1024:1536]
        rmean = rows[0:1, 0:NT]
        rvar = rows[0:1, NT:2 * NT]
        rsd = rows[0:1, 2 * NT:3 * NT]
        rb_ = rows[0:1, 3 * NT:4 * NT]
        p.op("vector", lambda e: e.tensor_scalar_mul(
            out=rmean, in0=s1[0:1, 0:NT], scalar1=1.0 / DM),
            wait=tok_stats + G[rkey])
        p.op("vector", lambda e: e.tensor_mul(out=rvar, in0=rmean, in1=rmean))
        p.op("vector", lambda e: e.scalar_tensor_tensor(
            out=rvar, in0=s2[0:1, 0:NT], scalar=1.0 / DM, in1=rvar,
            op0=OP.mult, op1=OP.subtract))
        tok_v = p.mark("vector", [("ve", 1)])
        G["lnb"] = tok_v
        p.op("scalar", lambda e: e.activation(
            out=rsd, in_=rvar, func=AF.Ln, bias=f32(EPS)), wait=tok_v)
        tok_sd = p.op("scalar", lambda e: e.activation(
            out=rsd, in_=rsd, func=AF.Exp, scale=f32(-0.5)), inc=[("ac", 1)])
        p.op("vector", lambda e: e.scalar_tensor_tensor(
            out=rb_, in0=rmean, scalar=-1.0, in1=rsd, op0=OP.mult, op1=OP.mult),
            wait=tok_sd)
        return p.mark("vector", [("ve", 1)])

    def ln_norm(xF, xnO, abcT, bbcT, tmpT, tok_rows, norm_wait=(),
                rows=None, rkey="rows", neng="gpsimd", tmp_bank3=False,
                abanks=False):
        rows = t_rows if rows is None else rows
        rsd = rows[0:1, 2 * NT:3 * NT]
        rb_ = rows[0:1, 3 * NT:4 * NT]
        if abanks:
            bA = sc4[:, 0:512]
            bB = sc4[:, 512:1024]
            wA = tok_rows + G["sc"] + G["sc0"] + G["sc1"]
        else:
            bA = sc4[:, 1536:2048]
            bB = pvE[:, 0:NT]
            wA = tok_rows + G["pvE"]
        p.op("tensor", lambda e: e.matmul(
            out=bA, lhsT=t_onesr[0:1, :], rhs=rsd,
            start=True, stop=True), wait=wA)
        tok_bc = p.op("tensor", lambda e: e.matmul(
            out=bB, lhsT=t_onesr[0:1, :], rhs=rb_,
            start=True, stop=True), inc=[("mm", 1)])
        G[rkey] = tok_bc
        p.op("vector", lambda e, bA=bA: e.tensor_copy(out=abcT[:, 0:NT],
                                                      in_=bA),
             wait=tok_bc + G["abc"])
        tok_cp = p.op("vector", lambda e, bB=bB: e.tensor_copy(
            out=bbcT[:, 0:NT], in_=bB), inc=[("ve", 1)])
        if tmp_bank3:
            tmpT = sc4[:, 1536:2048]
        nsem = {"gpsimd": "gp", "vector": "ve"}[neng]
        for dc in range(8):
            p.op(neng, lambda e, d=dc: e.tensor_mul(
                out=tmpT[:, 0:NT], in0=xF(d), in1=abcT[:, 0:NT]),
                wait=(tok_cp + norm_wait) if dc == 0 else ())
            p.op(neng, lambda e, d=dc: e.tensor_add(
                out=xnO(d), in0=tmpT[:, 0:NT], in1=bbcT[:, 0:NT]))
        tok_n = p.mark(neng, [(nsem, 1)])
        G["abc"] = tok_n
        if abanks:
            G["sc0"] = tok_cp
            G["sc1"] = tok_cp
            G["lnb"] = G["lnb"] + tok_cp
        else:
            G["pvE"] = tok_n if tmp_bank3 else tok_cp
        return tok_n

    def layer_norm(xF, xnO, xbT, sqT, abcT, bbcT, tmpT, wait_first,
                   bf_in=False, norm_wait=()):
        tok_stats = ln_stats(xF, xbT, sqT, wait_first, bf_in=bf_in)
        tok_rows = ln_rows(tok_stats)
        tok_n = ln_norm(xF, xnO, abcT, bbcT, tmpT, tok_rows,
                        norm_wait=norm_wait)
        return tok_n, ()

    tok_attn_done = tok_ring
    tok_x1_dma = {}
    tok_qf_free = ()
    tok_qpf_free = ()
    tok_qb_free = ()
    tok_t2_free = ()
    tok_sum = ()
    a_pre = {}
    a_xn_free = ()
    a_qk_last = ()
    a_stt = {}

    def emit_SA(ti):
        t0 = ti * NT
        if ti == 0:
            wt = tok_aw + tok_ring
        else:
            prev = a_pre[ti - 1]
            wt = prev["norm"] + prev["stats"] + a_qk_last
        p.dma(a_cos[:], cosT[:, t0:t0 + NT], wait=wt)
        p.dma(a_sin[:], sinT[:, t0:t0 + NT])
        tok_x0 = p.dma(a_x0[:].rearrange("p (dc t) -> p dc t", dc=8),
                       x0T.rearrange("(dc p) t -> p dc t", p=128)
                       [:, :, t0:t0 + NT])

        def ax(d):
            return a_x0[:, d * NT:(d + 1) * NT]

        wf = tok_x0 + (tok_attn_done if ti == 0 else ())
        tok_stats = ln_stats(ax, a_xb, a_sq, wf)
        tok_rows = ln_rows(tok_stats)
        a_pre[ti] = {"ax": ax, "stats": tok_stats, "rows": tok_rows,
                     "norm": ()}

    emit_SA(0)
    for ti in range(na):
        t0 = ti * NT
        b0 = 4 * ti
        s0 = b0 % 8
        pre = a_pre[ti]
        ax = pre["ax"]
        tok_ln = ln_norm(ax, lambda d: a_xn[:, d * NT:(d + 1) * NT],
                         a_abc, a_bbc, a_tmp, pre["rows"],
                         norm_wait=a_xn_free, abanks=True)
        pre["norm"] = tok_ln
        tok_cs = pre["rows"]

        if asub < 2:
            continue
        pend_rope = None

        def emit_ropetail():
            nonlocal pend_rope, tok_qb_free, tok_qpf_free, tok_qf_free, \
                tok_t2_free, tok_sum
            if pend_rope is None:
                return
            bank2s, bname2, tok_ev, tok_cast, dst, wt_extra = pend_rope
            pend_rope = None
            tok_pm = p.op("tensor", lambda e, b2=bank2s: e.matmul(
                out=b2, lhsT=t_r128[:], rhs=a_qb[:],
                start=True, stop=True),
                wait=tok_cast + G[bname2], inc=[("mm", 1)])
            tok_qb_free = tok_pm
            tok_evp = p.op("scalar", lambda e, b2=bank2s: e.copy(
                out=a_qpf[:], in_=b2),
                wait=tok_pm + tok_qpf_free, inc=[("ac", 1)])
            G[bname2] = tok_evp
            tok_t1 = p.op("vector", lambda e: e.tensor_mul(
                out=a_t1[:], in0=a_qf[:], in1=a_cos[:]),
                wait=tok_ev + tok_cs, inc=[("ve", 1)])
            tok_qf_free = tok_t1
            tok_t2 = p.op("gpsimd", lambda e: e.tensor_mul(
                out=a_t2[:], in0=a_qpf[:], in1=a_sin[:]),
                wait=tok_evp + tok_t2_free, inc=[("gp", 1)])
            tok_qpf_free = tok_t2
            tok_sum = p.op("vector", lambda e, d=dst: e.tensor_add(
                out=d, in0=a_t1[:], in1=a_t2[:]),
                wait=tok_t1 + tok_t2 + wt_extra, inc=[("ve", 1)])
            tok_t2_free = tok_sum

        for which in range(2):
            for oc in range(8):
                bank, bname = (pAb, "pA") if oc % 2 == 0 else (pBb, "pB")
                bank2s = sc4[:, (oc % 2) * 512:(oc % 2) * 512 + 512]
                bname2 = "sc0" if oc % 2 == 0 else "sc1"
                for dc in range(8):
                    wt = ()
                    if dc == 0:
                        wt = G[bname]
                        if oc == 0 and which == 0:
                            wt = wt + tok_ln
                    p.op("tensor", lambda e, d=dc, o_=oc, b=bank, w=which * 8192:
                         e.matmul(out=b[:, 0:NT],
                                  lhsT=a_w[:, w + d * 1024 + o_ * 128:
                                           w + d * 1024 + o_ * 128 + 128],
                                  rhs=a_xn[:, d * NT:(d + 1) * NT],
                                  start=(d == 0), stop=(d == 7)), wait=wt)
                tok_mm = p.mark("tensor", [("mm", 1)])
                emit_ropetail()
                bcol = (0 if which == 0 else 8) + oc
                tok_ev = p.op("scalar", lambda e, b=bank, bc=bcol: e.activation(
                    out=a_qf[:], in_=b[:, 0:NT], func=AF.Identity,
                    bias=t_cols[:, bc:bc + 1], scale=f32(1.0)),
                    wait=tok_mm + tok_qf_free, inc=[("ac", 1)])
                tok_cast = p.op("vector", lambda e: e.tensor_copy(
                    out=a_qb[:], in_=a_qf[:]),
                    wait=tok_ev + tok_qb_free, inc=[("ve", 1)])
                G[bname] = tok_cast
                if which == 0:
                    dst = a_qr[:, oc * NT:(oc + 1) * NT]
                else:
                    dst = a_kring[:, oc * 1024 + s0 * 128:
                                  oc * 1024 + s0 * 128 + 512]
                wt_extra = tok_attn_done if which == 1 and oc == 0 else ()
                pend_rope = (bank2s, bname2, tok_ev, tok_cast, dst, wt_extra)
        emit_ropetail()
        tok_qk = tok_sum
        a_qk_last = tok_qk

        if asub < 3:
            continue
        for tb in range(4):
            slot = (s0 + tb) % 8
            for fh in range(2):
                bank, bname = (pAb, "pA") if fh == 0 else (pBb, "pB")
                for dc in range(8):
                    wt = G[bname] if dc == 0 else ()
                    if dc == 0 and tb == 0 and fh == 0:
                        wt = wt + tok_attn_done
                    p.op("tensor", lambda e, d=dc, t=tb, f=fh, b=bank: e.matmul(
                        out=b[:, 0:512],
                        lhsT=a_xn[:, d * NT + t * 128: d * NT + t * 128 + 128],
                        rhs=a_w[:, 2 * 8192 + d * 1024 + f * 512:
                                 2 * 8192 + d * 1024 + f * 512 + 512],
                        start=(d == 0), stop=(d == 7)), wait=wt)
                tok_mm = p.mark("tensor", [("mm", 1)])
                tok_vv = ()
                for hh in range(8):
                    h = 8 * fh + hh
                    hp = h // 2
                    base = (slot * 8 + hp) * VSLOT
                    off = base + (0 if h % 2 == 0 else 65)
                    wt2 = tok_mm if hh == 0 else ()
                    tok_vv = p.op("vector", lambda e, b=bank, hh=hh, o_=off, f=fh:
                                  e.tensor_add(
                                      out=a_vring[:, o_:o_ + 64],
                                      in0=b[:, hh * 64:(hh + 1) * 64],
                                      in1=t_bvbc[:, f * 512 + hh * 64:
                                                 f * 512 + hh * 64 + 64]),
                                  wait=wt2, inc=[("ve", 1)] if hh == 7 else ())
                G[bname] = tok_vv
        tok_vdone = tok_vv
        a_xn_free = tok_mm
        if ti + 1 < na:
            emit_SA(ti + 1)

        if asub < 4:
            continue
        mask_t = t_maskA0 if ti == 0 else t_maskA
        tok_fin = tok_attn_done
        tok_qkm_h = {}
        tok_exp_h = {}
        tok_msk_h = {}
        tok_pvm_h = {}

        def a_qk(hp, wt0):
            cnt = 0
            for j in range(4):
                for ch in range(2):
                    slot = (b0 + j - 1 + ch) % 8
                    for eo in range(2):
                        kb = hp * 1024 + slot * 128
                        oo = eo * 1024 + j * 256 + ch * 128
                        wt = wt0 if cnt == 0 else ()
                        cnt += 1
                        p.op("tensor", lambda e, kb=kb, oo=oo, eo=eo, hp=hp, j=j:
                             e.matmul(out=sc4[:, oo:oo + 128],
                                      lhsT=a_kring[eo * 64:eo * 64 + 64,
                                                   kb:kb + 128],
                                      rhs=a_qr[eo * 64:eo * 64 + 64,
                                               hp * NT + j * 128:
                                               hp * NT + j * 128 + 128],
                                      start=True, stop=True), wait=wt)
            return p.mark("tensor", [("mm", 1)])

        def a_pv(hp):
            if asub < 5:
                return ()
            pb = (hp % 2) * 2048
            first = True
            for j in range(4):
                for ch in range(2):
                    slot = (b0 + j - 1 + ch) % 8
                    vb = (slot * 8 + hp) * VSLOT
                    wt = ()
                    if first:
                        wt = tok_msk_h[hp] + G["pvE"] + G["pvO"] + tok_fin
                        first = False
                    pe = a_pm[:, pb + j * 256 + ch * 128:
                              pb + j * 256 + ch * 128 + 128]
                    po = a_pm[:, pb + 1024 + j * 256 + ch * 128:
                              pb + 1024 + j * 256 + ch * 128 + 128]
                    p.op("tensor", lambda e, vb=vb, j=j, ch=ch, pe=pe: e.matmul(
                        out=pvE[0:65, j * 128:(j + 1) * 128],
                        lhsT=a_vring[:, vb:vb + 65],
                        rhs=pe, start=(ch == 0), stop=(ch == 1)), wait=wt)
                    p.op("tensor", lambda e, vb=vb, j=j, ch=ch, po=po: e.matmul(
                        out=pvO[0:65, j * 128:(j + 1) * 128],
                        lhsT=a_vring[:, vb + 65:vb + 130],
                        rhs=po, start=(ch == 0), stop=(ch == 1)))
            return p.mark("tensor", [("mm", 1)])

        def a_fin(hp):
            tok_pvm = tok_pvm_h[hp]
            p.op("vector", lambda e: e.tensor_copy(
                out=t_rows[0:1, 0:512], in_=pvE[64:65, :]), wait=tok_pvm)
            tok_r = p.op("vector", lambda e: e.tensor_copy(
                out=t_rows[0:1, 512:1024], in_=pvO[64:65, :]), inc=[("ve", 1)])
            tok_ob = p.op("vector", lambda e: e.tensor_copy(
                out=a_qb[0:64, :], in_=pvO[0:64, :]), inc=[("ve", 1)])
            p.op("tensor", lambda e: e.matmul(
                out=pAb[0:64, :], lhsT=t_onesr[0:1, 0:64],
                rhs=t_rows[0:1, 0:512], start=True, stop=True),
                wait=tok_r + G["pA"])
            tok_bcm = p.op("tensor", lambda e: e.matmul(
                out=pAb[64:128, :], lhsT=t_onesr[0:1, 0:64],
                rhs=t_rows[0:1, 512:1024], start=True, stop=True),
                inc=[("mm", 1)])
            tok_sh = p.op("tensor", lambda e: e.matmul(
                out=pBb[:, :], lhsT=t_shift[0:64, :], rhs=a_qb[0:64, :],
                start=True, stop=True),
                wait=tok_ob + G["pB"], inc=[("mm", 1)])
            p.op("scalar", lambda e: e.activation(
                out=a_lnx[:], in_=pAb[:, :], func=AF.Ln), wait=tok_bcm)
            tok_rbc = p.op("scalar", lambda e: e.activation(
                out=a_rb[:], in_=a_lnx[:], func=AF.Exp, scale=f32(-1.0)),
                inc=[("ac", 1)])
            G["pA"] = tok_rbc
            p.op("vector", lambda e, hp=hp: e.tensor_mul(
                out=a_oa[0:64, hp * NT:(hp + 1) * NT],
                in0=pvE[0:64, :], in1=a_rb[0:64, :]), wait=tok_rbc)
            tok_f = p.op("vector", lambda e, hp=hp: e.tensor_mul(
                out=a_oa[64:128, hp * NT:(hp + 1) * NT],
                in0=pBb[64:128, :], in1=a_rb[64:128, :]),
                wait=tok_sh, inc=[("ve", 1)])
            G["pvE"] = tok_f
            G["pvO"] = tok_ob
            G["pB"] = tok_f
            return tok_f

        for hp in range(8):
            wt0 = tok_exp_h.get(hp - 1, ())
            if hp == 0:
                wt0 = wt0 + tok_qk + tok_vdone + G["sc"] + G["lnb"]
            tok_qkm_h[hp] = a_qk(hp, wt0)
            tok_exp_h[hp] = p.op("scalar", lambda e: e.activation(
                out=a_p[:], in_=sc4[:, :], func=AF.Exp,
                scale=f32(0.125), bias=f32(-C_A)),
                wait=tok_qkm_h[hp] + tok_msk_h.get(hp - 1, ()), inc=[("ac", 1)])
            tok_msk_h[hp] = p.op("gpsimd", lambda e, m=mask_t, hp=hp: e.tensor_mul(
                out=a_pm[:, (hp % 2) * 2048:(hp % 2) * 2048 + 2048],
                in0=a_p[:], in1=m[:]),
                wait=tok_exp_h[hp] + tok_pvm_h.get(hp - 2, ()), inc=[("gp", 1)])
            if hp >= 1 and asub >= 5:
                tok_pvm_h[hp - 1] = a_pv(hp - 1)
                if asub >= 6:
                    tok_fin = a_fin(hp - 1)
        G["sc"] = tok_exp_h[7]
        if asub >= 5:
            tok_pvm_h[7] = a_pv(7)
            if asub >= 6:
                tok_fin = a_fin(7)
        if asub < 7:
            continue
        tok_x1_last = ()
        for oc in range(8):
            bank, bname = (pAb, "pA") if oc % 2 == 0 else (pBb, "pB")
            for dc in range(8):
                wt = ()
                if dc == 0:
                    wt = G[bname]
                    if oc == 0:
                        wt = wt + tok_fin
                p.op("tensor", lambda e, d=dc, o_=oc, b=bank: e.matmul(
                    out=b[:, 0:NT],
                    lhsT=a_w[:, 3 * 8192 + d * 1024 + o_ * 128:
                             3 * 8192 + d * 1024 + o_ * 128 + 128],
                    rhs=a_oa[:, d * NT:(d + 1) * NT],
                    start=(d == 0), stop=(d == 7)), wait=wt)
            tok_mm = p.mark("tensor", [("mm", 1)])
            sl = oc % 2
            xrt = a_xr0 if sl == 0 else a_qpf
            wxr = a_stt.get(sl, ())
            if oc < 2:
                wxr = wxr + tok_qk + tok_t2_free
            tok_xr = p.dma(xrt[:], x0T[oc * 128:(oc + 1) * 128, t0:t0 + NT],
                           sem=f"xr{sl}", wait=wxr)
            tok_x1 = p.op("vector", lambda e, b=bank, o_=oc, x=xrt: e.scalar_tensor_tensor(
                out=a_x1[:, (o_ % 2) * NT:(o_ % 2) * NT + NT],
                in0=b[:, 0:NT], scalar=t_cols[:, 16 + o_:17 + o_],
                in1=x[:], op0=OP.add, op1=OP.add),
                wait=tok_mm + tok_x1_dma.get(oc % 2, ()) + tok_xr,
                inc=[("ve", 1)])
            a_stt[sl] = tok_x1
            G[bname] = tok_x1
            tok_x1_dma[oc % 2] = p.dma(x1d[oc * 128:(oc + 1) * 128, t0:t0 + NT],
                                       a_x1[:, (oc % 2) * NT:(oc % 2) * NT + NT],
                                       sem=f"x1w{oc % 2}", wait=tok_x1,
                                       eng="gpsimd")
            tok_x1_last = tok_x1
        tok_qpf_free = tok_qpf_free + a_stt[1]
        tok_attn_done = tok_fin + tok_x1_last

    tok_x1_all = (("x1w0", p.cnt["x1w0"]), ("x1w1", p.cnt["x1w1"]))

    if "B" not in phases:
        _finish(nc, p)
        return nc

    gate = tok_attn_done
    for wi, wsrc in enumerate((wbk, wbv, wbq)):
        tok_l = ()
        for h in range(2):
            tok_l = tok_l + p.dma(
                b_fstage[:, h * 4096:(h + 1) * 4096]
                .rearrange("p (dc f) -> p dc f", dc=4),
                wsrc.rearrange("(dc p) f -> p dc f", p=128)
                [:, h * 4:h * 4 + 4, :],
                sem="wv", wait=gate if h == 0 else ())
        gate = p.op("vector", lambda e, wi=wi: e.tensor_copy(
            out=b_w[:, wi * 8192:(wi + 1) * 8192], in_=b_fstage[:]),
            wait=tok_l, inc=[("ve", 1)])
    tok_bw = gate

    Bst = {"hdone": (), "x2t": (), "gb": (), "x2done_mm": (), "xn3f": (),
           "sb_stats": ()}
    tok_x1_free = ()
    tok_x2_dma = {}
    wgu_free = {}
    wd_free = {}
    sg_free = {}
    tok_kvbuf = {}
    x1buf_free = {}
    xn_free = {}
    b_pre = {}

    sb_dat = {}

    sb_load = {}

    def emit_SB_load(ti):
        t0 = ti * NT
        xb1 = (ti % 2) * 8 * NT
        wt0 = (tok_bw + tok_x1_all) if ti == 0 \
            else x1buf_free.get(ti % 2, tok_bw)
        sb_load[ti] = p.dma(b_x1[:, xb1:xb1 + 8 * NT]
                            .rearrange("p (dc t) -> p dc t", dc=8),
                            x1d.rearrange("(dc p) t -> p dc t", p=128)
                            [:, :, t0:t0 + NT], wait=wt0)

    def emit_SB_a(ti):
        xb1 = (ti % 2) * 8 * NT
        tok_x1 = sb_load.pop(ti)

        def bx(d, xb1=xb1):
            return b_x1[:, xb1 + d * NT:xb1 + (d + 1) * NT]

        tok_stats = ln_stats(bx, b_xb, b_sq,
                             tok_x1 + Bst.get("ln3_stats", ()))
        sb_dat[ti] = (bx, tok_stats)
        Bst["sb_stats"] = tok_stats

    def emit_SB_r(ti):
        bx, tok_stats = sb_dat[ti]
        sb_dat[ti] = (bx, ln_rows(tok_stats, rows=b_rows2, rkey="rows2"))

    def emit_SB_b(ti):
        bx, tok_rows = sb_dat.pop(ti)
        xo = (ti % 2) * 8 * NT
        tok_n = ln_norm(bx, lambda d, xo=xo: b_xn[:, xo + d * NT:xo + (d + 1) * NT],
                        b_abc, b_bbc, b_tmp, tok_rows,
                        norm_wait=xn_free.get(ti % 2, ()),
                        rows=b_rows2, rkey="rows2",
                        neng="vector", tmp_bank3=True)
        b_pre[ti] = (bx, tok_n)

    def emit_gu(ti):
        xn1 = (ti % 2) * 8 * NT
        bx, tok_ln2 = b_pre[ti]
        for hc in range(24):
            buf = hc % 2
            wtw = wgu_free.get(buf, ())
            if ti == 0 and hc < 2:
                wtw = wtw + tok_wcache
            tok_wl = p.dma(b_wg[:, buf * 1024:(buf + 1) * 1024],
                           wgbf[hc], sem=f"wvg{buf}", wait=wtw)
            tok_wl = tok_wl + p.dma(b_wu[:, buf * 1024:(buf + 1) * 1024],
                                    wubf[hc], sem=f"wvg{buf}")
            gb, gbn = (pAb[:, 0:NT], "pA") if buf == 0 else (sc4[:, 0:NT], "sc0")
            ub, ubn = (pBb[:, 0:NT], "pB") if buf == 0 else (pvO[:, 0:NT], "pvO")
            for dc in range(8):
                wt = (tok_ln2 + tok_wl + G[gbn]) if dc == 0 else ()
                p.op("tensor", lambda e, d=dc, b=buf, gb=gb, xn1=xn1: e.matmul(
                    out=gb,
                    lhsT=b_wg[:, b * 1024 + d * 128: b * 1024 + d * 128 + 128],
                    rhs=b_xn[:, xn1 + d * NT:xn1 + (d + 1) * NT],
                    start=(d == 0), stop=(d == 7)), wait=wt)
            for dc in range(8):
                wt = G[ubn] if dc == 0 else ()
                p.op("tensor", lambda e, d=dc, b=buf, ub=ub, xn1=xn1: e.matmul(
                    out=ub,
                    lhsT=b_wu[:, b * 1024 + d * 128: b * 1024 + d * 128 + 128],
                    rhs=b_xn[:, xn1 + d * NT:xn1 + (d + 1) * NT],
                    start=(d == 0), stop=(d == 7)), wait=wt)
            tok_gu = p.mark("tensor", [("mm", 1)])
            wgu_free[buf] = tok_gu
            if hc == 23:
                xn_free[ti % 2] = tok_gu
            sgs = b_sg[:, (hc % 2) * NT:(hc % 2) * NT + NT]
            tok_sg = p.op("scalar", lambda e, gb=gb, s=sgs: e.activation(
                out=s, in_=gb, func=AF.Silu, scale=f32(DSC)),
                wait=tok_gu + sg_free.get(hc % 2, ()), inc=[("ac", 1)])
            tok_h = p.op("vector", lambda e, h=hc, ub=ub, s=sgs: e.tensor_mul(
                out=b_h[:, h * NT:(h + 1) * NT], in0=ub, in1=s),
                wait=tok_sg + (Bst["x2done_mm"] if hc == 0 else ()),
                inc=[("ve", 1)])
            sg_free[hc % 2] = tok_h
            G[gbn] = tok_sg
            G[ubn] = tok_h
            if hc == 1 and ti >= 1:
                emit_ln3_stats(ti - 1)
            if hc == 2 and ti >= 1:
                emit_ln3_rows(ti - 1)
            if hc == 3 and ti + 1 < nb:
                emit_SB_load(ti + 1)
            if hc == 4 and ti >= 1:
                emit_ln3_mid(ti - 1)
            if hc == 9 and ti + 1 < nb:
                emit_SB_a(ti + 1)
            if hc == 10 and ti + 1 < nb:
                emit_SB_r(ti + 1)
            if hc == 12 and ti + 1 < nb:
                emit_SB_b(ti + 1)
            if hc == 13 and ti >= 1:
                emit_ln3b_stats(ti - 1)
            if hc == 14 and ti >= 1:
                emit_ln3b_rows(ti - 1)
            if hc == 16 and ti >= 1:
                emit_ln3b_fin(ti - 1)
        tok_hdone = tok_h

        Bst["hdone"] = tok_hdone

    def emit_down(ti):
        t0 = ti * NT
        xb1 = (ti % 2) * 8 * NT
        tok_hdone = Bst["hdone"]
        for oc in range(8):
            buf = oc % 2
            tok_wd = p.dma(b_wd[:, buf * 3072:buf * 3072 + 1536],
                           wdbf[oc][:, 0:1536], sem=f"wvd{buf}",
                           wait=wd_free.get(buf, ()))
            tok_wd = tok_wd + p.dma(b_wd[:, buf * 3072 + 1536:(buf + 1) * 3072],
                                    wdbf[oc][:, 1536:3072], sem=f"wvd{buf}")
            bank, bname = (pAb, "pA") if oc % 2 == 0 else (pBb, "pB")
            for hc in range(24):
                wt = (tok_hdone + tok_wd + G[bname]) if hc == 0 else ()
                p.op("tensor", lambda e, h=hc, b=bank, bu=buf: e.matmul(
                    out=b[:, 0:NT],
                    lhsT=b_wd[:, bu * 3072 + h * 128: bu * 3072 + h * 128 + 128],
                    rhs=b_h[:, h * NT:(h + 1) * NT],
                    start=(h == 0), stop=(h == 23)), wait=wt)
            tok_mm = p.mark("tensor", [("mm", 1)])
            tok_x2 = p.op("vector", lambda e, b=bank, o_=oc, xb1=xb1: e.scalar_tensor_tensor(
                out=b_x2[:, o_ * NT:(o_ + 1) * NT], in0=b[:, 0:NT],
                scalar=DSC * DSC, in1=b_x1[:, xb1 + o_ * NT:xb1 + (o_ + 1) * NT],
                op0=OP.mult, op1=OP.add),
                wait=tok_mm + tok_x2_dma.get(oc, ()), inc=[("ve", 1)])
            G[bname] = tok_x2
            wd_free[buf] = tok_x2
            x1buf_free[ti % 2] = tok_x2
            tok_x2_dma[oc] = p.dma(x2d[oc * 128:(oc + 1) * 128, t0:t0 + NT],
                                   b_x2[:, oc * NT:(oc + 1) * NT],
                                   sem=f"x2w{oc}", wait=tok_x2, eng="gpsimd")
        tok_x2t = tok_x2
        tok_x1_free = tok_x2t

        Bst["x2t"] = tok_x2t
        Bst["x2done_mm"] = tok_mm

    def b2f(d):
        return b_x2[:, d * NT:(d + 1) * NT]

    def b3f(d):
        return b_xn3[:, d * NT:(d + 1) * NT]

    def emit_ln3_stats(ti):
        Bst["ln3_stats"] = ln_stats(b2f, b_xb, b_sq,
                                    Bst["x2t"] + Bst["sb_stats"])

    def emit_ln3_rows(ti):
        Bst["ln3_rows"] = ln_rows(Bst["ln3_stats"])

    def emit_ln3_mid(ti):
        Bst["ln3"] = ln_norm(b2f, lambda d: b_xn3[:, d * NT:(d + 1) * NT],
                             b_abc, b_bbc, b_tmp, Bst["ln3_rows"],
                             norm_wait=Bst["xn3f"])

    def emit_ln3b_stats(ti):
        Bst["ln3b_stats"] = ln_stats(b3f, b_xb, b_sq,
                                     Bst["ln3"] + Bst["sb_stats"], bf_in=True)

    def emit_ln3b_rows(ti):
        Bst["ln3b_rows"] = ln_rows(Bst["ln3b_stats"])

    def emit_ln3b_fin(ti):
        tok_ln3b = ln_norm(b3f, lambda d: b_xn3b[:, d * NT:(d + 1) * NT],
                           b_abc, b_bbc, b_tmp, Bst["ln3b_rows"],
                           norm_wait=Bst["xn3f"],
                           neng="vector", tmp_bank3=True)
        for dc in range(8):
            wt = tok_ln3b if dc == 0 else ()
            p.op("scalar", lambda e, d=dc: e.activation(
                out=b_xn3b[:, d * NT:(d + 1) * NT],
                in_=b_xn3b[:, d * NT:(d + 1) * NT], func=AF.Identity,
                bias=t_cols[:, 32 + d:33 + d], scale=t_cols[:, 24 + d:25 + d]),
                wait=wt)
        Bst["gb"] = p.mark("scalar", [("ac", 1)])

    def emit_kv(ti):
        t0 = ti * NT
        tok_gb = Bst["gb"]
        for wi, (wofs, dstd) in enumerate(((0, kd), (2 * 8192, qd))):
            for oc in range(8):
                bank, bname = (pAb, "pA") if oc % 2 == 0 else (pBb, "pB")
                for dc in range(8):
                    wt = G[bname] if dc == 0 else ()
                    if dc == 0 and oc == 0 and wi == 0:
                        wt = wt + tok_gb
                    p.op("tensor", lambda e, d=dc, o_=oc, b=bank, w=wofs: e.matmul(
                        out=b[:, 0:NT],
                        lhsT=b_w[:, w + d * 1024 + o_ * 128:
                                 w + d * 1024 + o_ * 128 + 128],
                        rhs=b_xn3b[:, d * NT:(d + 1) * NT],
                        start=(d == 0), stop=(d == 7)), wait=wt)
                tok_mm = p.mark("tensor", [("mm", 1)])
                sl = (wi * 8 + oc) % 4
                tok_ev = p.op("vector", lambda e, b=bank, s=sl: e.tensor_copy(
                    out=b_kv[:, s * NT:(s + 1) * NT], in_=b[:, 0:NT]),
                    wait=tok_mm + tok_kvbuf.get(sl, ()), inc=[("ve", 1)])
                G[bname] = tok_ev
                tok_kvbuf[sl] = p.dma(dstd[oc * 128:(oc + 1) * 128, t0:t0 + NT],
                                      b_kv[:, sl * NT:(sl + 1) * NT],
                                      sem=f"kvw{sl}", wait=tok_ev, eng="gpsimd")
        for tb in range(4):
            for fh in range(2):
                bank, bname = (pAb, "pA") if fh == 0 else (pBb, "pB")
                for dc in range(8):
                    wt = G[bname] if dc == 0 else ()
                    p.op("tensor", lambda e, d=dc, t=tb, f=fh, b=bank: e.matmul(
                        out=b[:, 0:512],
                        lhsT=b_xn3b[:, d * NT + t * 128: d * NT + t * 128 + 128],
                        rhs=b_w[:, 8192 + d * 1024 + f * 512:
                                 8192 + d * 1024 + f * 512 + 512],
                        start=(d == 0), stop=(d == 7)), wait=wt)
                tok_mm = p.mark("tensor", [("mm", 1)])
                sl = (tb * 2 + fh) % 4
                tok_ev = p.op("vector", lambda e, b=bank, s=sl: e.tensor_copy(
                    out=b_kv[:, s * NT:s * NT + 512], in_=b[:, 0:512]),
                    wait=tok_mm + tok_kvbuf.get(sl, ()), inc=[("ve", 1)])
                G[bname] = tok_ev
                tok_kvbuf[sl] = p.dma(vd[t0 + tb * 128:t0 + tb * 128 + 128,
                                         fh * 512:(fh + 1) * 512],
                                      b_kv[:, sl * NT:sl * NT + 512],
                                      sem=f"kvw{sl}", wait=tok_ev, eng="gpsimd")
                if tb == 3 and fh == 1:
                    Bst["xn3f"] = tok_mm


    emit_SB_load(0)
    emit_SB_a(0)
    emit_SB_r(0)
    emit_SB_b(0)
    emit_gu(0)
    for ti in range(nb):
        emit_down(ti)
        if ti + 1 < nb:
            emit_gu(ti + 1)
        else:
            emit_ln3_stats(ti)
            emit_ln3_rows(ti)
            emit_ln3_mid(ti)
            emit_ln3b_stats(ti)
            emit_ln3b_rows(ti)
            emit_ln3b_fin(ti)
        emit_kv(ti)

    tok_kv_all = tuple((f"kvw{i}", p.cnt[f"kvw{i}"]) for i in range(4))
    tok_x2_all = tuple((f"x2w{i}", p.cnt[f"x2w{i}"]) for i in range(8))
    tok_b_done = G["pA"] + G["pB"]

    if "C" not in phases:
        _finish(nc, p)
        return nc

    tok_l = ()
    for h in range(2):
        tok_l = tok_l + p.dma(
            c_fstage[:, h * 4096:(h + 1) * 4096]
            .rearrange("p (dc f) -> p dc f", dc=4),
            wbo.rearrange("(dc p) f -> p dc f", p=128)[:, h * 4:h * 4 + 4, :],
            sem="wv", wait=tok_b_done if h == 0 else ())
    tok_wo = p.op("vector", lambda e: e.tensor_copy(out=c_wo[:], in_=c_fstage[:]),
                  wait=tok_l, inc=[("ve", 1)])

    tok_qsel = ()
    for qt in range(8):
        q0 = qt * 512
        wt = (tok_kv_all + tok_wo) if qt == 0 else tok_qsel
        tok_a = p.dma(c_st0[:].rearrange("p (dc t) -> p dc t", dc=8),
                      qd.rearrange("(dc p) t -> p dc t", p=128)
                      [:, :, q0:q0 + 512], wait=wt)
        tok_b2 = p.dma(c_st1[:].rearrange("p (dc t) -> p dc t", dc=8),
                       qd.rearrange("(dc p) t -> p dc t", p=128)
                       [:, :, HALF + q0:HALF + q0 + 512])
        for dc in range(8):
            wt2 = (tok_a + tok_b2 + tok_cfl) if dc == 0 else ()
            p.op("vector", lambda e, d=dc: e.tensor_scalar_mul(
                out=c_st0[:, d * 512:(d + 1) * 512],
                in0=c_st0[:, d * 512:(d + 1) * 512],
                scalar1=t_flags[:, 0:1]), wait=wt2)
            p.op("vector", lambda e, d=dc, q=q0: e.scalar_tensor_tensor(
                out=c_q[:, d * HALF + q: d * HALF + q + 512],
                in0=c_st1[:, d * 512:(d + 1) * 512],
                scalar=t_flags[:, 1:2],
                in1=c_st0[:, d * 512:(d + 1) * 512],
                op0=OP.mult, op1=OP.add))
        tok_qsel = p.mark("vector", [("ve", 1)])

    G["sc"] = G["sc"] + G["sc0"] + G["sc1"]
    tok_vones = p.op("vector", lambda e: e.memset(
        c_ve[:].rearrange("p (c f) -> p c f", f=65)[:, :, 64:65], 1.0),
        wait=tok_qsel)
    tok_vones = p.op("vector", lambda e: e.memset(
        c_vo[:].rearrange("p (c f) -> p c f", f=65)[:, :, 64:65], 1.0),
        inc=[("ve", 1)])
    tok_kvld = ()
    tok_fin = ()
    tok_pv_last = {}
    for hp in range(8):
        wt_ld = (tok_kv_all + tok_vones) if hp == 0 else tok_kvld2
        tok_k = p.dma(c_k[:], kd[hp * 128:(hp + 1) * 128, :], wait=wt_ld)
        tok_v1 = ()
        for c0 in range(0, 64, 8):
            tok_v1 = tok_v1 + p.dma(
                c_ve[:].rearrange("p (c f) -> p c f", f=65)[:, c0:c0 + 8, 0:64],
                vd.rearrange("(c p) f -> p c f", p=128)
                [:, c0:c0 + 8, (2 * hp) * 64:(2 * hp) * 64 + 64])
            tok_v1 = tok_v1 + p.dma(
                c_vo[:].rearrange("p (c f) -> p c f", f=65)[:, c0:c0 + 8, 0:64],
                vd.rearrange("(c p) f -> p c f", p=128)
                [:, c0:c0 + 8, (2 * hp + 1) * 64:(2 * hp + 1) * 64 + 64])
        tok_kvld = tok_k + tok_v1
        tok_kvld2 = ()
        for qt in range(8):
            q0 = qt * 512
            tok_exp_g = {}
            tok_pv_last = {}

            def emit_qk(g, wt0):
                for eo in range(2):
                    p.op("tensor", lambda e, g=g, eo=eo, hp=hp, q=q0: e.matmul(
                        out=sc4[:, ((g % 2) * 2 + eo) * 512:
                                ((g % 2) * 2 + eo) * 512 + 512],
                        lhsT=c_k[eo * 64:eo * 64 + 64, g * 128:(g + 1) * 128],
                        rhs=c_q[eo * 64:eo * 64 + 64,
                                hp * HALF + q: hp * HALF + q + 512],
                        start=True, stop=True), wait=wt0 if eo == 0 else ())
                return p.mark("tensor", [("mm", 1)])

            def emit_pv_e(g, extra):
                pe = c_p[:, (g % 2) * 1024:(g % 2) * 1024 + 512]
                p.op("tensor", lambda e, g=g, pe=pe: e.matmul(
                    out=pvE[0:65, :], lhsT=c_ve[:, g * 65:g * 65 + 65],
                    rhs=pe, start=(g == 0), stop=(g == 63)),
                    wait=tok_exp_g[g] + extra)

            def emit_pv_o(g):
                po = c_p[:, (g % 2) * 1024 + 512:(g % 2) * 1024 + 1024]
                p.op("tensor", lambda e, g=g, po=po: e.matmul(
                    out=pvO[0:65, :], lhsT=c_vo[:, g * 65:g * 65 + 65],
                    rhs=po, start=(g == 0), stop=(g == 63)))
                return p.mark("tensor", [("mm", 1)])

            for g in range(64):
                wt0 = tok_exp_g.get(g - 2, ())
                if g <= 1:
                    wt0 = wt0 + tok_kvld + G["sc"]
                tok_qkm = emit_qk(g, wt0)
                if g >= 1:
                    extra = ()
                    if g == 1:
                        extra = tok_fin + G["pvE"] + G["pvO"]
                    emit_pv_e(g - 1, extra)
                    tok_pv_last[g - 1] = emit_pv_o(g - 1)
                tok_exp_g[g] = p.op("scalar", lambda e, g=g: e.activation(
                    out=c_p[:, (g % 2) * 1024:(g % 2) * 1024 + 1024],
                    in_=sc4[:, (g % 2) * 1024:(g % 2) * 1024 + 1024],
                    func=AF.Exp, scale=f32(0.125), bias=f32(-C_B)),
                    wait=tok_qkm + tok_pv_last.get(g - 2, ()), inc=[("ac", 1)])
            emit_pv_e(63, ())
            tok_pv_last[63] = emit_pv_o(63)
            G["sc"] = tok_exp_g[63]
            tok_pvm = tok_pv_last[63]
            p.op("vector", lambda e: e.tensor_copy(
                out=t_rows[0:1, 0:512], in_=pvE[64:65, :]), wait=tok_pvm)
            tok_lr = p.op("vector", lambda e: e.tensor_copy(
                out=t_rows[0:1, 512:1024], in_=pvO[64:65, :]), inc=[("ve", 1)])
            tok_ob = p.op("vector", lambda e: e.tensor_copy(
                out=c_ob[0:64, :], in_=pvO[0:64, :]), inc=[("ve", 1)])
            p.op("tensor", lambda e: e.matmul(
                out=pAb[0:64, :], lhsT=t_onesr[0:1, 0:64],
                rhs=t_rows[0:1, 0:512], start=True, stop=True),
                wait=tok_lr + G["pA"])
            tok_bcm = p.op("tensor", lambda e: e.matmul(
                out=pAb[64:128, :], lhsT=t_onesr[0:1, 0:64],
                rhs=t_rows[0:1, 512:1024], start=True, stop=True),
                inc=[("mm", 1)])
            tok_sh = p.op("tensor", lambda e: e.matmul(
                out=pBb[:, :], lhsT=t_shift[0:64, :], rhs=c_ob[0:64, :],
                start=True, stop=True),
                wait=tok_ob + G["pB"], inc=[("mm", 1)])
            p.op("scalar", lambda e: e.activation(
                out=c_lnx[:], in_=pAb[:, :], func=AF.Ln), wait=tok_bcm)
            tok_rb = p.op("scalar", lambda e: e.activation(
                out=c_rb[:], in_=c_lnx[:], func=AF.Exp, scale=f32(-1.0)),
                inc=[("ac", 1)])
            G["pA"] = tok_rb
            p.op("vector", lambda e, hp=hp, q=q0: e.tensor_mul(
                out=c_o[0:64, hp * HALF + q: hp * HALF + q + 512],
                in0=pvE[0:64, :], in1=c_rb[0:64, :]), wait=tok_rb)
            tok_fin = p.op("vector", lambda e, hp=hp, q=q0: e.tensor_mul(
                out=c_o[64:128, hp * HALF + q: hp * HALF + q + 512],
                in0=pBb[64:128, :], in1=c_rb[64:128, :]),
                wait=tok_sh, inc=[("ve", 1)])
            G["pvE"] = tok_fin
            G["pvO"] = tok_lr + tok_ob
            G["pB"] = tok_fin
        tok_kvld2 = tok_pv_last[63] + tok_exp_g[63]

    if taps:
        p.dma(od.rearrange("(dc p) t -> p dc t", p=128),
              c_o[:].rearrange("p (dc t) -> p dc t", dc=8),
              sem="s3", wait=tok_fin)

    if "D" not in phases:
        _finish(nc, p)
        return nc

    tok_out_dma = {}
    tok_stage_free = tok_kvld2 + tok_fin
    wgu_free = {0: tok_kvld2, 1: tok_kvld2}
    wd_free = {0: tok_kvld2, 1: tok_kvld2}
    sg_free = {}
    for ti in range(nd):
        t0 = ti * NT
        tok_a = p.dma(d_xs0[:].rearrange("p (dc t) -> p dc t", dc=8),
                      x2d.rearrange("(dc p) t -> p dc t", p=128)
                      [:, :, t0:t0 + NT], wait=tok_x2_all + tok_stage_free,
                      eng="gpsimd")
        tok_b2 = p.dma(d_xs1a[:].rearrange("p (dc t) -> p dc t", dc=6),
                       x2d.rearrange("(dc p) t -> p dc t", p=128)
                       [:, 0:6, HALF + t0:HALF + t0 + NT], eng="gpsimd")
        tok_b2 = tok_b2 + p.dma(
            d_xs1b[:].rearrange("p (dc t) -> p dc t", dc=2),
            x2d.rearrange("(dc p) t -> p dc t", p=128)
            [:, 6:8, HALF + t0:HALF + t0 + NT], eng="gpsimd")
        for dc in range(8):
            wt = (tok_a + tok_b2) if dc == 0 else ()
            p.op("vector", lambda e, d=dc: e.tensor_scalar_mul(
                out=d_xs0[:, d * NT:(d + 1) * NT],
                in0=d_xs0[:, d * NT:(d + 1) * NT],
                scalar1=t_flags[:, 0:1]), wait=wt)
            p.op("vector", lambda e, d=dc: e.scalar_tensor_tensor(
                out=d_xs0[:, d * NT:(d + 1) * NT],
                in0=dxs1(d), scalar=t_flags[:, 1:2],
                in1=d_xs0[:, d * NT:(d + 1) * NT], op0=OP.mult, op1=OP.add))
        tok_x2sel = p.mark("vector", [("ve", 1)])

        for oc in range(8):
            bank, bname = (pAb, "pA") if oc % 2 == 0 else (pBb, "pB")
            for dc in range(8):
                wt = ()
                if dc == 0:
                    wt = G[bname] + tok_x2sel
                p.op("tensor", lambda e, d=dc, o_=oc, b=bank, t=t0: e.matmul(
                    out=b[:, 0:NT],
                    lhsT=c_wo[:, d * 1024 + o_ * 128: d * 1024 + o_ * 128 + 128],
                    rhs=c_o[:, d * HALF + t: d * HALF + t + NT],
                    start=(d == 0), stop=(d == 7)), wait=wt)
            tok_mm = p.mark("tensor", [("mm", 1)])
            tok_x3 = p.op("vector", lambda e, b=bank, o_=oc: e.tensor_add(
                out=d_x3[:, o_ * NT:(o_ + 1) * NT], in0=b[:, 0:NT],
                in1=d_xs0[:, o_ * NT:(o_ + 1) * NT]),
                wait=tok_mm, inc=[("ve", 1)])
            G[bname] = tok_x3
        tok_x3f = tok_x3

        def dxf(d):
            return d_x3[:, d * NT:(d + 1) * NT]

        tok_ln4, _ = layer_norm(dxf, lambda d: d_xn[:, d * NT:(d + 1) * NT],
                                d_xb, d_sq, d_abc, d_bbc, d_tmp, tok_x3f)
        for hc in range(24):
            buf = hc % 2
            tok_wl = p.dma(d_wg[:, buf * 1024:(buf + 1) * 1024],
                           wgbf[hc], sem=f"wvg{buf}", wait=wgu_free.get(buf, ()))
            tok_wl = tok_wl + p.dma(d_wu[:, buf * 1024:(buf + 1) * 1024],
                                    wubf[hc], sem=f"wvg{buf}")
            gb, gbn = (pAb[:, 0:NT], "pA") if buf == 0 else (sc4[:, 0:NT], "sc0")
            ub, ubn = (pBb[:, 0:NT], "pB") if buf == 0 else (pvO[:, 0:NT], "pvO")
            for dc in range(8):
                wt = (tok_ln4 + tok_wl + G[gbn]) if dc == 0 else ()
                p.op("tensor", lambda e, d=dc, b=buf, gb=gb: e.matmul(
                    out=gb,
                    lhsT=d_wg[:, b * 1024 + d * 128: b * 1024 + d * 128 + 128],
                    rhs=d_xn[:, d * NT:(d + 1) * NT],
                    start=(d == 0), stop=(d == 7)), wait=wt)
            for dc in range(8):
                wt = G[ubn] if dc == 0 else ()
                p.op("tensor", lambda e, d=dc, b=buf, ub=ub: e.matmul(
                    out=ub,
                    lhsT=d_wu[:, b * 1024 + d * 128: b * 1024 + d * 128 + 128],
                    rhs=d_xn[:, d * NT:(d + 1) * NT],
                    start=(d == 0), stop=(d == 7)), wait=wt)
            tok_gu = p.mark("tensor", [("mm", 1)])
            wgu_free[buf] = tok_gu
            sgs = d_sg[:, (hc % 2) * NT:(hc % 2) * NT + NT]
            tok_sg = p.op("scalar", lambda e, gb=gb, s=sgs: e.activation(
                out=s, in_=gb, func=AF.Silu, scale=f32(DSC)),
                wait=tok_gu + sg_free.get(hc % 2, ()), inc=[("ac", 1)])
            tok_h = p.op("vector", lambda e, h=hc, ub=ub, s=sgs: e.tensor_mul(
                out=d_h[:, h * NT:(h + 1) * NT], in0=ub, in1=s),
                wait=tok_sg, inc=[("ve", 1)])
            sg_free[hc % 2] = tok_h
            G[gbn] = tok_sg
            G[ubn] = tok_h
        tok_hdone = tok_h
        for oc in range(8):
            buf = oc % 2
            tok_wd = p.dma(d_wd[:, buf * 3072:buf * 3072 + 1536],
                           wdbf[oc][:, 0:1536], sem=f"wvd{buf}",
                           wait=wd_free.get(buf, ()))
            tok_wd = tok_wd + p.dma(d_wd[:, buf * 3072 + 1536:(buf + 1) * 3072],
                                    wdbf[oc][:, 1536:3072], sem=f"wvd{buf}")
            bank, bname = (pAb, "pA") if oc % 2 == 0 else (pBb, "pB")
            for hc in range(24):
                wt = (tok_hdone + tok_wd + G[bname]) if hc == 0 else ()
                p.op("tensor", lambda e, h=hc, b=bank, bu=buf: e.matmul(
                    out=b[:, 0:NT],
                    lhsT=d_wd[:, bu * 3072 + h * 128: bu * 3072 + h * 128 + 128],
                    rhs=d_h[:, h * NT:(h + 1) * NT],
                    start=(h == 0), stop=(h == 23)), wait=wt)
            tok_mm = p.mark("tensor", [("mm", 1)])
            tok_o = p.op("vector", lambda e, b=bank, o_=oc: e.scalar_tensor_tensor(
                out=d_out[:, (o_ % 2) * NT:(o_ % 2) * NT + NT],
                in0=b[:, 0:NT], scalar=DSC * DSC,
                in1=d_x3[:, o_ * NT:(o_ + 1) * NT], op0=OP.mult, op1=OP.add),
                wait=tok_mm + tok_out_dma.get(oc % 2, ()), inc=[("ve", 1)])
            G[bname] = tok_o
            wd_free[buf] = tok_o
            tok_out_dma[oc % 2] = p.dma(outT[oc * 128:(oc + 1) * 128, t0:t0 + NT],
                                        d_out[:, (oc % 2) * NT:(oc % 2) * NT + NT],
                                        sem=f"ow{oc % 2}", wait=tok_o,
                                        eng="gpsimd")
        tok_stage_free = tok_x3f

    _finish(nc, p)
    return nc


def _finish(nc, p):
    p.op("sync", lambda e: e.nop(),
         wait=[(s, v) for s, v in p.cnt.items() if v > 0])
    import contextlib
    stack = contextlib.ExitStack()
    sems = {s: stack.enter_context(nc.semaphore(s)) for s in p.cnt}
    with nc.Block() as block:
        _replay(block, p, sems)
    stack.close()


def eo_off(ch):
    return 256 + ch * 128


def _host_constants():
    pos = np.arange(S, dtype=np.float64)
    inv = 1.0 / (10000.0 ** (np.arange(0, HD, 2, dtype=np.float64) / HD))
    ang = pos[:, None] * inv[None, :]
    cos64 = np.tile(np.cos(ang), (1, 2)).astype(np.float32)
    sin64 = np.tile(np.sin(ang), (1, 2)).astype(np.float32)
    sgn = np.ones(64, np.float32)
    sgn[:32] = -1.0
    sinp = sin64 * sgn[None, :]
    cosT_ = np.ascontiguousarray(np.concatenate([cos64.T, cos64.T], axis=0))
    sinT_ = np.ascontiguousarray(np.concatenate([sinp.T, sinp.T], axis=0))
    r = np.zeros((128, 128), np.float32)
    for h in range(2):
        for i in range(64):
            r[h * 64 + (i + 32) % 64, h * 64 + i] = 1.0
    r128_ = np.ascontiguousarray(r.astype(bf16))
    sh = np.zeros((128, 128), np.float32)
    for k in range(64):
        sh[k, 64 + k] = 1.0
    shmat_ = np.ascontiguousarray(sh.astype(bf16))
    k_i = np.arange(128)[:, None]
    q_i = np.arange(128)[None, :]
    mprev = (k_i >= q_i).astype(np.float32)
    mcur = (k_i <= q_i).astype(np.float32)
    blockm = np.concatenate([mprev, mcur], axis=1)
    maskA_ = np.ascontiguousarray(np.tile(blockm, (1, 8)).astype(bf16))
    block0 = np.concatenate([0 * mprev, mcur], axis=1)
    half = np.concatenate([block0] + [blockm] * 3, axis=1)
    maskA0_ = np.ascontiguousarray(np.tile(half, (1, 2)).astype(bf16))
    return cosT_, sinT_, r128_, maskA_, maskA0_, shmat_


def make_in_maps(L0_attn_a_hidden_states, L0_attn_a_Wq, L0_attn_a_bq,
                 L0_attn_a_Wk, L0_attn_a_bk, L0_attn_a_Wv, L0_attn_a_bv,
                 L0_attn_a_Wo, L0_attn_a_bo, L0_attn_b_ln_gamma,
                 L0_attn_b_ln_beta, L0_attn_b_Wq, L0_attn_b_Wk, L0_attn_b_Wv,
                 L0_attn_b_Wo, L0_ffn_w_gate, L0_ffn_w_up, L0_ffn_w_down):
    x = np.asarray(L0_attn_a_hidden_states, np.float32)
    cosT_, sinT_, r128_, maskA_, maskA0_, shmat_ = _host_constants()

    cols = np.zeros((128, 48), np.float32)
    cols[:, 0:8] = np.asarray(L0_attn_a_bq, np.float32).reshape(8, 128).T
    cols[:, 8:16] = np.asarray(L0_attn_a_bk, np.float32).reshape(8, 128).T
    cols[:, 16:24] = np.asarray(L0_attn_a_bo, np.float32).reshape(8, 128).T
    cols[:, 24:32] = np.asarray(L0_attn_b_ln_gamma, np.float32).reshape(8, 128).T
    cols[:, 32:40] = np.asarray(L0_attn_b_ln_beta, np.float32).reshape(8, 128).T
    bvrow_ = np.asarray(L0_attn_a_bv, np.float32).reshape(1, DM)

    shared = {
        "waqT": np.ascontiguousarray(np.asarray(L0_attn_a_Wq, np.float32).T),
        "wakT": np.ascontiguousarray(np.asarray(L0_attn_a_Wk, np.float32).T),
        "wavT": np.ascontiguousarray(np.asarray(L0_attn_a_Wv, np.float32).T),
        "waoT": np.ascontiguousarray(np.asarray(L0_attn_a_Wo, np.float32).T),
        "acols": cols, "bvrow": bvrow_, "r128": r128_,
        "cosT": cosT_, "sinT": sinT_, "maskA": maskA_, "maskA0": maskA0_,
        "shmat": shmat_,
        "wbq": np.ascontiguousarray(np.asarray(L0_attn_b_Wq, np.float32)),
        "wbk": np.ascontiguousarray(np.asarray(L0_attn_b_Wk, np.float32)),
        "wbv": np.ascontiguousarray(np.asarray(L0_attn_b_Wv, np.float32)),
        "wbo": np.ascontiguousarray(np.asarray(L0_attn_b_Wo, np.float32)),
        "wgT": np.ascontiguousarray(np.asarray(L0_ffn_w_gate, np.int32).T),
        "wuT": np.ascontiguousarray(np.asarray(L0_ffn_w_up, np.int32).T),
        "wdT": np.ascontiguousarray(np.asarray(L0_ffn_w_down, np.int32).T),
    }

    in_maps = []
    for c in range(8):
        b, h = c // 2, c % 2
        fl = np.zeros((128, 4), np.float32)
        fl[:, 0] = 1.0 if h == 0 else 0.0
        fl[:, 1] = 1.0 if h == 1 else 0.0
        m = dict(shared)
        m["x0T"] = np.ascontiguousarray(x[b].T)
        m["flags"] = fl
        m["flagsb"] = np.ascontiguousarray(fl.astype(bf16))
        in_maps.append(m)

    return in_maps


def kernel(**inputs):
    in_maps = make_in_maps(**inputs)
    nc = _build(taps=False)
    res = run_bass_kernel_spmd(nc, in_maps, core_ids=list(range(8)))
    out = np.empty((BATCH, S, DM), np.float32)
    for c in range(8):
        b, h = c // 2, c % 2
        out[b, h * HALF:(h + 1) * HALF, :] = res.results[c]["outT"].T
    return out


# version 31
# speedup vs baseline: 1.1378x; 1.1378x over previous
import numpy as np
import ml_dtypes
import concourse.bass as bass
import concourse.mybir as mybir
from concourse.bass_utils import run_bass_kernel_spmd

F32 = mybir.dt.float32
BF16 = mybir.dt.bfloat16
I32 = mybir.dt.int32
AF = mybir.ActivationFunctionType
OP = mybir.AluOpType
f32 = np.float32
bf16 = ml_dtypes.bfloat16

BATCH, S, DM, NH, HD = 4, 8192, 1024, 16, 64
FH = 3072
HALF = S // 2
NT = 512
EPS = 1e-5
C_A = 8.0
C_B = 16.0
DSC = 1.0 / 127.0
VSLOT = 130

ENG = ("sync", "tensor", "vector", "scalar", "gpsimd")
KB = 1024


class Plan:
    def __init__(self):
        self.ops = {e: [] for e in ENG}
        self.cnt = {}

    def sem(self, name):
        self.cnt[name] = 0

    def op(self, eng, fn, wait=(), inc=()):
        by = {}
        for s, v in wait:
            by[s] = max(v, by.get(s, 0))
        wait = tuple(sorted(by.items()))
        for s, v in wait:
            assert v <= self.cnt[s], f"wait on future: {s} {v} > {self.cnt[s]}"
        toks = []
        for s, n in inc:
            self.cnt[s] += n
            toks.append((s, self.cnt[s]))
        self.ops[eng].append((fn, wait, tuple(inc)))
        return tuple(toks)

    def mark(self, eng, inc):
        fn, waits, incs = self.ops[eng][-1]
        toks = []
        for s, n in inc:
            self.cnt[s] += n
            toks.append((s, self.cnt[s]))
        self.ops[eng][-1] = (fn, waits, incs + tuple(inc))
        return tuple(toks)

    def dma(self, dst, src, sem="ld", wait=(), eng="sync"):
        return self.op(eng, lambda e, d=dst, s=src: e.dma_start(out=d, in_=s),
                       wait=wait, inc=[(sem, 16)])


def _replay(block, plan, sems):
    def mk(ops):
        def body(eng):
            for fn, waits, incs in ops:
                for s, v in waits:
                    eng.wait_ge(sems[s], v)
        return body

    for e in ENG:
        ops = plan.ops[e]

        def body(eng, ops=ops):
            for fn, waits, incs in ops:
                for s, v in waits:
                    eng.wait_ge(sems[s], v)
                inst = fn(eng)
                if incs:
                    if inst is None:
                        inst = eng.nop()
                    for s, n in incs:
                        inst = inst.then_inc(sems[s], n)
        getattr(block, e)(body)


def _build(taps=False, phases="ABCD", na=16, nb=16, nd=8, asub=7):
    nc = bass.Bass("TRN2", target_bir_lowering=False, debug=False,
                   detect_race_conditions=False)

    def ein(name, shape, dtype):
        return nc.dram_tensor(name, list(shape), dtype, kind="ExternalInput").ap()

    x0T = ein("x0T", (DM, S), F32)
    waqT = ein("waqT", (DM, DM), F32)
    wakT = ein("wakT", (DM, DM), F32)
    wavT = ein("wavT", (DM, DM), F32)
    waoT = ein("waoT", (DM, DM), F32)
    acols = ein("acols", (128, 48), F32)
    bvrow = ein("bvrow", (1, DM), F32)
    r128 = ein("r128", (128, 128), BF16)
    cosT = ein("cosT", (128, S), F32)
    sinT = ein("sinT", (128, S), F32)
    maskA = ein("maskA", (128, 2048), BF16)
    maskA0 = ein("maskA0", (128, 2048), BF16)
    wbq = ein("wbq", (DM, DM), F32)
    wbk = ein("wbk", (DM, DM), F32)
    wbv = ein("wbv", (DM, DM), F32)
    wbo = ein("wbo", (DM, DM), F32)
    wgT = ein("wgT", (DM, FH), I32)
    wuT = ein("wuT", (DM, FH), I32)
    wdT = ein("wdT", (FH, DM), I32)
    flags = ein("flags", (128, 4), F32)
    flagsb = ein("flagsb", (128, 4), BF16)
    shmat = ein("shmat", (128, 128), BF16)

    outT = nc.dram_tensor("outT", [DM, HALF], F32, kind="ExternalOutput").ap()
    ikind = "ExternalOutput" if taps else "Internal"
    x1d = nc.dram_tensor("x1d", [DM, S], F32, kind=ikind).ap()
    x2d = nc.dram_tensor("x2d", [DM, S], F32, kind=ikind).ap()
    kd = nc.dram_tensor("kd", [DM, S], BF16, kind=ikind).ap()
    vd = nc.dram_tensor("vd", [S, DM], BF16, kind=ikind).ap()
    qd = nc.dram_tensor("qd", [DM, S], BF16, kind=ikind).ap()
    od = nc.dram_tensor("od", [DM, HALF], BF16,
                        kind="ExternalOutput" if taps else "Internal").ap()
    wgbf = nc.dram_tensor("wgbf", [24, 128, 1024], BF16, kind="Internal").ap()
    wubf = nc.dram_tensor("wubf", [24, 128, 1024], BF16, kind="Internal").ap()
    wdbf = nc.dram_tensor("wdbf", [8, 128, 3072], BF16, kind="Internal").ap()

    LIMIT = 212832
    SB_BASE = 16512

    def sb(name, shape, dtype, off):
        nb = int(np.prod(shape[1:])) * mybir.dt.size(dtype)
        assert off >= 0 and off + nb <= LIMIT, (name, off, nb)
        assert SB_BASE + off + nb <= nc.sbuf_top, (name, off, nb)
        return nc.alloc_sbuf_tensor_at(name, list(shape), dtype,
                                       offset=SB_BASE + off)

    t_cols = sb("cols", [128, 48], F32, 0)
    t_onesb = sb("onesb", [128, 16], BF16, 192)
    t_onesr = sb("onesr", [1, 128], F32, 224)
    t_r128 = sb("r128s", [128, 128], BF16, 736)
    t_flags = sb("flagss", [128, 4], F32, 992)
    t_flagsb = sb("flagsbs", [128, 4], BF16, 1024)
    t_rows = sb("rowsP", [1, 2048], F32, 1056)
    t_maskA = sb("maskAs", [128, 2048], BF16, 9248)
    t_maskA0 = sb("maskA0s", [128, 2048], BF16, 13344)
    t_bvbc = sb("bvbc", [128, DM], F32, 17440)
    t_shift = sb("shifts", [128, 128], BF16, 21536)
    PH = 22 * KB

    o = PH
    a_w = sb("a_w", [128, 4 * 8192], BF16, o); o += 64 * KB
    a_x0 = sb("a_x0", [128, 8 * NT], F32, o); o += 16 * KB
    a_xb = sb("a_xb", [128, 2 * NT], BF16, o); o += 2 * KB
    a_sq = sb("a_sq", [128, 2 * NT], BF16, o); o += 2 * KB
    a_xn = sb("a_xn", [128, 8 * NT], BF16, o); o += 8 * KB
    a_abc = sb("a_abc", [128, NT], F32, o); o += 2 * KB
    a_bbc = sb("a_bbc", [128, NT], F32, o); o += 2 * KB
    a_tmp = sb("a_tmp", [128, NT], F32, o); o += 2 * KB
    a_qf = sb("a_qf", [128, NT], F32, o); o += 2 * KB
    a_qb = sb("a_qb", [128, NT], BF16, o); o += 1 * KB
    a_qpf = sb("a_qpf", [128, NT], F32, o); o += 2 * KB
    a_t1 = sb("a_t1", [128, NT], F32, o); o += 2 * KB
    a_t2 = sb("a_t2", [128, NT], F32, o); o += 2 * KB
    a_qr = sb("a_qr", [128, 8 * NT], BF16, o); o += 8 * KB
    a_cos = sb("a_cos", [128, NT], F32, o); o += 2 * KB
    a_sin = sb("a_sin", [128, NT], F32, o); o += 2 * KB
    a_kring = sb("a_kring", [128, 8 * 8 * 128], BF16, o); o += 16 * KB
    a_vring = sb("a_vring", [128, 8 * 8 * VSLOT], BF16, o); o += 8 * 8 * VSLOT * 2
    a_p = sb("a_p", [128, 2048], BF16, o); o += 4 * KB
    a_pm = sb("a_pm", [128, 2 * 2048], BF16, o); o += 8 * KB
    a_oa = sb("a_oa", [128, 8 * NT], BF16, o); o += 8 * KB
    a_rb = sb("a_rb", [128, NT], F32, o); o += 2 * KB
    a_x1 = sb("a_x1", [128, 2 * NT], F32, o); o += 4 * KB
    a_lnx = sb("a_lnx", [128, NT], F32, o); o += 2 * KB
    a_xr0 = sb("a_xr0", [128, NT], F32, o); o += 2 * KB
    assert o <= LIMIT, o

    i_wstage = sb("i_wstage", [128, 12288], I32, LIMIT - 48 * KB)
    i_wcvt = sb("i_wcvt", [128, 12288], BF16, LIMIT - 72 * KB)
    i_astage = sb("i_astage", [128, 8192], F32, LIMIT - 104 * KB)

    o = PH
    b_w = sb("b_w", [128, 3 * 8192], BF16, o); o += 48 * KB
    b_x1 = sb("b_x1", [128, 2 * 8 * NT], F32, o); o += 32 * KB
    b_xb = sb("b_xb", [128, 2 * NT], BF16, o); o += 2 * KB
    b_sq = sb("b_sq", [128, 2 * NT], BF16, o); o += 2 * KB
    b_xn = sb("b_xn", [128, 2 * 8 * NT], BF16, o); o += 16 * KB
    b_abc = sb("b_abc", [128, NT], F32, o); o += 2 * KB
    b_bbc = sb("b_bbc", [128, NT], F32, o); o += 2 * KB
    b_tmp = sb("b_tmp", [128, NT], F32, o); o += 2 * KB
    b_wg = sb("b_wg", [128, 2 * 1024], BF16, o); o += 4 * KB
    b_wu = sb("b_wu", [128, 2 * 1024], BF16, o); o += 4 * KB
    b_sg = sb("b_sg", [128, 2 * NT], BF16, o); o += 2 * KB
    b_h = sb("b_h", [128, 24 * NT], BF16, o); o += 24 * KB
    b_wd = sb("b_wd", [128, 2 * 3072], BF16, o); o += 12 * KB
    b_x2 = sb("b_x2", [128, 8 * NT], F32, o); o += 16 * KB
    b_xn3 = sb("b_xn3", [128, 8 * NT], BF16, o); o += 8 * KB
    b_xn3b = sb("b_xn3b", [128, 8 * NT], BF16, o); o += 8 * KB
    assert o <= LIMIT, o
    b_rows2 = sb("b_rows2", [1, 2048], F32, 9248)
    b_kv = sb("b_kv", [128, 4 * NT], BF16, 17440)
    b_fstage = sb("b_fstage", [128, 8192], F32,
                  PH + 48 * KB + 16 * KB + 2 * KB + 2 * KB + 8 * KB
                  + 2 * KB + 2 * KB + 2 * KB + 4 * KB + 4 * KB + 1 * KB)

    o = PH
    c_q = sb("c_q", [128, 8 * HALF], BF16, o); o += 64 * KB
    C_O_OFF = o
    c_o = sb("c_o", [128, 8 * HALF], BF16, o); o += 64 * KB
    c_wo = sb("c_wo", [128, 8192], BF16, o); o += 16 * KB
    C_K_OFF = o
    c_st0 = sb("c_st0", [128, 8 * NT], BF16, C_K_OFF)
    c_st1 = sb("c_st1", [128, 8 * NT], BF16, C_K_OFF + 8 * KB)
    c_k = sb("c_k", [128, S], BF16, C_K_OFF); o += 16 * KB
    c_ve = sb("c_ve", [128, 64 * 65], BF16, o); o += 64 * 65 * 2
    c_vo = sb("c_vo", [128, 64 * 65], BF16, o); o += 64 * 65 * 2
    c_p = sb("c_p", [128, 2 * 2048], BF16, o); o += 8 * KB
    assert o <= LIMIT, o
    c_rb = sb("c_rb", [128, NT], F32, 17440)
    c_lnx = sb("c_lnx", [128, NT], F32, 13344)
    c_ob = sb("c_ob", [128, NT], BF16, 15392)
    c_fstage = sb("c_fstage", [128, 8192], F32, C_O_OFF)

    o = PH
    d_x3 = sb("d_x3", [128, 8 * NT], F32, o); o += 16 * KB
    d_xs0 = sb("d_xs0", [128, 8 * NT], F32, o); o += 16 * KB
    d_xn = sb("d_xn", [128, 8 * NT], BF16, o); o += 8 * KB
    d_xb = sb("d_xb", [128, 2 * NT], BF16, o); o += 2 * KB
    d_sq = sb("d_sq", [128, 2 * NT], BF16, o); o += 2 * KB
    d_abc = sb("d_abc", [128, NT], F32, o); o += 2 * KB
    d_bbc = sb("d_bbc", [128, NT], F32, o); o += 2 * KB
    d_tmp = sb("d_tmp", [128, NT], F32, o); o += 2 * KB
    d_sg = sb("d_sg", [128, 2 * NT], BF16, o); o += 2 * KB
    d_out = sb("d_out", [128, 2 * NT], F32, o); o += 4 * KB
    d_wg = sb("d_wg", [128, 2 * 1024], BF16, o); o += 4 * KB
    d_wu = sb("d_wu", [128, 2 * 1024], BF16, o); o += 4 * KB
    assert o <= PH + 64 * KB, o
    o2 = C_K_OFF
    d_h = sb("d_h", [128, 24 * NT], BF16, o2); o2 += 24 * KB
    d_wd = sb("d_wd", [128, 2 * 3072], BF16, o2); o2 += 12 * KB
    assert o2 <= LIMIT, o2
    d_xs1a = sb("d_xs1a", [128, 6 * NT], F32, 9248)
    d_xs1b = sb("d_xs1b", [128, 2 * NT], F32, C_K_OFF + 36 * KB)

    def dxs1(dd):
        if dd < 6:
            return d_xs1a[:, dd * NT:(dd + 1) * NT]
        return d_xs1b[:, (dd - 6) * NT:(dd - 5) * NT]

    sc4 = nc.alloc_psum_tensor("sc4", [128, 2048], F32)
    pvE = nc.alloc_psum_tensor("pvE", [128, 512], F32)
    pvO = nc.alloc_psum_tensor("pvO", [128, 512], F32)
    pAb = nc.alloc_psum_tensor("pAb", [128, 512], F32)
    pBb = nc.alloc_psum_tensor("pBb", [128, 512], F32)

    p = Plan()
    for s in ("ld", "wv", "mm", "ve", "ac", "gp", "s2", "s3", "fin"):
        p.sem(s)
    for s in ("x1w0", "x1w1"):
        p.sem(s)
    for i in range(8):
        p.sem(f"x2w{i}")
    for i in range(4):
        p.sem(f"kvw{i}")
    p.sem("ow0"); p.sem("ow1")
    p.sem("xr0"); p.sem("xr1")
    for s in ("wvg0", "wvg1", "wvd0", "wvd1"):
        p.sem(s)
    for s in ("wva0", "wva1", "wvf0", "wvf1", "s30", "s31", "bx1"):
        p.sem(s)

    G = {"sc": (), "sc0": (), "sc1": (), "pvE": (), "pvO": (),
         "pA": (), "pB": (), "lnb": (), "rows": (), "rows2": (), "abc": ()}

    p.dma(t_cols[:], acols)
    p.dma(t_r128[:], r128)
    p.dma(t_maskA[:], maskA)
    p.dma(t_maskA0[:], maskA0)
    p.dma(t_flags[:], flags)
    tok_cfl = p.dma(t_flagsb[:], flagsb)
    p.dma(t_shift[:], shmat)
    p.op("vector", lambda e: e.memset(t_onesb[:], 1.0))
    p.op("vector", lambda e: e.memset(t_onesr[:], 1.0))
    tok_ones = p.mark("vector", [("ve", 1)])

    tok_bvr = p.dma(i_astage[0:1, 0:DM], bvrow)
    for j in range(2):
        p.op("tensor", lambda e, j=j: e.matmul(
            out=sc4[:, j * 512:(j + 1) * 512], lhsT=t_onesr[0:1, :],
            rhs=i_astage[0:1, j * 512:(j + 1) * 512], start=True, stop=True),
            wait=tok_bvr + tok_ones if j == 0 else ())
    tok_bvm = p.mark("tensor", [("mm", 1)])
    tok_bvc = p.op("vector", lambda e: e.tensor_copy(
        out=t_bvbc[:], in_=sc4[:, 0:1024]), wait=tok_bvm, inc=[("ve", 1)])
    G["sc"] = tok_bvc

    cvt_done = {}
    for u in range(8):
        wi, h = u // 2, u % 2
        wsrc = (waqT, wakT, wavT, waoT)[wi]
        sl = (u % 2) * 4096
        tok_l = p.dma(i_astage[:, sl:sl + 4096]
                      .rearrange("p (dc f) -> p dc f", dc=4),
                      wsrc.rearrange("(dc p) f -> p dc f", p=128)
                      [:, h * 4:h * 4 + 4, :],
                      sem=f"wva{u % 2}", wait=(tok_bvc if u == 0 else ())
                      + cvt_done.get(u - 2, ()),
                      eng=("gpsimd", "scalar")[u % 2])
        cvt_done[u] = p.op("vector", lambda e, wi=wi, h=h, sl=sl: e.tensor_copy(
            out=a_w[:, wi * 8192 + h * 4096:wi * 8192 + h * 4096 + 4096],
            in_=i_astage[:, sl:sl + 4096]),
            wait=tok_l, inc=[("ve", 1)])
    tok_aw = cvt_done[6] + cvt_done[7]

    units = []
    for wsrc, wdst, n_in, n_out in ((wgT, wgbf, 8, 24), (wuT, wubf, 8, 24),
                                    (wdT, wdbf, 24, 8)):
        for ic in range(n_in):
            units.append((wsrc, wdst, ic, n_out))
    f_cvt = {}
    f_store = {}
    for u, (wsrc, wdst, ic, n_out) in enumerate(units):
        sl = (u % 2) * 6144
        nh = n_out * 128 // 2
        tok_l = ()
        for h in range(2):
            tok_l = tok_l + p.dma(
                i_wstage[:, sl + h * nh:sl + (h + 1) * nh],
                wsrc[ic * 128:(ic + 1) * 128, h * nh:(h + 1) * nh],
                sem=f"wvf{u % 2}", wait=f_cvt.get(u - 2, ()),
                eng=("gpsimd", "scalar")[h])
        tok_c = p.op("vector", lambda e, sl=sl, n=n_out: e.tensor_copy(
            out=i_wcvt[:, sl:sl + n * 128], in_=i_wstage[:, sl:sl + n * 128]),
            wait=tok_l + f_store.get(u - 2, ()), inc=[("ve", 1)])
        f_cvt[u] = tok_c
        tok_s = ()
        stp = 8
        for o0 in range(0, n_out, stp):
            tok_s = tok_s + p.dma(
                wdst[o0:o0 + stp, :, ic * 128:(ic + 1) * 128]
                .rearrange("oc p f -> p oc f"),
                i_wcvt[:, sl + o0 * 128:sl + (o0 + stp) * 128]
                .rearrange("p (oc f) -> p oc f", f=128),
                sem=f"s3{u % 2}", wait=tok_c if o0 == 0 else ())
        f_store[u] = tok_s
    tok_wcache = f_store[len(units) - 1] + f_store[len(units) - 2]

    p.op("vector", lambda e: e.memset(a_kring[:], 0.0),
         wait=tok_wcache + tok_aw)
    p.op("vector", lambda e: e.memset(a_vring[:], 0.0))
    p.op("vector", lambda e: e.memset(
        a_vring[:].rearrange("p (s f) -> p s f", f=VSLOT)[:, :, 64:65], 1.0))
    p.op("vector", lambda e: e.memset(
        a_vring[:].rearrange("p (s f) -> p s f", f=VSLOT)[:, :, 129:130], 1.0))
    tok_ring = p.mark("vector", [("ve", 1)])

    def ln_stats(xF, xbT, sqT, wait_first, bf_in=False):
        s1 = sc4[:, 512:1024]
        s2 = sc4[:, 1024:1536]
        mm_toks = {}
        for dc in range(8):
            sl = (dc % 2) * NT
            w = wait_first if dc == 0 else ()
            w = w + mm_toks.get(dc - 2, ())
            if bf_in:
                tsq = p.op("gpsimd", lambda e, d=dc, s=sl: e.tensor_mul(
                    out=sqT[:, s:s + NT], in0=xF(d), in1=xF(d)),
                    wait=w, inc=[("gp", 1)])
                srcap = xF(dc)
                tdat = tsq
            else:
                tdat = p.op("vector", lambda e, d=dc, s=sl: e.tensor_copy(
                    out=xbT[:, s:s + NT], in_=xF(d)), wait=w, inc=[("ve", 1)])
                tsq = p.op("gpsimd", lambda e, d=dc, s=sl: e.tensor_mul(
                    out=sqT[:, s:s + NT], in0=xbT[:, s:s + NT],
                    in1=xbT[:, s:s + NT]), wait=tdat, inc=[("gp", 1)])
                srcap = xbT[:, sl:sl + NT]
            wmm = (wait_first + G["lnb"] + G["sc"]) if dc == 0 else ()
            p.op("tensor", lambda e, d=dc, s=srcap: e.matmul(
                out=s1[0:1, 0:NT], lhsT=t_onesb[:, 0:1], rhs=s,
                start=(d == 0), stop=(d == 7)), wait=wmm + tdat)
            tok2 = p.op("tensor", lambda e, d=dc, s=(dc % 2) * NT: e.matmul(
                out=s2[0:1, 0:NT], lhsT=t_onesb[:, 0:1], rhs=sqT[:, s:s + NT],
                start=(d == 0), stop=(d == 7)), wait=tsq, inc=[("mm", 1)])
            mm_toks[dc] = tok2
        return mm_toks[7]

    def ln_rows(tok_stats, rows=None, rkey="rows"):
        rows = t_rows if rows is None else rows
        s1 = sc4[:, 512:1024]
        s2 = sc4[:, 1024:1536]
        rmean = rows[0:1, 0:NT]
        rvar = rows[0:1, NT:2 * NT]
        rsd = rows[0:1, 2 * NT:3 * NT]
        rb_ = rows[0:1, 3 * NT:4 * NT]
        p.op("vector", lambda e: e.tensor_scalar_mul(
            out=rmean, in0=s1[0:1, 0:NT], scalar1=1.0 / DM),
            wait=tok_stats + G[rkey])
        p.op("vector", lambda e: e.tensor_mul(out=rvar, in0=rmean, in1=rmean))
        p.op("vector", lambda e: e.scalar_tensor_tensor(
            out=rvar, in0=s2[0:1, 0:NT], scalar=1.0 / DM, in1=rvar,
            op0=OP.mult, op1=OP.subtract))
        tok_v = p.mark("vector", [("ve", 1)])
        G["lnb"] = tok_v
        p.op("scalar", lambda e: e.activation(
            out=rsd, in_=rvar, func=AF.Ln, bias=f32(EPS)), wait=tok_v)
        tok_sd = p.op("scalar", lambda e: e.activation(
            out=rsd, in_=rsd, func=AF.Exp, scale=f32(-0.5)), inc=[("ac", 1)])
        p.op("vector", lambda e: e.scalar_tensor_tensor(
            out=rb_, in0=rmean, scalar=-1.0, in1=rsd, op0=OP.mult, op1=OP.mult),
            wait=tok_sd)
        return p.mark("vector", [("ve", 1)])

    def ln_norm(xF, xnO, abcT, bbcT, tmpT, tok_rows, norm_wait=(),
                rows=None, rkey="rows", neng="gpsimd", tmp_bank3=False,
                abanks=False):
        rows = t_rows if rows is None else rows
        rsd = rows[0:1, 2 * NT:3 * NT]
        rb_ = rows[0:1, 3 * NT:4 * NT]
        if abanks:
            bA = sc4[:, 0:512]
            bB = sc4[:, 512:1024]
            wA = tok_rows + G["sc"] + G["sc0"] + G["sc1"]
        else:
            bA = sc4[:, 1536:2048]
            bB = pvE[:, 0:NT]
            wA = tok_rows + G["pvE"]
        p.op("tensor", lambda e: e.matmul(
            out=bA, lhsT=t_onesr[0:1, :], rhs=rsd,
            start=True, stop=True), wait=wA)
        tok_bc = p.op("tensor", lambda e: e.matmul(
            out=bB, lhsT=t_onesr[0:1, :], rhs=rb_,
            start=True, stop=True), inc=[("mm", 1)])
        G[rkey] = tok_bc
        p.op("vector", lambda e, bA=bA: e.tensor_copy(out=abcT[:, 0:NT],
                                                      in_=bA),
             wait=tok_bc + G["abc"])
        tok_cp = p.op("vector", lambda e, bB=bB: e.tensor_copy(
            out=bbcT[:, 0:NT], in_=bB), inc=[("ve", 1)])
        if tmp_bank3:
            tmpT = sc4[:, 1536:2048]
        nsem = {"gpsimd": "gp", "vector": "ve"}[neng]
        for dc in range(8):
            p.op(neng, lambda e, d=dc: e.tensor_mul(
                out=tmpT[:, 0:NT], in0=xF(d), in1=abcT[:, 0:NT]),
                wait=(tok_cp + norm_wait) if dc == 0 else ())
            p.op(neng, lambda e, d=dc: e.tensor_add(
                out=xnO(d), in0=tmpT[:, 0:NT], in1=bbcT[:, 0:NT]))
        tok_n = p.mark(neng, [(nsem, 1)])
        G["abc"] = tok_n
        if abanks:
            G["sc0"] = tok_cp
            G["sc1"] = tok_cp
            G["lnb"] = G["lnb"] + tok_cp
        else:
            G["pvE"] = tok_n if tmp_bank3 else tok_cp
        return tok_n

    def layer_norm(xF, xnO, xbT, sqT, abcT, bbcT, tmpT, wait_first,
                   bf_in=False, norm_wait=()):
        tok_stats = ln_stats(xF, xbT, sqT, wait_first, bf_in=bf_in)
        tok_rows = ln_rows(tok_stats)
        tok_n = ln_norm(xF, xnO, abcT, bbcT, tmpT, tok_rows,
                        norm_wait=norm_wait)
        return tok_n, ()

    tok_attn_done = tok_ring
    tok_x1_dma = {}
    tok_qf_free = ()
    tok_qpf_free = ()
    tok_qb_free = ()
    tok_t2_free = ()
    tok_sum = ()
    a_pre = {}
    a_xn_free = ()
    a_qk_last = ()
    a_stt = {}

    def emit_SA(ti):
        t0 = ti * NT
        if ti == 0:
            wt = tok_aw + tok_ring
        else:
            prev = a_pre[ti - 1]
            wt = prev["norm"] + prev["stats"] + a_qk_last
        p.dma(a_cos[:], cosT[:, t0:t0 + NT], wait=wt)
        p.dma(a_sin[:], sinT[:, t0:t0 + NT])
        tok_x0 = p.dma(a_x0[:].rearrange("p (dc t) -> p dc t", dc=8),
                       x0T.rearrange("(dc p) t -> p dc t", p=128)
                       [:, :, t0:t0 + NT])

        def ax(d):
            return a_x0[:, d * NT:(d + 1) * NT]

        wf = tok_x0 + (tok_attn_done if ti == 0 else ())
        tok_stats = ln_stats(ax, a_xb, a_sq, wf)
        tok_rows = ln_rows(tok_stats)
        a_pre[ti] = {"ax": ax, "stats": tok_stats, "rows": tok_rows,
                     "norm": ()}

    emit_SA(0)
    for ti in range(na):
        t0 = ti * NT
        b0 = 4 * ti
        s0 = b0 % 8
        pre = a_pre[ti]
        ax = pre["ax"]
        tok_ln = ln_norm(ax, lambda d: a_xn[:, d * NT:(d + 1) * NT],
                         a_abc, a_bbc, a_tmp, pre["rows"],
                         norm_wait=a_xn_free, abanks=True)
        pre["norm"] = tok_ln
        tok_cs = pre["rows"]

        if asub < 2:
            continue
        pend_rope = None

        def emit_ropetail():
            nonlocal pend_rope, tok_qb_free, tok_qpf_free, tok_qf_free, \
                tok_t2_free, tok_sum
            if pend_rope is None:
                return
            bank2s, bname2, tok_ev, tok_cast, dst, wt_extra = pend_rope
            pend_rope = None
            tok_pm = p.op("tensor", lambda e, b2=bank2s: e.matmul(
                out=b2, lhsT=t_r128[:], rhs=a_qb[:],
                start=True, stop=True),
                wait=tok_cast + G[bname2], inc=[("mm", 1)])
            tok_qb_free = tok_pm
            tok_evp = p.op("scalar", lambda e, b2=bank2s: e.copy(
                out=a_qpf[:], in_=b2),
                wait=tok_pm + tok_qpf_free, inc=[("ac", 1)])
            G[bname2] = tok_evp
            tok_t1 = p.op("vector", lambda e: e.tensor_mul(
                out=a_t1[:], in0=a_qf[:], in1=a_cos[:]),
                wait=tok_ev + tok_cs, inc=[("ve", 1)])
            tok_qf_free = tok_t1
            tok_t2 = p.op("gpsimd", lambda e: e.tensor_mul(
                out=a_t2[:], in0=a_qpf[:], in1=a_sin[:]),
                wait=tok_evp + tok_t2_free, inc=[("gp", 1)])
            tok_qpf_free = tok_t2
            tok_sum = p.op("vector", lambda e, d=dst: e.tensor_add(
                out=d, in0=a_t1[:], in1=a_t2[:]),
                wait=tok_t1 + tok_t2 + wt_extra, inc=[("ve", 1)])
            tok_t2_free = tok_sum

        for which in range(2):
            for oc in range(8):
                bank, bname = (pAb, "pA") if oc % 2 == 0 else (pBb, "pB")
                bank2s = sc4[:, (oc % 2) * 512:(oc % 2) * 512 + 512]
                bname2 = "sc0" if oc % 2 == 0 else "sc1"
                for dc in range(8):
                    wt = ()
                    if dc == 0:
                        wt = G[bname]
                        if oc == 0 and which == 0:
                            wt = wt + tok_ln
                    p.op("tensor", lambda e, d=dc, o_=oc, b=bank, w=which * 8192:
                         e.matmul(out=b[:, 0:NT],
                                  lhsT=a_w[:, w + d * 1024 + o_ * 128:
                                           w + d * 1024 + o_ * 128 + 128],
                                  rhs=a_xn[:, d * NT:(d + 1) * NT],
                                  start=(d == 0), stop=(d == 7)), wait=wt)
                tok_mm = p.mark("tensor", [("mm", 1)])
                emit_ropetail()
                bcol = (0 if which == 0 else 8) + oc
                tok_ev = p.op("scalar", lambda e, b=bank, bc=bcol: e.activation(
                    out=a_qf[:], in_=b[:, 0:NT], func=AF.Identity,
                    bias=t_cols[:, bc:bc + 1], scale=f32(1.0)),
                    wait=tok_mm + tok_qf_free, inc=[("ac", 1)])
                tok_cast = p.op("vector", lambda e: e.tensor_copy(
                    out=a_qb[:], in_=a_qf[:]),
                    wait=tok_ev + tok_qb_free, inc=[("ve", 1)])
                G[bname] = tok_cast
                if which == 0:
                    dst = a_qr[:, oc * NT:(oc + 1) * NT]
                else:
                    dst = a_kring[:, oc * 1024 + s0 * 128:
                                  oc * 1024 + s0 * 128 + 512]
                wt_extra = tok_attn_done if which == 1 and oc == 0 else ()
                pend_rope = (bank2s, bname2, tok_ev, tok_cast, dst, wt_extra)
        emit_ropetail()
        tok_qk = tok_sum
        a_qk_last = tok_qk

        if asub < 3:
            continue
        for tb in range(4):
            slot = (s0 + tb) % 8
            for fh in range(2):
                bank, bname = (pAb, "pA") if fh == 0 else (pBb, "pB")
                for dc in range(8):
                    wt = G[bname] if dc == 0 else ()
                    if dc == 0 and tb == 0 and fh == 0:
                        wt = wt + tok_attn_done
                    p.op("tensor", lambda e, d=dc, t=tb, f=fh, b=bank: e.matmul(
                        out=b[:, 0:512],
                        lhsT=a_xn[:, d * NT + t * 128: d * NT + t * 128 + 128],
                        rhs=a_w[:, 2 * 8192 + d * 1024 + f * 512:
                                 2 * 8192 + d * 1024 + f * 512 + 512],
                        start=(d == 0), stop=(d == 7)), wait=wt)
                tok_mm = p.mark("tensor", [("mm", 1)])
                tok_vv = ()
                for hh in range(8):
                    h = 8 * fh + hh
                    hp = h // 2
                    base = (slot * 8 + hp) * VSLOT
                    off = base + (0 if h % 2 == 0 else 65)
                    wt2 = tok_mm if hh == 0 else ()
                    tok_vv = p.op("vector", lambda e, b=bank, hh=hh, o_=off, f=fh:
                                  e.tensor_add(
                                      out=a_vring[:, o_:o_ + 64],
                                      in0=b[:, hh * 64:(hh + 1) * 64],
                                      in1=t_bvbc[:, f * 512 + hh * 64:
                                                 f * 512 + hh * 64 + 64]),
                                  wait=wt2, inc=[("ve", 1)] if hh == 7 else ())
                G[bname] = tok_vv
        tok_vdone = tok_vv
        a_xn_free = tok_mm
        if ti + 1 < na:
            emit_SA(ti + 1)

        if asub < 4:
            continue
        mask_t = t_maskA0 if ti == 0 else t_maskA
        tok_fin = tok_attn_done
        tok_qkm_h = {}
        tok_exp_h = {}
        tok_msk_h = {}
        tok_pvm_h = {}

        def a_qk(hp, wt0):
            cnt = 0
            for j in range(4):
                for ch in range(2):
                    slot = (b0 + j - 1 + ch) % 8
                    for eo in range(2):
                        kb = hp * 1024 + slot * 128
                        oo = eo * 1024 + j * 256 + ch * 128
                        wt = wt0 if cnt == 0 else ()
                        cnt += 1
                        p.op("tensor", lambda e, kb=kb, oo=oo, eo=eo, hp=hp, j=j:
                             e.matmul(out=sc4[:, oo:oo + 128],
                                      lhsT=a_kring[eo * 64:eo * 64 + 64,
                                                   kb:kb + 128],
                                      rhs=a_qr[eo * 64:eo * 64 + 64,
                                               hp * NT + j * 128:
                                               hp * NT + j * 128 + 128],
                                      start=True, stop=True), wait=wt)
            return p.mark("tensor", [("mm", 1)])

        def a_pv(hp):
            if asub < 5:
                return ()
            pb = (hp % 2) * 2048
            first = True
            for j in range(4):
                for ch in range(2):
                    slot = (b0 + j - 1 + ch) % 8
                    vb = (slot * 8 + hp) * VSLOT
                    wt = ()
                    if first:
                        wt = tok_msk_h[hp] + G["pvE"] + G["pvO"] + tok_fin
                        first = False
                    pe = a_pm[:, pb + j * 256 + ch * 128:
                              pb + j * 256 + ch * 128 + 128]
                    po = a_pm[:, pb + 1024 + j * 256 + ch * 128:
                              pb + 1024 + j * 256 + ch * 128 + 128]
                    p.op("tensor", lambda e, vb=vb, j=j, ch=ch, pe=pe: e.matmul(
                        out=pvE[0:65, j * 128:(j + 1) * 128],
                        lhsT=a_vring[:, vb:vb + 65],
                        rhs=pe, start=(ch == 0), stop=(ch == 1)), wait=wt)
                    p.op("tensor", lambda e, vb=vb, j=j, ch=ch, po=po: e.matmul(
                        out=pvO[0:65, j * 128:(j + 1) * 128],
                        lhsT=a_vring[:, vb + 65:vb + 130],
                        rhs=po, start=(ch == 0), stop=(ch == 1)))
            return p.mark("tensor", [("mm", 1)])

        def a_fin(hp):
            tok_pvm = tok_pvm_h[hp]
            p.op("vector", lambda e: e.tensor_copy(
                out=t_rows[0:1, 0:512], in_=pvE[64:65, :]), wait=tok_pvm)
            tok_r = p.op("vector", lambda e: e.tensor_copy(
                out=t_rows[0:1, 512:1024], in_=pvO[64:65, :]), inc=[("ve", 1)])
            tok_ob = p.op("vector", lambda e: e.tensor_copy(
                out=a_qb[0:64, :], in_=pvO[0:64, :]), inc=[("ve", 1)])
            p.op("tensor", lambda e: e.matmul(
                out=pAb[0:64, :], lhsT=t_onesr[0:1, 0:64],
                rhs=t_rows[0:1, 0:512], start=True, stop=True),
                wait=tok_r + G["pA"])
            tok_bcm = p.op("tensor", lambda e: e.matmul(
                out=pAb[64:128, :], lhsT=t_onesr[0:1, 0:64],
                rhs=t_rows[0:1, 512:1024], start=True, stop=True),
                inc=[("mm", 1)])
            tok_sh = p.op("tensor", lambda e: e.matmul(
                out=pBb[:, :], lhsT=t_shift[0:64, :], rhs=a_qb[0:64, :],
                start=True, stop=True),
                wait=tok_ob + G["pB"], inc=[("mm", 1)])
            p.op("scalar", lambda e: e.activation(
                out=a_lnx[:], in_=pAb[:, :], func=AF.Ln), wait=tok_bcm)
            tok_rbc = p.op("scalar", lambda e: e.activation(
                out=a_rb[:], in_=a_lnx[:], func=AF.Exp, scale=f32(-1.0)),
                inc=[("ac", 1)])
            G["pA"] = tok_rbc
            p.op("vector", lambda e, hp=hp: e.tensor_mul(
                out=a_oa[0:64, hp * NT:(hp + 1) * NT],
                in0=pvE[0:64, :], in1=a_rb[0:64, :]), wait=tok_rbc)
            tok_f = p.op("vector", lambda e, hp=hp: e.tensor_mul(
                out=a_oa[64:128, hp * NT:(hp + 1) * NT],
                in0=pBb[64:128, :], in1=a_rb[64:128, :]),
                wait=tok_sh, inc=[("ve", 1)])
            G["pvE"] = tok_f
            G["pvO"] = tok_ob
            G["pB"] = tok_f
            return tok_f

        for hp in range(8):
            wt0 = tok_exp_h.get(hp - 1, ())
            if hp == 0:
                wt0 = wt0 + tok_qk + tok_vdone + G["sc"] + G["lnb"]
            tok_qkm_h[hp] = a_qk(hp, wt0)
            tok_exp_h[hp] = p.op("scalar", lambda e: e.activation(
                out=a_p[:], in_=sc4[:, :], func=AF.Exp,
                scale=f32(0.125), bias=f32(-C_A)),
                wait=tok_qkm_h[hp] + tok_msk_h.get(hp - 1, ()), inc=[("ac", 1)])
            tok_msk_h[hp] = p.op("gpsimd", lambda e, m=mask_t, hp=hp: e.tensor_mul(
                out=a_pm[:, (hp % 2) * 2048:(hp % 2) * 2048 + 2048],
                in0=a_p[:], in1=m[:]),
                wait=tok_exp_h[hp] + tok_pvm_h.get(hp - 2, ()), inc=[("gp", 1)])
            if hp >= 1 and asub >= 5:
                tok_pvm_h[hp - 1] = a_pv(hp - 1)
                if asub >= 6:
                    tok_fin = a_fin(hp - 1)
        G["sc"] = tok_exp_h[7]
        if asub >= 5:
            tok_pvm_h[7] = a_pv(7)
            if asub >= 6:
                tok_fin = a_fin(7)
        if asub < 7:
            continue
        tok_x1_last = ()
        for oc in range(8):
            bank, bname = (pAb, "pA") if oc % 2 == 0 else (pBb, "pB")
            for dc in range(8):
                wt = ()
                if dc == 0:
                    wt = G[bname]
                    if oc == 0:
                        wt = wt + tok_fin
                p.op("tensor", lambda e, d=dc, o_=oc, b=bank: e.matmul(
                    out=b[:, 0:NT],
                    lhsT=a_w[:, 3 * 8192 + d * 1024 + o_ * 128:
                             3 * 8192 + d * 1024 + o_ * 128 + 128],
                    rhs=a_oa[:, d * NT:(d + 1) * NT],
                    start=(d == 0), stop=(d == 7)), wait=wt)
            tok_mm = p.mark("tensor", [("mm", 1)])
            sl = oc % 2
            xrt = a_xr0 if sl == 0 else a_qpf
            wxr = a_stt.get(sl, ())
            if oc < 2:
                wxr = wxr + tok_qk + tok_t2_free
            tok_xr = p.dma(xrt[:], x0T[oc * 128:(oc + 1) * 128, t0:t0 + NT],
                           sem=f"xr{sl}", wait=wxr)
            tok_x1 = p.op("vector", lambda e, b=bank, o_=oc, x=xrt: e.scalar_tensor_tensor(
                out=a_x1[:, (o_ % 2) * NT:(o_ % 2) * NT + NT],
                in0=b[:, 0:NT], scalar=t_cols[:, 16 + o_:17 + o_],
                in1=x[:], op0=OP.add, op1=OP.add),
                wait=tok_mm + tok_x1_dma.get(oc % 2, ()) + tok_xr,
                inc=[("ve", 1)])
            a_stt[sl] = tok_x1
            G[bname] = tok_x1
            tok_x1_dma[oc % 2] = p.dma(x1d[oc * 128:(oc + 1) * 128, t0:t0 + NT],
                                       a_x1[:, (oc % 2) * NT:(oc % 2) * NT + NT],
                                       sem=f"x1w{oc % 2}", wait=tok_x1,
                                       eng="gpsimd")
            tok_x1_last = tok_x1
        tok_qpf_free = tok_qpf_free + a_stt[1]
        tok_attn_done = tok_fin + tok_x1_last

    tok_x1_all = (("x1w0", p.cnt["x1w0"]), ("x1w1", p.cnt["x1w1"]))

    if "B" not in phases:
        _finish(nc, p)
        return nc

    gate = tok_attn_done
    for wi, wsrc in enumerate((wbk, wbv, wbq)):
        tok_l = ()
        for h in range(2):
            tok_l = tok_l + p.dma(
                b_fstage[:, h * 4096:(h + 1) * 4096]
                .rearrange("p (dc f) -> p dc f", dc=4),
                wsrc.rearrange("(dc p) f -> p dc f", p=128)
                [:, h * 4:h * 4 + 4, :],
                sem="wv", wait=gate if h == 0 else ())
        gate = p.op("vector", lambda e, wi=wi: e.tensor_copy(
            out=b_w[:, wi * 8192:(wi + 1) * 8192], in_=b_fstage[:]),
            wait=tok_l, inc=[("ve", 1)])
    tok_bw = gate

    Bst = {"hdone": (), "x2t": (), "gb": (), "x2done_mm": (), "xn3f": (),
           "sb_stats": ()}
    tok_x1_free = ()
    tok_x2_dma = {}
    wgu_free = {}
    wd_free = {}
    sg_free = {}
    tok_kvbuf = {}
    x1buf_free = {}
    xn_free = {}
    b_pre = {}

    sb_dat = {}

    sb_load = {}

    def emit_SB_load(ti):
        t0 = ti * NT
        xb1 = (ti % 2) * 8 * NT
        wt0 = (tok_bw + tok_x1_all) if ti == 0 else x1buf_free.get(ti % 2, ())
        sb_load[ti] = p.dma(b_x1[:, xb1:xb1 + 8 * NT]
                            .rearrange("p (dc t) -> p dc t", dc=8),
                            x1d.rearrange("(dc p) t -> p dc t", p=128)
                            [:, :, t0:t0 + NT], wait=wt0, sem="bx1")

    def emit_SB_a(ti):
        xb1 = (ti % 2) * 8 * NT
        tok_x1 = sb_load.pop(ti)

        def bx(d, xb1=xb1):
            return b_x1[:, xb1 + d * NT:xb1 + (d + 1) * NT]

        tok_stats = ln_stats(bx, b_xb, b_sq,
                             tok_x1 + Bst.get("ln3_stats", ()))
        sb_dat[ti] = (bx, tok_stats)
        Bst["sb_stats"] = tok_stats

    def emit_SB_r(ti):
        bx, tok_stats = sb_dat[ti]
        sb_dat[ti] = (bx, ln_rows(tok_stats, rows=b_rows2, rkey="rows2"))

    def emit_SB_b(ti):
        bx, tok_rows = sb_dat.pop(ti)
        xo = (ti % 2) * 8 * NT
        tok_n = ln_norm(bx, lambda d, xo=xo: b_xn[:, xo + d * NT:xo + (d + 1) * NT],
                        b_abc, b_bbc, b_tmp, tok_rows,
                        norm_wait=xn_free.get(ti % 2, ()),
                        rows=b_rows2, rkey="rows2",
                        neng="vector", tmp_bank3=True)
        b_pre[ti] = (bx, tok_n)

    def emit_gu(ti):
        xn1 = (ti % 2) * 8 * NT
        bx, tok_ln2 = b_pre[ti]
        for hc in range(24):
            buf = hc % 2
            wtw = wgu_free.get(buf, ())
            if ti == 0 and hc < 2:
                wtw = wtw + tok_wcache
            tok_wl = p.dma(b_wg[:, buf * 1024:buf * 1024 + 512],
                           wgbf[hc][:, 0:512], sem=f"wvg{buf}", wait=wtw)
            tok_wl = tok_wl + p.dma(b_wg[:, buf * 1024 + 512:(buf + 1) * 1024],
                                    wgbf[hc][:, 512:1024], sem=f"wvg{buf}")
            tok_wl = tok_wl + p.dma(b_wu[:, buf * 1024:buf * 1024 + 512],
                                    wubf[hc][:, 0:512], sem=f"wvg{buf}")
            tok_wl = tok_wl + p.dma(b_wu[:, buf * 1024 + 512:(buf + 1) * 1024],
                                    wubf[hc][:, 512:1024], sem=f"wvg{buf}")
            gb, gbn = (pAb[:, 0:NT], "pA") if buf == 0 else (sc4[:, 0:NT], "sc0")
            ub, ubn = (pBb[:, 0:NT], "pB") if buf == 0 else (pvO[:, 0:NT], "pvO")
            for dc in range(8):
                wt = (tok_ln2 + tok_wl + G[gbn]) if dc == 0 else ()
                p.op("tensor", lambda e, d=dc, b=buf, gb=gb, xn1=xn1: e.matmul(
                    out=gb,
                    lhsT=b_wg[:, b * 1024 + d * 128: b * 1024 + d * 128 + 128],
                    rhs=b_xn[:, xn1 + d * NT:xn1 + (d + 1) * NT],
                    start=(d == 0), stop=(d == 7)), wait=wt)
            for dc in range(8):
                wt = G[ubn] if dc == 0 else ()
                p.op("tensor", lambda e, d=dc, b=buf, ub=ub, xn1=xn1: e.matmul(
                    out=ub,
                    lhsT=b_wu[:, b * 1024 + d * 128: b * 1024 + d * 128 + 128],
                    rhs=b_xn[:, xn1 + d * NT:xn1 + (d + 1) * NT],
                    start=(d == 0), stop=(d == 7)), wait=wt)
            tok_gu = p.mark("tensor", [("mm", 1)])
            wgu_free[buf] = tok_gu
            if hc == 23:
                xn_free[ti % 2] = tok_gu
            sgs = b_sg[:, (hc % 2) * NT:(hc % 2) * NT + NT]
            tok_sg = p.op("scalar", lambda e, gb=gb, s=sgs: e.activation(
                out=s, in_=gb, func=AF.Silu, scale=f32(DSC)),
                wait=tok_gu + sg_free.get(hc % 2, ()), inc=[("ac", 1)])
            tok_h = p.op("vector", lambda e, h=hc, ub=ub, s=sgs: e.tensor_mul(
                out=b_h[:, h * NT:(h + 1) * NT], in0=ub, in1=s),
                wait=tok_sg + (Bst["x2done_mm"] if hc == 0 else ()),
                inc=[("ve", 1)])
            sg_free[hc % 2] = tok_h
            G[gbn] = tok_sg
            G[ubn] = tok_h
            if hc == 1 and ti >= 1:
                emit_ln3_stats(ti - 1)
            if hc == 2 and ti >= 1:
                emit_ln3_rows(ti - 1)
            if hc == 3 and ti + 1 < nb:
                emit_SB_load(ti + 1)
            if hc == 4 and ti >= 1:
                emit_ln3_mid(ti - 1)
            if hc == 9 and ti + 1 < nb:
                emit_SB_a(ti + 1)
            if hc == 10 and ti + 1 < nb:
                emit_SB_r(ti + 1)
            if hc == 12 and ti + 1 < nb:
                emit_SB_b(ti + 1)
            if hc == 13 and ti >= 1:
                emit_ln3b_stats(ti - 1)
            if hc == 14 and ti >= 1:
                emit_ln3b_rows(ti - 1)
            if hc == 16 and ti >= 1:
                emit_ln3b_fin(ti - 1)
        tok_hdone = tok_h

        Bst["hdone"] = tok_hdone

    def emit_down(ti):
        t0 = ti * NT
        xb1 = (ti % 2) * 8 * NT
        tok_hdone = Bst["hdone"]
        for oc in range(8):
            buf = oc % 2
            tok_wd = p.dma(b_wd[:, buf * 3072:buf * 3072 + 1536],
                           wdbf[oc][:, 0:1536], sem=f"wvd{buf}",
                           wait=wd_free.get(buf, ()))
            tok_wd = tok_wd + p.dma(b_wd[:, buf * 3072 + 1536:(buf + 1) * 3072],
                                    wdbf[oc][:, 1536:3072], sem=f"wvd{buf}")
            bank, bname = (pAb, "pA") if oc % 2 == 0 else (pBb, "pB")
            for hc in range(24):
                wt = (tok_hdone + tok_wd + G[bname]) if hc == 0 else ()
                p.op("tensor", lambda e, h=hc, b=bank, bu=buf: e.matmul(
                    out=b[:, 0:NT],
                    lhsT=b_wd[:, bu * 3072 + h * 128: bu * 3072 + h * 128 + 128],
                    rhs=b_h[:, h * NT:(h + 1) * NT],
                    start=(h == 0), stop=(h == 23)), wait=wt)
            tok_mm = p.mark("tensor", [("mm", 1)])
            tok_x2 = p.op("vector", lambda e, b=bank, o_=oc, xb1=xb1: e.scalar_tensor_tensor(
                out=b_x2[:, o_ * NT:(o_ + 1) * NT], in0=b[:, 0:NT],
                scalar=DSC * DSC, in1=b_x1[:, xb1 + o_ * NT:xb1 + (o_ + 1) * NT],
                op0=OP.mult, op1=OP.add),
                wait=tok_mm + tok_x2_dma.get(oc, ()), inc=[("ve", 1)])
            G[bname] = tok_x2
            wd_free[buf] = tok_x2
            x1buf_free[ti % 2] = tok_x2
            tok_x2_dma[oc] = p.dma(x2d[oc * 128:(oc + 1) * 128, t0:t0 + NT],
                                   b_x2[:, oc * NT:(oc + 1) * NT],
                                   sem=f"x2w{oc}", wait=tok_x2, eng="gpsimd")
        tok_x2t = tok_x2
        tok_x1_free = tok_x2t

        Bst["x2t"] = tok_x2t
        Bst["x2done_mm"] = tok_mm

    def b2f(d):
        return b_x2[:, d * NT:(d + 1) * NT]

    def b3f(d):
        return b_xn3[:, d * NT:(d + 1) * NT]

    def emit_ln3_stats(ti):
        Bst["ln3_stats"] = ln_stats(b2f, b_xb, b_sq,
                                    Bst["x2t"] + Bst["sb_stats"])

    def emit_ln3_rows(ti):
        Bst["ln3_rows"] = ln_rows(Bst["ln3_stats"])

    def emit_ln3_mid(ti):
        Bst["ln3"] = ln_norm(b2f, lambda d: b_xn3[:, d * NT:(d + 1) * NT],
                             b_abc, b_bbc, b_tmp, Bst["ln3_rows"],
                             norm_wait=Bst["xn3f"])

    def emit_ln3b_stats(ti):
        Bst["ln3b_stats"] = ln_stats(b3f, b_xb, b_sq,
                                     Bst["ln3"] + Bst["sb_stats"], bf_in=True)

    def emit_ln3b_rows(ti):
        Bst["ln3b_rows"] = ln_rows(Bst["ln3b_stats"])

    def emit_ln3b_fin(ti):
        tok_ln3b = ln_norm(b3f, lambda d: b_xn3b[:, d * NT:(d + 1) * NT],
                           b_abc, b_bbc, b_tmp, Bst["ln3b_rows"],
                           norm_wait=Bst["xn3f"],
                           neng="vector", tmp_bank3=True)
        for dc in range(8):
            wt = tok_ln3b if dc == 0 else ()
            p.op("scalar", lambda e, d=dc: e.activation(
                out=b_xn3b[:, d * NT:(d + 1) * NT],
                in_=b_xn3b[:, d * NT:(d + 1) * NT], func=AF.Identity,
                bias=t_cols[:, 32 + d:33 + d], scale=t_cols[:, 24 + d:25 + d]),
                wait=wt)
        Bst["gb"] = p.mark("scalar", [("ac", 1)])

    def emit_kv(ti):
        t0 = ti * NT
        tok_gb = Bst["gb"]
        for wi, (wofs, dstd) in enumerate(((0, kd), (2 * 8192, qd))):
            for oc in range(8):
                bank, bname = (pAb, "pA") if oc % 2 == 0 else (pBb, "pB")
                for dc in range(8):
                    wt = G[bname] if dc == 0 else ()
                    if dc == 0 and oc == 0 and wi == 0:
                        wt = wt + tok_gb
                    p.op("tensor", lambda e, d=dc, o_=oc, b=bank, w=wofs: e.matmul(
                        out=b[:, 0:NT],
                        lhsT=b_w[:, w + d * 1024 + o_ * 128:
                                 w + d * 1024 + o_ * 128 + 128],
                        rhs=b_xn3b[:, d * NT:(d + 1) * NT],
                        start=(d == 0), stop=(d == 7)), wait=wt)
                tok_mm = p.mark("tensor", [("mm", 1)])
                sl = (wi * 8 + oc) % 4
                tok_ev = p.op("vector", lambda e, b=bank, s=sl: e.tensor_copy(
                    out=b_kv[:, s * NT:(s + 1) * NT], in_=b[:, 0:NT]),
                    wait=tok_mm + tok_kvbuf.get(sl, ()), inc=[("ve", 1)])
                G[bname] = tok_ev
                tok_kvbuf[sl] = p.dma(dstd[oc * 128:(oc + 1) * 128, t0:t0 + NT],
                                      b_kv[:, sl * NT:(sl + 1) * NT],
                                      sem=f"kvw{sl}", wait=tok_ev, eng="gpsimd")
        for tb in range(4):
            for fh in range(2):
                bank, bname = (pAb, "pA") if fh == 0 else (pBb, "pB")
                for dc in range(8):
                    wt = G[bname] if dc == 0 else ()
                    p.op("tensor", lambda e, d=dc, t=tb, f=fh, b=bank: e.matmul(
                        out=b[:, 0:512],
                        lhsT=b_xn3b[:, d * NT + t * 128: d * NT + t * 128 + 128],
                        rhs=b_w[:, 8192 + d * 1024 + f * 512:
                                 8192 + d * 1024 + f * 512 + 512],
                        start=(d == 0), stop=(d == 7)), wait=wt)
                tok_mm = p.mark("tensor", [("mm", 1)])
                sl = (tb * 2 + fh) % 4
                tok_ev = p.op("vector", lambda e, b=bank, s=sl: e.tensor_copy(
                    out=b_kv[:, s * NT:s * NT + 512], in_=b[:, 0:512]),
                    wait=tok_mm + tok_kvbuf.get(sl, ()), inc=[("ve", 1)])
                G[bname] = tok_ev
                tok_kvbuf[sl] = p.dma(vd[t0 + tb * 128:t0 + tb * 128 + 128,
                                         fh * 512:(fh + 1) * 512],
                                      b_kv[:, sl * NT:sl * NT + 512],
                                      sem=f"kvw{sl}", wait=tok_ev, eng="gpsimd")
                if tb == 3 and fh == 1:
                    Bst["xn3f"] = tok_mm


    emit_SB_load(0)
    emit_SB_a(0)
    emit_SB_r(0)
    emit_SB_b(0)
    emit_gu(0)
    for ti in range(nb):
        emit_down(ti)
        if ti + 1 < nb:
            emit_gu(ti + 1)
        else:
            emit_ln3_stats(ti)
            emit_ln3_rows(ti)
            emit_ln3_mid(ti)
            emit_ln3b_stats(ti)
            emit_ln3b_rows(ti)
            emit_ln3b_fin(ti)
        emit_kv(ti)

    tok_kv_all = tuple((f"kvw{i}", p.cnt[f"kvw{i}"]) for i in range(4))
    tok_x2_all = tuple((f"x2w{i}", p.cnt[f"x2w{i}"]) for i in range(8))
    tok_b_done = G["pA"] + G["pB"]

    if "C" not in phases:
        _finish(nc, p)
        return nc

    tok_l = ()
    for h in range(2):
        tok_l = tok_l + p.dma(
            c_fstage[:, h * 4096:(h + 1) * 4096]
            .rearrange("p (dc f) -> p dc f", dc=4),
            wbo.rearrange("(dc p) f -> p dc f", p=128)[:, h * 4:h * 4 + 4, :],
            sem="wv", wait=tok_b_done if h == 0 else ())
    tok_wo = p.op("vector", lambda e: e.tensor_copy(out=c_wo[:], in_=c_fstage[:]),
                  wait=tok_l, inc=[("ve", 1)])

    tok_qsel = ()
    for qt in range(8):
        q0 = qt * 512
        wt = (tok_kv_all + tok_wo) if qt == 0 else tok_qsel
        tok_a = p.dma(c_st0[:].rearrange("p (dc t) -> p dc t", dc=8),
                      qd.rearrange("(dc p) t -> p dc t", p=128)
                      [:, :, q0:q0 + 512], wait=wt)
        tok_b2 = p.dma(c_st1[:].rearrange("p (dc t) -> p dc t", dc=8),
                       qd.rearrange("(dc p) t -> p dc t", p=128)
                       [:, :, HALF + q0:HALF + q0 + 512])
        for dc in range(8):
            wt2 = (tok_a + tok_b2 + tok_cfl) if dc == 0 else ()
            p.op("vector", lambda e, d=dc: e.tensor_scalar_mul(
                out=c_st0[:, d * 512:(d + 1) * 512],
                in0=c_st0[:, d * 512:(d + 1) * 512],
                scalar1=t_flags[:, 0:1]), wait=wt2)
            p.op("vector", lambda e, d=dc, q=q0: e.scalar_tensor_tensor(
                out=c_q[:, d * HALF + q: d * HALF + q + 512],
                in0=c_st1[:, d * 512:(d + 1) * 512],
                scalar=t_flags[:, 1:2],
                in1=c_st0[:, d * 512:(d + 1) * 512],
                op0=OP.mult, op1=OP.add))
        tok_qsel = p.mark("vector", [("ve", 1)])

    G["sc"] = G["sc"] + G["sc0"] + G["sc1"]
    tok_vones = p.op("vector", lambda e: e.memset(
        c_ve[:].rearrange("p (c f) -> p c f", f=65)[:, :, 64:65], 1.0),
        wait=tok_qsel)
    tok_vones = p.op("vector", lambda e: e.memset(
        c_vo[:].rearrange("p (c f) -> p c f", f=65)[:, :, 64:65], 1.0),
        inc=[("ve", 1)])
    tok_kvld = ()
    tok_fin = ()
    tok_pv_last = {}
    for hp in range(8):
        wt_ld = (tok_kv_all + tok_vones) if hp == 0 else tok_kvld2
        tok_k = p.dma(c_k[:], kd[hp * 128:(hp + 1) * 128, :], wait=wt_ld)
        tok_v1 = ()
        for c0 in range(0, 64, 8):
            tok_v1 = tok_v1 + p.dma(
                c_ve[:].rearrange("p (c f) -> p c f", f=65)[:, c0:c0 + 8, 0:64],
                vd.rearrange("(c p) f -> p c f", p=128)
                [:, c0:c0 + 8, (2 * hp) * 64:(2 * hp) * 64 + 64])
            tok_v1 = tok_v1 + p.dma(
                c_vo[:].rearrange("p (c f) -> p c f", f=65)[:, c0:c0 + 8, 0:64],
                vd.rearrange("(c p) f -> p c f", p=128)
                [:, c0:c0 + 8, (2 * hp + 1) * 64:(2 * hp + 1) * 64 + 64])
        tok_kvld = tok_k + tok_v1
        tok_kvld2 = ()
        for qt in range(8):
            q0 = qt * 512
            tok_exp_g = {}
            tok_pv_last = {}

            def emit_qk(g, wt0):
                for eo in range(2):
                    p.op("tensor", lambda e, g=g, eo=eo, hp=hp, q=q0: e.matmul(
                        out=sc4[:, ((g % 2) * 2 + eo) * 512:
                                ((g % 2) * 2 + eo) * 512 + 512],
                        lhsT=c_k[eo * 64:eo * 64 + 64, g * 128:(g + 1) * 128],
                        rhs=c_q[eo * 64:eo * 64 + 64,
                                hp * HALF + q: hp * HALF + q + 512],
                        start=True, stop=True), wait=wt0 if eo == 0 else ())
                return p.mark("tensor", [("mm", 1)])

            def emit_pv_e(g, extra):
                pe = c_p[:, (g % 2) * 1024:(g % 2) * 1024 + 512]
                p.op("tensor", lambda e, g=g, pe=pe: e.matmul(
                    out=pvE[0:65, :], lhsT=c_ve[:, g * 65:g * 65 + 65],
                    rhs=pe, start=(g == 0), stop=(g == 63)),
                    wait=tok_exp_g[g] + extra)

            def emit_pv_o(g):
                po = c_p[:, (g % 2) * 1024 + 512:(g % 2) * 1024 + 1024]
                p.op("tensor", lambda e, g=g, po=po: e.matmul(
                    out=pvO[0:65, :], lhsT=c_vo[:, g * 65:g * 65 + 65],
                    rhs=po, start=(g == 0), stop=(g == 63)))
                return p.mark("tensor", [("mm", 1)])

            for g in range(64):
                wt0 = tok_exp_g.get(g - 2, ())
                if g <= 1:
                    wt0 = wt0 + tok_kvld + G["sc"]
                tok_qkm = emit_qk(g, wt0)
                if g >= 1:
                    extra = ()
                    if g == 1:
                        extra = tok_fin + G["pvE"] + G["pvO"]
                    emit_pv_e(g - 1, extra)
                    tok_pv_last[g - 1] = emit_pv_o(g - 1)
                tok_exp_g[g] = p.op("scalar", lambda e, g=g: e.activation(
                    out=c_p[:, (g % 2) * 1024:(g % 2) * 1024 + 1024],
                    in_=sc4[:, (g % 2) * 1024:(g % 2) * 1024 + 1024],
                    func=AF.Exp, scale=f32(0.125), bias=f32(-C_B)),
                    wait=tok_qkm + tok_pv_last.get(g - 2, ()), inc=[("ac", 1)])
            emit_pv_e(63, ())
            tok_pv_last[63] = emit_pv_o(63)
            G["sc"] = tok_exp_g[63]
            tok_pvm = tok_pv_last[63]
            p.op("vector", lambda e: e.tensor_copy(
                out=t_rows[0:1, 0:512], in_=pvE[64:65, :]), wait=tok_pvm)
            tok_lr = p.op("vector", lambda e: e.tensor_copy(
                out=t_rows[0:1, 512:1024], in_=pvO[64:65, :]), inc=[("ve", 1)])
            tok_ob = p.op("vector", lambda e: e.tensor_copy(
                out=c_ob[0:64, :], in_=pvO[0:64, :]), inc=[("ve", 1)])
            p.op("tensor", lambda e: e.matmul(
                out=pAb[0:64, :], lhsT=t_onesr[0:1, 0:64],
                rhs=t_rows[0:1, 0:512], start=True, stop=True),
                wait=tok_lr + G["pA"])
            tok_bcm = p.op("tensor", lambda e: e.matmul(
                out=pAb[64:128, :], lhsT=t_onesr[0:1, 0:64],
                rhs=t_rows[0:1, 512:1024], start=True, stop=True),
                inc=[("mm", 1)])
            tok_sh = p.op("tensor", lambda e: e.matmul(
                out=pBb[:, :], lhsT=t_shift[0:64, :], rhs=c_ob[0:64, :],
                start=True, stop=True),
                wait=tok_ob + G["pB"], inc=[("mm", 1)])
            p.op("scalar", lambda e: e.activation(
                out=c_lnx[:], in_=pAb[:, :], func=AF.Ln), wait=tok_bcm)
            tok_rb = p.op("scalar", lambda e: e.activation(
                out=c_rb[:], in_=c_lnx[:], func=AF.Exp, scale=f32(-1.0)),
                inc=[("ac", 1)])
            G["pA"] = tok_rb
            p.op("vector", lambda e, hp=hp, q=q0: e.tensor_mul(
                out=c_o[0:64, hp * HALF + q: hp * HALF + q + 512],
                in0=pvE[0:64, :], in1=c_rb[0:64, :]), wait=tok_rb)
            tok_fin = p.op("vector", lambda e, hp=hp, q=q0: e.tensor_mul(
                out=c_o[64:128, hp * HALF + q: hp * HALF + q + 512],
                in0=pBb[64:128, :], in1=c_rb[64:128, :]),
                wait=tok_sh, inc=[("ve", 1)])
            G["pvE"] = tok_fin
            G["pvO"] = tok_lr + tok_ob
            G["pB"] = tok_fin
        tok_kvld2 = tok_pv_last[63] + tok_exp_g[63]

    if taps:
        p.dma(od.rearrange("(dc p) t -> p dc t", p=128),
              c_o[:].rearrange("p (dc t) -> p dc t", dc=8),
              sem="s3", wait=tok_fin)

    if "D" not in phases:
        _finish(nc, p)
        return nc

    tok_out_dma = {}
    tok_stage_free = tok_kvld2 + tok_fin
    wgu_free = {0: tok_kvld2, 1: tok_kvld2}
    wd_free = {0: tok_kvld2, 1: tok_kvld2}
    sg_free = {}
    for ti in range(nd):
        t0 = ti * NT
        tok_a = p.dma(d_xs0[:].rearrange("p (dc t) -> p dc t", dc=8),
                      x2d.rearrange("(dc p) t -> p dc t", p=128)
                      [:, :, t0:t0 + NT], wait=tok_x2_all + tok_stage_free)
        tok_b2 = p.dma(d_xs1a[:].rearrange("p (dc t) -> p dc t", dc=6),
                       x2d.rearrange("(dc p) t -> p dc t", p=128)
                       [:, 0:6, HALF + t0:HALF + t0 + NT])
        tok_b2 = tok_b2 + p.dma(
            d_xs1b[:].rearrange("p (dc t) -> p dc t", dc=2),
            x2d.rearrange("(dc p) t -> p dc t", p=128)
            [:, 6:8, HALF + t0:HALF + t0 + NT])
        for dc in range(8):
            wt = (tok_a + tok_b2) if dc == 0 else ()
            p.op("vector", lambda e, d=dc: e.tensor_scalar_mul(
                out=d_xs0[:, d * NT:(d + 1) * NT],
                in0=d_xs0[:, d * NT:(d + 1) * NT],
                scalar1=t_flags[:, 0:1]), wait=wt)
            p.op("vector", lambda e, d=dc: e.scalar_tensor_tensor(
                out=d_xs0[:, d * NT:(d + 1) * NT],
                in0=dxs1(d), scalar=t_flags[:, 1:2],
                in1=d_xs0[:, d * NT:(d + 1) * NT], op0=OP.mult, op1=OP.add))
        tok_x2sel = p.mark("vector", [("ve", 1)])

        for oc in range(8):
            bank, bname = (pAb, "pA") if oc % 2 == 0 else (pBb, "pB")
            for dc in range(8):
                wt = ()
                if dc == 0:
                    wt = G[bname] + tok_x2sel
                p.op("tensor", lambda e, d=dc, o_=oc, b=bank, t=t0: e.matmul(
                    out=b[:, 0:NT],
                    lhsT=c_wo[:, d * 1024 + o_ * 128: d * 1024 + o_ * 128 + 128],
                    rhs=c_o[:, d * HALF + t: d * HALF + t + NT],
                    start=(d == 0), stop=(d == 7)), wait=wt)
            tok_mm = p.mark("tensor", [("mm", 1)])
            tok_x3 = p.op("vector", lambda e, b=bank, o_=oc: e.tensor_add(
                out=d_x3[:, o_ * NT:(o_ + 1) * NT], in0=b[:, 0:NT],
                in1=d_xs0[:, o_ * NT:(o_ + 1) * NT]),
                wait=tok_mm, inc=[("ve", 1)])
            G[bname] = tok_x3
        tok_x3f = tok_x3

        def dxf(d):
            return d_x3[:, d * NT:(d + 1) * NT]

        tok_ln4, _ = layer_norm(dxf, lambda d: d_xn[:, d * NT:(d + 1) * NT],
                                d_xb, d_sq, d_abc, d_bbc, d_tmp, tok_x3f)
        for hc in range(24):
            buf = hc % 2
            tok_wl = p.dma(d_wg[:, buf * 1024:buf * 1024 + 512],
                           wgbf[hc][:, 0:512], sem=f"wvg{buf}",
                           wait=wgu_free.get(buf, ()))
            tok_wl = tok_wl + p.dma(d_wg[:, buf * 1024 + 512:(buf + 1) * 1024],
                                    wgbf[hc][:, 512:1024], sem=f"wvg{buf}")
            tok_wl = tok_wl + p.dma(d_wu[:, buf * 1024:buf * 1024 + 512],
                                    wubf[hc][:, 0:512], sem=f"wvg{buf}")
            tok_wl = tok_wl + p.dma(d_wu[:, buf * 1024 + 512:(buf + 1) * 1024],
                                    wubf[hc][:, 512:1024], sem=f"wvg{buf}")
            gb, gbn = (pAb[:, 0:NT], "pA") if buf == 0 else (sc4[:, 0:NT], "sc0")
            ub, ubn = (pBb[:, 0:NT], "pB") if buf == 0 else (pvO[:, 0:NT], "pvO")
            for dc in range(8):
                wt = (tok_ln4 + tok_wl + G[gbn]) if dc == 0 else ()
                p.op("tensor", lambda e, d=dc, b=buf, gb=gb: e.matmul(
                    out=gb,
                    lhsT=d_wg[:, b * 1024 + d * 128: b * 1024 + d * 128 + 128],
                    rhs=d_xn[:, d * NT:(d + 1) * NT],
                    start=(d == 0), stop=(d == 7)), wait=wt)
            for dc in range(8):
                wt = G[ubn] if dc == 0 else ()
                p.op("tensor", lambda e, d=dc, b=buf, ub=ub: e.matmul(
                    out=ub,
                    lhsT=d_wu[:, b * 1024 + d * 128: b * 1024 + d * 128 + 128],
                    rhs=d_xn[:, d * NT:(d + 1) * NT],
                    start=(d == 0), stop=(d == 7)), wait=wt)
            tok_gu = p.mark("tensor", [("mm", 1)])
            wgu_free[buf] = tok_gu
            sgs = d_sg[:, (hc % 2) * NT:(hc % 2) * NT + NT]
            tok_sg = p.op("scalar", lambda e, gb=gb, s=sgs: e.activation(
                out=s, in_=gb, func=AF.Silu, scale=f32(DSC)),
                wait=tok_gu + sg_free.get(hc % 2, ()), inc=[("ac", 1)])
            tok_h = p.op("vector", lambda e, h=hc, ub=ub, s=sgs: e.tensor_mul(
                out=d_h[:, h * NT:(h + 1) * NT], in0=ub, in1=s),
                wait=tok_sg, inc=[("ve", 1)])
            sg_free[hc % 2] = tok_h
            G[gbn] = tok_sg
            G[ubn] = tok_h
        tok_hdone = tok_h
        for oc in range(8):
            buf = oc % 2
            tok_wd = p.dma(d_wd[:, buf * 3072:buf * 3072 + 1536],
                           wdbf[oc][:, 0:1536], sem=f"wvd{buf}",
                           wait=wd_free.get(buf, ()))
            tok_wd = tok_wd + p.dma(d_wd[:, buf * 3072 + 1536:(buf + 1) * 3072],
                                    wdbf[oc][:, 1536:3072], sem=f"wvd{buf}")
            bank, bname = (pAb, "pA") if oc % 2 == 0 else (pBb, "pB")
            for hc in range(24):
                wt = (tok_hdone + tok_wd + G[bname]) if hc == 0 else ()
                p.op("tensor", lambda e, h=hc, b=bank, bu=buf: e.matmul(
                    out=b[:, 0:NT],
                    lhsT=d_wd[:, bu * 3072 + h * 128: bu * 3072 + h * 128 + 128],
                    rhs=d_h[:, h * NT:(h + 1) * NT],
                    start=(h == 0), stop=(h == 23)), wait=wt)
            tok_mm = p.mark("tensor", [("mm", 1)])
            tok_o = p.op("vector", lambda e, b=bank, o_=oc: e.scalar_tensor_tensor(
                out=d_out[:, (o_ % 2) * NT:(o_ % 2) * NT + NT],
                in0=b[:, 0:NT], scalar=DSC * DSC,
                in1=d_x3[:, o_ * NT:(o_ + 1) * NT], op0=OP.mult, op1=OP.add),
                wait=tok_mm + tok_out_dma.get(oc % 2, ()), inc=[("ve", 1)])
            G[bname] = tok_o
            wd_free[buf] = tok_o
            tok_out_dma[oc % 2] = p.dma(outT[oc * 128:(oc + 1) * 128, t0:t0 + NT],
                                        d_out[:, (oc % 2) * NT:(oc % 2) * NT + NT],
                                        sem=f"ow{oc % 2}", wait=tok_o,
                                        eng="gpsimd")
        tok_stage_free = tok_x3f

    _finish(nc, p)
    return nc


def _finish(nc, p):
    p.op("sync", lambda e: e.nop(),
         wait=[(s, v) for s, v in p.cnt.items() if v > 0])
    import contextlib
    stack = contextlib.ExitStack()
    sems = {s: stack.enter_context(nc.semaphore(s)) for s in p.cnt}
    with nc.Block() as block:
        _replay(block, p, sems)
    stack.close()


def eo_off(ch):
    return 256 + ch * 128


def _host_constants():
    pos = np.arange(S, dtype=np.float64)
    inv = 1.0 / (10000.0 ** (np.arange(0, HD, 2, dtype=np.float64) / HD))
    ang = pos[:, None] * inv[None, :]
    cos64 = np.tile(np.cos(ang), (1, 2)).astype(np.float32)
    sin64 = np.tile(np.sin(ang), (1, 2)).astype(np.float32)
    sgn = np.ones(64, np.float32)
    sgn[:32] = -1.0
    sinp = sin64 * sgn[None, :]
    cosT_ = np.ascontiguousarray(np.concatenate([cos64.T, cos64.T], axis=0))
    sinT_ = np.ascontiguousarray(np.concatenate([sinp.T, sinp.T], axis=0))
    r = np.zeros((128, 128), np.float32)
    for h in range(2):
        for i in range(64):
            r[h * 64 + (i + 32) % 64, h * 64 + i] = 1.0
    r128_ = np.ascontiguousarray(r.astype(bf16))
    sh = np.zeros((128, 128), np.float32)
    for k in range(64):
        sh[k, 64 + k] = 1.0
    shmat_ = np.ascontiguousarray(sh.astype(bf16))
    k_i = np.arange(128)[:, None]
    q_i = np.arange(128)[None, :]
    mprev = (k_i >= q_i).astype(np.float32)
    mcur = (k_i <= q_i).astype(np.float32)
    blockm = np.concatenate([mprev, mcur], axis=1)
    maskA_ = np.ascontiguousarray(np.tile(blockm, (1, 8)).astype(bf16))
    block0 = np.concatenate([0 * mprev, mcur], axis=1)
    half = np.concatenate([block0] + [blockm] * 3, axis=1)
    maskA0_ = np.ascontiguousarray(np.tile(half, (1, 2)).astype(bf16))
    return cosT_, sinT_, r128_, maskA_, maskA0_, shmat_


def make_in_maps(L0_attn_a_hidden_states, L0_attn_a_Wq, L0_attn_a_bq,
                 L0_attn_a_Wk, L0_attn_a_bk, L0_attn_a_Wv, L0_attn_a_bv,
                 L0_attn_a_Wo, L0_attn_a_bo, L0_attn_b_ln_gamma,
                 L0_attn_b_ln_beta, L0_attn_b_Wq, L0_attn_b_Wk, L0_attn_b_Wv,
                 L0_attn_b_Wo, L0_ffn_w_gate, L0_ffn_w_up, L0_ffn_w_down):
    x = np.asarray(L0_attn_a_hidden_states, np.float32)
    cosT_, sinT_, r128_, maskA_, maskA0_, shmat_ = _host_constants()

    cols = np.zeros((128, 48), np.float32)
    cols[:, 0:8] = np.asarray(L0_attn_a_bq, np.float32).reshape(8, 128).T
    cols[:, 8:16] = np.asarray(L0_attn_a_bk, np.float32).reshape(8, 128).T
    cols[:, 16:24] = np.asarray(L0_attn_a_bo, np.float32).reshape(8, 128).T
    cols[:, 24:32] = np.asarray(L0_attn_b_ln_gamma, np.float32).reshape(8, 128).T
    cols[:, 32:40] = np.asarray(L0_attn_b_ln_beta, np.float32).reshape(8, 128).T
    bvrow_ = np.asarray(L0_attn_a_bv, np.float32).reshape(1, DM)

    shared = {
        "waqT": np.ascontiguousarray(np.asarray(L0_attn_a_Wq, np.float32).T),
        "wakT": np.ascontiguousarray(np.asarray(L0_attn_a_Wk, np.float32).T),
        "wavT": np.ascontiguousarray(np.asarray(L0_attn_a_Wv, np.float32).T),
        "waoT": np.ascontiguousarray(np.asarray(L0_attn_a_Wo, np.float32).T),
        "acols": cols, "bvrow": bvrow_, "r128": r128_,
        "cosT": cosT_, "sinT": sinT_, "maskA": maskA_, "maskA0": maskA0_,
        "shmat": shmat_,
        "wbq": np.ascontiguousarray(np.asarray(L0_attn_b_Wq, np.float32)),
        "wbk": np.ascontiguousarray(np.asarray(L0_attn_b_Wk, np.float32)),
        "wbv": np.ascontiguousarray(np.asarray(L0_attn_b_Wv, np.float32)),
        "wbo": np.ascontiguousarray(np.asarray(L0_attn_b_Wo, np.float32)),
        "wgT": np.ascontiguousarray(np.asarray(L0_ffn_w_gate, np.int32).T),
        "wuT": np.ascontiguousarray(np.asarray(L0_ffn_w_up, np.int32).T),
        "wdT": np.ascontiguousarray(np.asarray(L0_ffn_w_down, np.int32).T),
    }

    in_maps = []
    for c in range(8):
        b, h = c // 2, c % 2
        fl = np.zeros((128, 4), np.float32)
        fl[:, 0] = 1.0 if h == 0 else 0.0
        fl[:, 1] = 1.0 if h == 1 else 0.0
        m = dict(shared)
        m["x0T"] = np.ascontiguousarray(x[b].T)
        m["flags"] = fl
        m["flagsb"] = np.ascontiguousarray(fl.astype(bf16))
        in_maps.append(m)

    return in_maps


def kernel(**inputs):
    in_maps = make_in_maps(**inputs)
    nc = _build(taps=False)
    res = run_bass_kernel_spmd(nc, in_maps, core_ids=list(range(8)))
    out = np.empty((BATCH, S, DM), np.float32)
    for c in range(8):
        b, h = c // 2, c % 2
        out[b, h * HALF:(h + 1) * HALF, :] = res.results[c]["outT"].T
    return out


# version 32
# speedup vs baseline: 1.1569x; 1.0169x over previous
import numpy as np
import ml_dtypes
import concourse.bass as bass
import concourse.mybir as mybir
from concourse.bass_utils import run_bass_kernel_spmd

F32 = mybir.dt.float32
BF16 = mybir.dt.bfloat16
I32 = mybir.dt.int32
AF = mybir.ActivationFunctionType
OP = mybir.AluOpType
f32 = np.float32
bf16 = ml_dtypes.bfloat16

BATCH, S, DM, NH, HD = 4, 8192, 1024, 16, 64
FH = 3072
HALF = S // 2
NT = 512
EPS = 1e-5
C_A = 8.0
C_B = 16.0
DSC = 1.0 / 127.0
VSLOT = 130

ENG = ("sync", "tensor", "vector", "scalar", "gpsimd")
KB = 1024


class Plan:
    def __init__(self):
        self.ops = {e: [] for e in ENG}
        self.cnt = {}

    def sem(self, name):
        self.cnt[name] = 0

    def op(self, eng, fn, wait=(), inc=()):
        by = {}
        for s, v in wait:
            by[s] = max(v, by.get(s, 0))
        wait = tuple(sorted(by.items()))
        for s, v in wait:
            assert v <= self.cnt[s], f"wait on future: {s} {v} > {self.cnt[s]}"
        toks = []
        for s, n in inc:
            self.cnt[s] += n
            toks.append((s, self.cnt[s]))
        self.ops[eng].append((fn, wait, tuple(inc)))
        return tuple(toks)

    def mark(self, eng, inc):
        fn, waits, incs = self.ops[eng][-1]
        toks = []
        for s, n in inc:
            self.cnt[s] += n
            toks.append((s, self.cnt[s]))
        self.ops[eng][-1] = (fn, waits, incs + tuple(inc))
        return tuple(toks)

    def dma(self, dst, src, sem="ld", wait=(), eng="sync"):
        return self.op(eng, lambda e, d=dst, s=src: e.dma_start(out=d, in_=s),
                       wait=wait, inc=[(sem, 16)])


def _replay(block, plan, sems):
    def mk(ops):
        def body(eng):
            for fn, waits, incs in ops:
                for s, v in waits:
                    eng.wait_ge(sems[s], v)
        return body

    for e in ENG:
        ops = plan.ops[e]

        def body(eng, ops=ops):
            for fn, waits, incs in ops:
                for s, v in waits:
                    eng.wait_ge(sems[s], v)
                inst = fn(eng)
                if incs:
                    if inst is None:
                        inst = eng.nop()
                    for s, n in incs:
                        inst = inst.then_inc(sems[s], n)
        getattr(block, e)(body)


def _build(taps=False, phases="ABCD", na=16, nb=16, nd=8, asub=7):
    nc = bass.Bass("TRN2", target_bir_lowering=False, debug=False,
                   detect_race_conditions=False)

    def ein(name, shape, dtype):
        return nc.dram_tensor(name, list(shape), dtype, kind="ExternalInput").ap()

    x0T = ein("x0T", (DM, S), F32)
    waqT = ein("waqT", (DM, DM), F32)
    wakT = ein("wakT", (DM, DM), F32)
    wavT = ein("wavT", (DM, DM), F32)
    waoT = ein("waoT", (DM, DM), F32)
    acols = ein("acols", (128, 48), F32)
    bvrow = ein("bvrow", (1, DM), F32)
    r128 = ein("r128", (128, 128), BF16)
    cosT = ein("cosT", (128, S), F32)
    sinT = ein("sinT", (128, S), F32)
    maskA = ein("maskA", (128, 2048), BF16)
    maskA0 = ein("maskA0", (128, 2048), BF16)
    wbq = ein("wbq", (DM, DM), F32)
    wbk = ein("wbk", (DM, DM), F32)
    wbv = ein("wbv", (DM, DM), F32)
    wbo = ein("wbo", (DM, DM), F32)
    wgT = ein("wgT", (DM, FH), I32)
    wuT = ein("wuT", (DM, FH), I32)
    wdT = ein("wdT", (FH, DM), I32)
    flags = ein("flags", (128, 4), F32)
    flagsb = ein("flagsb", (128, 4), BF16)
    shmat = ein("shmat", (128, 128), BF16)

    outT = nc.dram_tensor("outT", [DM, HALF], F32, kind="ExternalOutput").ap()
    ikind = "ExternalOutput" if taps else "Internal"
    x1d = nc.dram_tensor("x1d", [DM, S], F32, kind=ikind).ap()
    x2d = nc.dram_tensor("x2d", [DM, S], F32, kind=ikind).ap()
    kd = nc.dram_tensor("kd", [DM, S], BF16, kind=ikind).ap()
    vd = nc.dram_tensor("vd", [S, DM], BF16, kind=ikind).ap()
    qd = nc.dram_tensor("qd", [DM, S], BF16, kind=ikind).ap()
    od = nc.dram_tensor("od", [DM, HALF], BF16,
                        kind="ExternalOutput" if taps else "Internal").ap()
    wgbf = nc.dram_tensor("wgbf", [24, 128, 1024], BF16, kind="Internal").ap()
    wubf = nc.dram_tensor("wubf", [24, 128, 1024], BF16, kind="Internal").ap()
    wdbf = nc.dram_tensor("wdbf", [8, 128, 3072], BF16, kind="Internal").ap()

    LIMIT = 212832
    SB_BASE = 16512

    def sb(name, shape, dtype, off):
        nb = int(np.prod(shape[1:])) * mybir.dt.size(dtype)
        assert off >= 0 and off + nb <= LIMIT, (name, off, nb)
        assert SB_BASE + off + nb <= nc.sbuf_top, (name, off, nb)
        return nc.alloc_sbuf_tensor_at(name, list(shape), dtype,
                                       offset=SB_BASE + off)

    t_cols = sb("cols", [128, 48], F32, 0)
    t_onesb = sb("onesb", [128, 16], BF16, 192)
    t_onesr = sb("onesr", [1, 128], F32, 224)
    t_r128 = sb("r128s", [128, 128], BF16, 736)
    t_flags = sb("flagss", [128, 4], F32, 992)
    t_flagsb = sb("flagsbs", [128, 4], BF16, 1024)
    t_rows = sb("rowsP", [1, 2048], F32, 1056)
    t_maskA = sb("maskAs", [128, 2048], BF16, 9248)
    t_maskA0 = sb("maskA0s", [128, 2048], BF16, 13344)
    t_bvbc = sb("bvbc", [128, DM], F32, 17440)
    t_shift = sb("shifts", [128, 128], BF16, 21536)
    PH = 22 * KB

    o = PH
    a_w = sb("a_w", [128, 4 * 8192], BF16, o); o += 64 * KB
    a_x0 = sb("a_x0", [128, 8 * NT], F32, o); o += 16 * KB
    a_xb = sb("a_xb", [128, 2 * NT], BF16, o); o += 2 * KB
    a_sq = sb("a_sq", [128, 2 * NT], BF16, o); o += 2 * KB
    a_xn = sb("a_xn", [128, 8 * NT], BF16, o); o += 8 * KB
    a_abc = sb("a_abc", [128, NT], F32, o); o += 2 * KB
    a_bbc = sb("a_bbc", [128, NT], F32, o); o += 2 * KB
    a_tmp = sb("a_tmp", [128, NT], F32, o); o += 2 * KB
    a_qf = sb("a_qf", [128, NT], F32, o); o += 2 * KB
    a_qb = sb("a_qb", [128, NT], BF16, o); o += 1 * KB
    a_qpf = sb("a_qpf", [128, NT], F32, o); o += 2 * KB
    a_t1 = sb("a_t1", [128, NT], F32, o); o += 2 * KB
    a_t2 = sb("a_t2", [128, NT], F32, o); o += 2 * KB
    a_qr = sb("a_qr", [128, 8 * NT], BF16, o); o += 8 * KB
    a_cos = sb("a_cos", [128, NT], F32, o); o += 2 * KB
    a_sin = sb("a_sin", [128, NT], F32, o); o += 2 * KB
    a_kring = sb("a_kring", [128, 8 * 8 * 128], BF16, o); o += 16 * KB
    a_vring = sb("a_vring", [128, 8 * 8 * VSLOT], BF16, o); o += 8 * 8 * VSLOT * 2
    a_p = sb("a_p", [128, 2048], BF16, o); o += 4 * KB
    a_pm = sb("a_pm", [128, 2 * 2048], BF16, o); o += 8 * KB
    a_oa = sb("a_oa", [128, 8 * NT], BF16, o); o += 8 * KB
    a_rb = sb("a_rb", [128, NT], F32, o); o += 2 * KB
    a_x1 = sb("a_x1", [128, 2 * NT], F32, o); o += 4 * KB
    a_lnx = sb("a_lnx", [128, NT], F32, o); o += 2 * KB
    a_xr0 = sb("a_xr0", [128, NT], F32, o); o += 2 * KB
    assert o <= LIMIT, o

    i_wstage = sb("i_wstage", [128, 12288], I32, LIMIT - 48 * KB)
    i_wcvt = sb("i_wcvt", [128, 12288], BF16, LIMIT - 72 * KB)
    i_astage = sb("i_astage", [128, 8192], F32, LIMIT - 104 * KB)

    o = PH
    b_w = sb("b_w", [128, 3 * 8192], BF16, o); o += 48 * KB
    b_x1 = sb("b_x1", [128, 2 * 8 * NT], F32, o); o += 32 * KB
    b_xb = sb("b_xb", [128, 2 * NT], BF16, o); o += 2 * KB
    b_sq = sb("b_sq", [128, 2 * NT], BF16, o); o += 2 * KB
    b_xn = sb("b_xn", [128, 2 * 8 * NT], BF16, o); o += 16 * KB
    b_abc = sb("b_abc", [128, NT], F32, o); o += 2 * KB
    b_bbc = sb("b_bbc", [128, NT], F32, o); o += 2 * KB
    b_tmp = sb("b_tmp", [128, NT], F32, o); o += 2 * KB
    b_wg = sb("b_wg", [128, 2 * 1024], BF16, o); o += 4 * KB
    b_wu = sb("b_wu", [128, 2 * 1024], BF16, o); o += 4 * KB
    b_sg = sb("b_sg", [128, 2 * NT], BF16, o); o += 2 * KB
    b_h = sb("b_h", [128, 24 * NT], BF16, o); o += 24 * KB
    b_wd = sb("b_wd", [128, 2 * 3072], BF16, o); o += 12 * KB
    b_x2 = sb("b_x2", [128, 8 * NT], F32, o); o += 16 * KB
    b_xn3 = sb("b_xn3", [128, 8 * NT], BF16, o); o += 8 * KB
    b_xn3b = sb("b_xn3b", [128, 8 * NT], BF16, o); o += 8 * KB
    assert o <= LIMIT, o
    b_rows2 = sb("b_rows2", [1, 2048], F32, 9248)
    b_kv = sb("b_kv", [128, 4 * NT], BF16, 17440)
    b_fstage = sb("b_fstage", [128, 8192], F32,
                  PH + 48 * KB + 16 * KB + 2 * KB + 2 * KB + 8 * KB
                  + 2 * KB + 2 * KB + 2 * KB + 4 * KB + 4 * KB + 1 * KB)

    o = PH
    c_q = sb("c_q", [128, 8 * HALF], BF16, o); o += 64 * KB
    C_O_OFF = o
    c_o = sb("c_o", [128, 8 * HALF], BF16, o); o += 64 * KB
    c_wo = sb("c_wo", [128, 8192], BF16, o); o += 16 * KB
    C_K_OFF = o
    c_st0 = sb("c_st0", [128, 8 * NT], BF16, C_K_OFF)
    c_st1 = sb("c_st1", [128, 8 * NT], BF16, C_K_OFF + 8 * KB)
    c_k = sb("c_k", [128, S], BF16, C_K_OFF); o += 16 * KB
    c_ve = sb("c_ve", [128, 64 * 65], BF16, o); o += 64 * 65 * 2
    c_vo = sb("c_vo", [128, 64 * 65], BF16, o); o += 64 * 65 * 2
    c_p = sb("c_p", [128, 2 * 2048], BF16, o); o += 8 * KB
    assert o <= LIMIT, o
    c_rb = sb("c_rb", [128, NT], F32, 17440)
    c_lnx = sb("c_lnx", [128, NT], F32, 13344)
    c_ob = sb("c_ob", [128, NT], BF16, 15392)
    c_fstage = sb("c_fstage", [128, 8192], F32, C_O_OFF)

    o = PH
    d_x3 = sb("d_x3", [128, 8 * NT], F32, o); o += 16 * KB
    d_xs0 = sb("d_xs0", [128, 8 * NT], F32, o); o += 16 * KB
    d_xn = sb("d_xn", [128, 8 * NT], BF16, o); o += 8 * KB
    d_xb = sb("d_xb", [128, 2 * NT], BF16, o); o += 2 * KB
    d_sq = sb("d_sq", [128, 2 * NT], BF16, o); o += 2 * KB
    d_abc = sb("d_abc", [128, NT], F32, o); o += 2 * KB
    d_bbc = sb("d_bbc", [128, NT], F32, o); o += 2 * KB
    d_tmp = sb("d_tmp", [128, NT], F32, o); o += 2 * KB
    d_sg = sb("d_sg", [128, 2 * NT], BF16, o); o += 2 * KB
    d_out = sb("d_out", [128, 2 * NT], F32, o); o += 4 * KB
    d_wg = sb("d_wg", [128, 2 * 1024], BF16, o); o += 4 * KB
    d_wu = sb("d_wu", [128, 2 * 1024], BF16, o); o += 4 * KB
    assert o <= PH + 64 * KB, o
    o2 = C_K_OFF
    d_h = sb("d_h", [128, 24 * NT], BF16, o2); o2 += 24 * KB
    d_wd = sb("d_wd", [128, 2 * 3072], BF16, o2); o2 += 12 * KB
    assert o2 <= LIMIT, o2
    d_xs1a = sb("d_xs1a", [128, 6 * NT], F32, 9248)
    d_xs1b = sb("d_xs1b", [128, 2 * NT], F32, C_K_OFF + 36 * KB)

    def dxs1(dd):
        if dd < 6:
            return d_xs1a[:, dd * NT:(dd + 1) * NT]
        return d_xs1b[:, (dd - 6) * NT:(dd - 5) * NT]

    sc4 = nc.alloc_psum_tensor("sc4", [128, 2048], F32)
    pvE = nc.alloc_psum_tensor("pvE", [128, 512], F32)
    pvO = nc.alloc_psum_tensor("pvO", [128, 512], F32)
    pAb = nc.alloc_psum_tensor("pAb", [128, 512], F32)
    pBb = nc.alloc_psum_tensor("pBb", [128, 512], F32)

    p = Plan()
    for s in ("ld", "wv", "mm", "ve", "ac", "gp", "s2", "s3", "fin"):
        p.sem(s)
    for s in ("x1w0", "x1w1"):
        p.sem(s)
    for i in range(8):
        p.sem(f"x2w{i}")
    for i in range(4):
        p.sem(f"kvw{i}")
    p.sem("ow0"); p.sem("ow1")
    p.sem("xr0"); p.sem("xr1")
    for s in ("wvg0", "wvg1", "wvd0", "wvd1"):
        p.sem(s)
    for s in ("wva0", "wva1", "wvf0", "wvf1", "s30", "s31"):
        p.sem(s)

    G = {"sc": (), "sc0": (), "sc1": (), "pvE": (), "pvO": (),
         "pA": (), "pB": (), "lnb": (), "rows": (), "rows2": (), "abc": ()}

    p.dma(t_cols[:], acols)
    p.dma(t_r128[:], r128)
    p.dma(t_maskA[:], maskA)
    p.dma(t_maskA0[:], maskA0)
    p.dma(t_flags[:], flags)
    tok_cfl = p.dma(t_flagsb[:], flagsb)
    p.dma(t_shift[:], shmat)
    p.op("vector", lambda e: e.memset(t_onesb[:], 1.0))
    p.op("vector", lambda e: e.memset(t_onesr[:], 1.0))
    tok_ones = p.mark("vector", [("ve", 1)])

    tok_bvr = p.dma(i_astage[0:1, 0:DM], bvrow)
    for j in range(2):
        p.op("tensor", lambda e, j=j: e.matmul(
            out=sc4[:, j * 512:(j + 1) * 512], lhsT=t_onesr[0:1, :],
            rhs=i_astage[0:1, j * 512:(j + 1) * 512], start=True, stop=True),
            wait=tok_bvr + tok_ones if j == 0 else ())
    tok_bvm = p.mark("tensor", [("mm", 1)])
    tok_bvc = p.op("vector", lambda e: e.tensor_copy(
        out=t_bvbc[:], in_=sc4[:, 0:1024]), wait=tok_bvm, inc=[("ve", 1)])
    G["sc"] = tok_bvc

    cvt_done = {}
    for u in range(8):
        wi, h = u // 2, u % 2
        wsrc = (waqT, wakT, wavT, waoT)[wi]
        sl = (u % 2) * 4096
        tok_l = p.dma(i_astage[:, sl:sl + 4096]
                      .rearrange("p (dc f) -> p dc f", dc=4),
                      wsrc.rearrange("(dc p) f -> p dc f", p=128)
                      [:, h * 4:h * 4 + 4, :],
                      sem=f"wva{u % 2}", wait=(tok_bvc if u == 0 else ())
                      + cvt_done.get(u - 2, ()),
                      eng=("gpsimd", "scalar")[u % 2])
        cvt_done[u] = p.op("vector", lambda e, wi=wi, h=h, sl=sl: e.tensor_copy(
            out=a_w[:, wi * 8192 + h * 4096:wi * 8192 + h * 4096 + 4096],
            in_=i_astage[:, sl:sl + 4096]),
            wait=tok_l, inc=[("ve", 1)])
    tok_aw = cvt_done[6] + cvt_done[7]

    units = []
    for wsrc, wdst, n_in, n_out in ((wgT, wgbf, 8, 24), (wuT, wubf, 8, 24),
                                    (wdT, wdbf, 24, 8)):
        for ic in range(n_in):
            units.append((wsrc, wdst, ic, n_out))
    f_cvt = {}
    f_store = {}
    for u, (wsrc, wdst, ic, n_out) in enumerate(units):
        sl = (u % 2) * 6144
        nh = n_out * 128 // 2
        tok_l = ()
        for h in range(2):
            tok_l = tok_l + p.dma(
                i_wstage[:, sl + h * nh:sl + (h + 1) * nh],
                wsrc[ic * 128:(ic + 1) * 128, h * nh:(h + 1) * nh],
                sem=f"wvf{u % 2}", wait=f_cvt.get(u - 2, ()),
                eng=("gpsimd", "scalar")[h])
        tok_c = p.op("vector", lambda e, sl=sl, n=n_out: e.tensor_copy(
            out=i_wcvt[:, sl:sl + n * 128], in_=i_wstage[:, sl:sl + n * 128]),
            wait=tok_l + f_store.get(u - 2, ()), inc=[("ve", 1)])
        f_cvt[u] = tok_c
        tok_s = ()
        stp = 8
        for o0 in range(0, n_out, stp):
            tok_s = tok_s + p.dma(
                wdst[o0:o0 + stp, :, ic * 128:(ic + 1) * 128]
                .rearrange("oc p f -> p oc f"),
                i_wcvt[:, sl + o0 * 128:sl + (o0 + stp) * 128]
                .rearrange("p (oc f) -> p oc f", f=128),
                sem=f"s3{u % 2}", wait=tok_c if o0 == 0 else ())
        f_store[u] = tok_s
    tok_wcache = f_store[len(units) - 1] + f_store[len(units) - 2]

    p.op("vector", lambda e: e.memset(a_kring[:], 0.0),
         wait=tok_wcache + tok_aw)
    p.op("vector", lambda e: e.memset(a_vring[:], 0.0))
    p.op("vector", lambda e: e.memset(
        a_vring[:].rearrange("p (s f) -> p s f", f=VSLOT)[:, :, 64:65], 1.0))
    p.op("vector", lambda e: e.memset(
        a_vring[:].rearrange("p (s f) -> p s f", f=VSLOT)[:, :, 129:130], 1.0))
    tok_ring = p.mark("vector", [("ve", 1)])

    def ln_stats(xF, xbT, sqT, wait_first, bf_in=False):
        s1 = sc4[:, 512:1024]
        s2 = sc4[:, 1024:1536]
        mm_toks = {}
        for dc in range(8):
            sl = (dc % 2) * NT
            w = wait_first if dc == 0 else ()
            w = w + mm_toks.get(dc - 2, ())
            if bf_in:
                tsq = p.op("gpsimd", lambda e, d=dc, s=sl: e.tensor_mul(
                    out=sqT[:, s:s + NT], in0=xF(d), in1=xF(d)),
                    wait=w, inc=[("gp", 1)])
                srcap = xF(dc)
                tdat = tsq
            else:
                tdat = p.op("vector", lambda e, d=dc, s=sl: e.tensor_copy(
                    out=xbT[:, s:s + NT], in_=xF(d)), wait=w, inc=[("ve", 1)])
                tsq = p.op("gpsimd", lambda e, d=dc, s=sl: e.tensor_mul(
                    out=sqT[:, s:s + NT], in0=xbT[:, s:s + NT],
                    in1=xbT[:, s:s + NT]), wait=tdat, inc=[("gp", 1)])
                srcap = xbT[:, sl:sl + NT]
            wmm = (wait_first + G["lnb"] + G["sc"]) if dc == 0 else ()
            p.op("tensor", lambda e, d=dc, s=srcap: e.matmul(
                out=s1[0:1, 0:NT], lhsT=t_onesb[:, 0:1], rhs=s,
                start=(d == 0), stop=(d == 7)), wait=wmm + tdat)
            tok2 = p.op("tensor", lambda e, d=dc, s=(dc % 2) * NT: e.matmul(
                out=s2[0:1, 0:NT], lhsT=t_onesb[:, 0:1], rhs=sqT[:, s:s + NT],
                start=(d == 0), stop=(d == 7)), wait=tsq, inc=[("mm", 1)])
            mm_toks[dc] = tok2
        return mm_toks[7]

    def ln_rows(tok_stats, rows=None, rkey="rows"):
        rows = t_rows if rows is None else rows
        s1 = sc4[:, 512:1024]
        s2 = sc4[:, 1024:1536]
        rmean = rows[0:1, 0:NT]
        rvar = rows[0:1, NT:2 * NT]
        rsd = rows[0:1, 2 * NT:3 * NT]
        rb_ = rows[0:1, 3 * NT:4 * NT]
        p.op("vector", lambda e: e.tensor_scalar_mul(
            out=rmean, in0=s1[0:1, 0:NT], scalar1=1.0 / DM),
            wait=tok_stats + G[rkey])
        p.op("vector", lambda e: e.tensor_mul(out=rvar, in0=rmean, in1=rmean))
        p.op("vector", lambda e: e.scalar_tensor_tensor(
            out=rvar, in0=s2[0:1, 0:NT], scalar=1.0 / DM, in1=rvar,
            op0=OP.mult, op1=OP.subtract))
        tok_v = p.mark("vector", [("ve", 1)])
        G["lnb"] = tok_v
        p.op("scalar", lambda e: e.activation(
            out=rsd, in_=rvar, func=AF.Ln, bias=f32(EPS)), wait=tok_v)
        tok_sd = p.op("scalar", lambda e: e.activation(
            out=rsd, in_=rsd, func=AF.Exp, scale=f32(-0.5)), inc=[("ac", 1)])
        p.op("vector", lambda e: e.scalar_tensor_tensor(
            out=rb_, in0=rmean, scalar=-1.0, in1=rsd, op0=OP.mult, op1=OP.mult),
            wait=tok_sd)
        return p.mark("vector", [("ve", 1)])

    def ln_norm(xF, xnO, abcT, bbcT, tmpT, tok_rows, norm_wait=(),
                rows=None, rkey="rows", neng="gpsimd", tmp_bank3=False,
                abanks=False):
        rows = t_rows if rows is None else rows
        rsd = rows[0:1, 2 * NT:3 * NT]
        rb_ = rows[0:1, 3 * NT:4 * NT]
        if abanks:
            bA = sc4[:, 0:512]
            bB = sc4[:, 512:1024]
            wA = tok_rows + G["sc"] + G["sc0"] + G["sc1"]
        else:
            bA = sc4[:, 1536:2048]
            bB = pvE[:, 0:NT]
            wA = tok_rows + G["pvE"]
        p.op("tensor", lambda e: e.matmul(
            out=bA, lhsT=t_onesr[0:1, :], rhs=rsd,
            start=True, stop=True), wait=wA)
        tok_bc = p.op("tensor", lambda e: e.matmul(
            out=bB, lhsT=t_onesr[0:1, :], rhs=rb_,
            start=True, stop=True), inc=[("mm", 1)])
        G[rkey] = tok_bc
        p.op("vector", lambda e, bA=bA: e.tensor_copy(out=abcT[:, 0:NT],
                                                      in_=bA),
             wait=tok_bc + G["abc"])
        tok_cp = p.op("vector", lambda e, bB=bB: e.tensor_copy(
            out=bbcT[:, 0:NT], in_=bB), inc=[("ve", 1)])
        if tmp_bank3:
            tmpT = sc4[:, 1536:2048]
        nsem = {"gpsimd": "gp", "vector": "ve"}[neng]
        for dc in range(8):
            p.op(neng, lambda e, d=dc: e.tensor_mul(
                out=tmpT[:, 0:NT], in0=xF(d), in1=abcT[:, 0:NT]),
                wait=(tok_cp + norm_wait) if dc == 0 else ())
            p.op(neng, lambda e, d=dc: e.tensor_add(
                out=xnO(d), in0=tmpT[:, 0:NT], in1=bbcT[:, 0:NT]))
        tok_n = p.mark(neng, [(nsem, 1)])
        G["abc"] = tok_n
        if abanks:
            G["sc0"] = tok_cp
            G["sc1"] = tok_cp
            G["lnb"] = G["lnb"] + tok_cp
        else:
            G["pvE"] = tok_n if tmp_bank3 else tok_cp
        return tok_n

    def layer_norm(xF, xnO, xbT, sqT, abcT, bbcT, tmpT, wait_first,
                   bf_in=False, norm_wait=()):
        tok_stats = ln_stats(xF, xbT, sqT, wait_first, bf_in=bf_in)
        tok_rows = ln_rows(tok_stats)
        tok_n = ln_norm(xF, xnO, abcT, bbcT, tmpT, tok_rows,
                        norm_wait=norm_wait)
        return tok_n, ()

    tok_attn_done = tok_ring
    tok_x1_dma = {}
    tok_qf_free = ()
    tok_qpf_free = ()
    tok_qb_free = ()
    tok_t2_free = ()
    tok_sum = ()
    a_pre = {}
    a_xn_free = ()
    a_qk_last = ()
    a_stt = {}

    def emit_SA(ti):
        t0 = ti * NT
        if ti == 0:
            wt = tok_aw + tok_ring
        else:
            prev = a_pre[ti - 1]
            wt = prev["norm"] + prev["stats"] + a_qk_last
        p.dma(a_cos[:], cosT[:, t0:t0 + NT], wait=wt)
        p.dma(a_sin[:], sinT[:, t0:t0 + NT])
        tok_x0 = p.dma(a_x0[:].rearrange("p (dc t) -> p dc t", dc=8),
                       x0T.rearrange("(dc p) t -> p dc t", p=128)
                       [:, :, t0:t0 + NT])

        def ax(d):
            return a_x0[:, d * NT:(d + 1) * NT]

        wf = tok_x0 + (tok_attn_done if ti == 0 else ())
        tok_stats = ln_stats(ax, a_xb, a_sq, wf)
        tok_rows = ln_rows(tok_stats)
        a_pre[ti] = {"ax": ax, "stats": tok_stats, "rows": tok_rows,
                     "norm": ()}

    emit_SA(0)
    for ti in range(na):
        t0 = ti * NT
        b0 = 4 * ti
        s0 = b0 % 8
        pre = a_pre[ti]
        ax = pre["ax"]
        tok_ln = ln_norm(ax, lambda d: a_xn[:, d * NT:(d + 1) * NT],
                         a_abc, a_bbc, a_tmp, pre["rows"],
                         norm_wait=a_xn_free, abanks=True)
        pre["norm"] = tok_ln
        tok_cs = pre["rows"]

        if asub < 2:
            continue
        pend_rope = None

        def emit_ropetail():
            nonlocal pend_rope, tok_qb_free, tok_qpf_free, tok_qf_free, \
                tok_t2_free, tok_sum
            if pend_rope is None:
                return
            bank2s, bname2, tok_ev, tok_cast, dst, wt_extra = pend_rope
            pend_rope = None
            tok_pm = p.op("tensor", lambda e, b2=bank2s: e.matmul(
                out=b2, lhsT=t_r128[:], rhs=a_qb[:],
                start=True, stop=True),
                wait=tok_cast + G[bname2], inc=[("mm", 1)])
            tok_qb_free = tok_pm
            tok_evp = p.op("scalar", lambda e, b2=bank2s: e.copy(
                out=a_qpf[:], in_=b2),
                wait=tok_pm + tok_qpf_free, inc=[("ac", 1)])
            G[bname2] = tok_evp
            tok_t1 = p.op("vector", lambda e: e.tensor_mul(
                out=a_t1[:], in0=a_qf[:], in1=a_cos[:]),
                wait=tok_ev + tok_cs, inc=[("ve", 1)])
            tok_qf_free = tok_t1
            tok_t2 = p.op("gpsimd", lambda e: e.tensor_mul(
                out=a_t2[:], in0=a_qpf[:], in1=a_sin[:]),
                wait=tok_evp + tok_t2_free, inc=[("gp", 1)])
            tok_qpf_free = tok_t2
            tok_sum = p.op("vector", lambda e, d=dst: e.tensor_add(
                out=d, in0=a_t1[:], in1=a_t2[:]),
                wait=tok_t1 + tok_t2 + wt_extra, inc=[("ve", 1)])
            tok_t2_free = tok_sum

        for which in range(2):
            for oc in range(8):
                bank, bname = (pAb, "pA") if oc % 2 == 0 else (pBb, "pB")
                bank2s = sc4[:, (oc % 2) * 512:(oc % 2) * 512 + 512]
                bname2 = "sc0" if oc % 2 == 0 else "sc1"
                for dc in range(8):
                    wt = ()
                    if dc == 0:
                        wt = G[bname]
                        if oc == 0 and which == 0:
                            wt = wt + tok_ln
                    p.op("tensor", lambda e, d=dc, o_=oc, b=bank, w=which * 8192:
                         e.matmul(out=b[:, 0:NT],
                                  lhsT=a_w[:, w + d * 1024 + o_ * 128:
                                           w + d * 1024 + o_ * 128 + 128],
                                  rhs=a_xn[:, d * NT:(d + 1) * NT],
                                  start=(d == 0), stop=(d == 7)), wait=wt)
                tok_mm = p.mark("tensor", [("mm", 1)])
                emit_ropetail()
                bcol = (0 if which == 0 else 8) + oc
                tok_ev = p.op("scalar", lambda e, b=bank, bc=bcol: e.activation(
                    out=a_qf[:], in_=b[:, 0:NT], func=AF.Identity,
                    bias=t_cols[:, bc:bc + 1], scale=f32(1.0)),
                    wait=tok_mm + tok_qf_free, inc=[("ac", 1)])
                tok_cast = p.op("vector", lambda e: e.tensor_copy(
                    out=a_qb[:], in_=a_qf[:]),
                    wait=tok_ev + tok_qb_free, inc=[("ve", 1)])
                G[bname] = tok_cast
                if which == 0:
                    dst = a_qr[:, oc * NT:(oc + 1) * NT]
                else:
                    dst = a_kring[:, oc * 1024 + s0 * 128:
                                  oc * 1024 + s0 * 128 + 512]
                wt_extra = tok_attn_done if which == 1 and oc == 0 else ()
                pend_rope = (bank2s, bname2, tok_ev, tok_cast, dst, wt_extra)
        emit_ropetail()
        tok_qk = tok_sum
        a_qk_last = tok_qk

        if asub < 3:
            continue
        for tb in range(4):
            slot = (s0 + tb) % 8
            for fh in range(2):
                bank, bname = (pAb, "pA") if fh == 0 else (pBb, "pB")
                for dc in range(8):
                    wt = G[bname] if dc == 0 else ()
                    if dc == 0 and tb == 0 and fh == 0:
                        wt = wt + tok_attn_done
                    p.op("tensor", lambda e, d=dc, t=tb, f=fh, b=bank: e.matmul(
                        out=b[:, 0:512],
                        lhsT=a_xn[:, d * NT + t * 128: d * NT + t * 128 + 128],
                        rhs=a_w[:, 2 * 8192 + d * 1024 + f * 512:
                                 2 * 8192 + d * 1024 + f * 512 + 512],
                        start=(d == 0), stop=(d == 7)), wait=wt)
                tok_mm = p.mark("tensor", [("mm", 1)])
                tok_vv = ()
                for hh in range(8):
                    h = 8 * fh + hh
                    hp = h // 2
                    base = (slot * 8 + hp) * VSLOT
                    off = base + (0 if h % 2 == 0 else 65)
                    wt2 = tok_mm if hh == 0 else ()
                    tok_vv = p.op("vector", lambda e, b=bank, hh=hh, o_=off, f=fh:
                                  e.tensor_add(
                                      out=a_vring[:, o_:o_ + 64],
                                      in0=b[:, hh * 64:(hh + 1) * 64],
                                      in1=t_bvbc[:, f * 512 + hh * 64:
                                                 f * 512 + hh * 64 + 64]),
                                  wait=wt2, inc=[("ve", 1)] if hh == 7 else ())
                G[bname] = tok_vv
        tok_vdone = tok_vv
        a_xn_free = tok_mm
        if ti + 1 < na:
            emit_SA(ti + 1)

        if asub < 4:
            continue
        mask_t = t_maskA0 if ti == 0 else t_maskA
        tok_fin = tok_attn_done
        tok_qkm_h = {}
        tok_exp_h = {}
        tok_msk_h = {}
        tok_pvm_h = {}

        def a_qk(hp, wt0):
            cnt = 0
            for j in range(4):
                for ch in range(2):
                    slot = (b0 + j - 1 + ch) % 8
                    for eo in range(2):
                        kb = hp * 1024 + slot * 128
                        oo = eo * 1024 + j * 256 + ch * 128
                        wt = wt0 if cnt == 0 else ()
                        cnt += 1
                        p.op("tensor", lambda e, kb=kb, oo=oo, eo=eo, hp=hp, j=j:
                             e.matmul(out=sc4[:, oo:oo + 128],
                                      lhsT=a_kring[eo * 64:eo * 64 + 64,
                                                   kb:kb + 128],
                                      rhs=a_qr[eo * 64:eo * 64 + 64,
                                               hp * NT + j * 128:
                                               hp * NT + j * 128 + 128],
                                      start=True, stop=True), wait=wt)
            return p.mark("tensor", [("mm", 1)])

        def a_pv(hp):
            if asub < 5:
                return ()
            pb = (hp % 2) * 2048
            first = True
            for j in range(4):
                for ch in range(2):
                    slot = (b0 + j - 1 + ch) % 8
                    vb = (slot * 8 + hp) * VSLOT
                    wt = ()
                    if first:
                        wt = tok_msk_h[hp] + G["pvE"] + G["pvO"] + tok_fin
                        first = False
                    pe = a_pm[:, pb + j * 256 + ch * 128:
                              pb + j * 256 + ch * 128 + 128]
                    po = a_pm[:, pb + 1024 + j * 256 + ch * 128:
                              pb + 1024 + j * 256 + ch * 128 + 128]
                    p.op("tensor", lambda e, vb=vb, j=j, ch=ch, pe=pe: e.matmul(
                        out=pvE[0:65, j * 128:(j + 1) * 128],
                        lhsT=a_vring[:, vb:vb + 65],
                        rhs=pe, start=(ch == 0), stop=(ch == 1)), wait=wt)
                    p.op("tensor", lambda e, vb=vb, j=j, ch=ch, po=po: e.matmul(
                        out=pvO[0:65, j * 128:(j + 1) * 128],
                        lhsT=a_vring[:, vb + 65:vb + 130],
                        rhs=po, start=(ch == 0), stop=(ch == 1)))
            return p.mark("tensor", [("mm", 1)])

        def a_fin(hp):
            tok_pvm = tok_pvm_h[hp]
            p.op("vector", lambda e: e.tensor_copy(
                out=t_rows[0:1, 0:512], in_=pvE[64:65, :]), wait=tok_pvm)
            tok_r = p.op("vector", lambda e: e.tensor_copy(
                out=t_rows[0:1, 512:1024], in_=pvO[64:65, :]), inc=[("ve", 1)])
            tok_ob = p.op("vector", lambda e: e.tensor_copy(
                out=a_qb[0:64, :], in_=pvO[0:64, :]), inc=[("ve", 1)])
            p.op("tensor", lambda e: e.matmul(
                out=pAb[0:64, :], lhsT=t_onesr[0:1, 0:64],
                rhs=t_rows[0:1, 0:512], start=True, stop=True),
                wait=tok_r + G["pA"])
            tok_bcm = p.op("tensor", lambda e: e.matmul(
                out=pAb[64:128, :], lhsT=t_onesr[0:1, 0:64],
                rhs=t_rows[0:1, 512:1024], start=True, stop=True),
                inc=[("mm", 1)])
            tok_sh = p.op("tensor", lambda e: e.matmul(
                out=pBb[:, :], lhsT=t_shift[0:64, :], rhs=a_qb[0:64, :],
                start=True, stop=True),
                wait=tok_ob + G["pB"], inc=[("mm", 1)])
            p.op("scalar", lambda e: e.activation(
                out=a_lnx[:], in_=pAb[:, :], func=AF.Ln), wait=tok_bcm)
            tok_rbc = p.op("scalar", lambda e: e.activation(
                out=a_rb[:], in_=a_lnx[:], func=AF.Exp, scale=f32(-1.0)),
                inc=[("ac", 1)])
            G["pA"] = tok_rbc
            p.op("vector", lambda e, hp=hp: e.tensor_mul(
                out=a_oa[0:64, hp * NT:(hp + 1) * NT],
                in0=pvE[0:64, :], in1=a_rb[0:64, :]), wait=tok_rbc)
            tok_f = p.op("vector", lambda e, hp=hp: e.tensor_mul(
                out=a_oa[64:128, hp * NT:(hp + 1) * NT],
                in0=pBb[64:128, :], in1=a_rb[64:128, :]),
                wait=tok_sh, inc=[("ve", 1)])
            G["pvE"] = tok_f
            G["pvO"] = tok_ob
            G["pB"] = tok_f
            return tok_f

        for hp in range(8):
            wt0 = tok_exp_h.get(hp - 1, ())
            if hp == 0:
                wt0 = wt0 + tok_qk + tok_vdone + G["sc"] + G["lnb"]
            tok_qkm_h[hp] = a_qk(hp, wt0)
            tok_exp_h[hp] = p.op("scalar", lambda e: e.activation(
                out=a_p[:], in_=sc4[:, :], func=AF.Exp,
                scale=f32(0.125), bias=f32(-C_A)),
                wait=tok_qkm_h[hp] + tok_msk_h.get(hp - 1, ()), inc=[("ac", 1)])
            tok_msk_h[hp] = p.op("vector", lambda e, m=mask_t, hp=hp: e.tensor_mul(
                out=a_pm[:, (hp % 2) * 2048:(hp % 2) * 2048 + 2048],
                in0=a_p[:], in1=m[:]),
                wait=tok_exp_h[hp] + tok_pvm_h.get(hp - 2, ()), inc=[("ve", 1)])
            if hp >= 1 and asub >= 5:
                tok_pvm_h[hp - 1] = a_pv(hp - 1)
                if asub >= 6:
                    tok_fin = a_fin(hp - 1)
        G["sc"] = tok_exp_h[7]
        if asub >= 5:
            tok_pvm_h[7] = a_pv(7)
            if asub >= 6:
                tok_fin = a_fin(7)
        if asub < 7:
            continue
        tok_x1_last = ()
        for oc in range(8):
            bank, bname = (pAb, "pA") if oc % 2 == 0 else (pBb, "pB")
            for dc in range(8):
                wt = ()
                if dc == 0:
                    wt = G[bname]
                    if oc == 0:
                        wt = wt + tok_fin
                p.op("tensor", lambda e, d=dc, o_=oc, b=bank: e.matmul(
                    out=b[:, 0:NT],
                    lhsT=a_w[:, 3 * 8192 + d * 1024 + o_ * 128:
                             3 * 8192 + d * 1024 + o_ * 128 + 128],
                    rhs=a_oa[:, d * NT:(d + 1) * NT],
                    start=(d == 0), stop=(d == 7)), wait=wt)
            tok_mm = p.mark("tensor", [("mm", 1)])
            sl = oc % 2
            xrt = a_xr0 if sl == 0 else a_qpf
            wxr = a_stt.get(sl, ())
            if oc < 2:
                wxr = wxr + tok_qk + tok_t2_free
            tok_xr = p.dma(xrt[:], x0T[oc * 128:(oc + 1) * 128, t0:t0 + NT],
                           sem=f"xr{sl}", wait=wxr)
            tok_x1 = p.op("vector", lambda e, b=bank, o_=oc, x=xrt: e.scalar_tensor_tensor(
                out=a_x1[:, (o_ % 2) * NT:(o_ % 2) * NT + NT],
                in0=b[:, 0:NT], scalar=t_cols[:, 16 + o_:17 + o_],
                in1=x[:], op0=OP.add, op1=OP.add),
                wait=tok_mm + tok_x1_dma.get(oc % 2, ()) + tok_xr,
                inc=[("ve", 1)])
            a_stt[sl] = tok_x1
            G[bname] = tok_x1
            tok_x1_dma[oc % 2] = p.dma(x1d[oc * 128:(oc + 1) * 128, t0:t0 + NT],
                                       a_x1[:, (oc % 2) * NT:(oc % 2) * NT + NT],
                                       sem=f"x1w{oc % 2}", wait=tok_x1,
                                       eng="gpsimd")
            tok_x1_last = tok_x1
        tok_qpf_free = tok_qpf_free + a_stt[1]
        tok_attn_done = tok_fin + tok_x1_last

    tok_x1_all = (("x1w0", p.cnt["x1w0"]), ("x1w1", p.cnt["x1w1"]))

    if "B" not in phases:
        _finish(nc, p)
        return nc

    gate = tok_attn_done
    for wi, wsrc in enumerate((wbk, wbv, wbq)):
        tok_l = ()
        for h in range(2):
            tok_l = tok_l + p.dma(
                b_fstage[:, h * 4096:(h + 1) * 4096]
                .rearrange("p (dc f) -> p dc f", dc=4),
                wsrc.rearrange("(dc p) f -> p dc f", p=128)
                [:, h * 4:h * 4 + 4, :],
                sem="wv", wait=gate if h == 0 else ())
        gate = p.op("vector", lambda e, wi=wi: e.tensor_copy(
            out=b_w[:, wi * 8192:(wi + 1) * 8192], in_=b_fstage[:]),
            wait=tok_l, inc=[("ve", 1)])
    tok_bw = gate

    Bst = {"hdone": (), "x2t": (), "gb": (), "x2done_mm": (), "xn3f": (),
           "sb_stats": ()}
    tok_x1_free = ()
    tok_x2_dma = {}
    wgu_free = {}
    wd_free = {}
    sg_free = {}
    tok_kvbuf = {}
    x1buf_free = {}
    xn_free = {}
    b_pre = {}

    sb_dat = {}

    sb_load = {}

    def emit_SB_load(ti):
        t0 = ti * NT
        xb1 = (ti % 2) * 8 * NT
        wt0 = (tok_bw + tok_x1_all) if ti == 0 else x1buf_free.get(ti % 2, ())
        sb_load[ti] = p.dma(b_x1[:, xb1:xb1 + 8 * NT]
                            .rearrange("p (dc t) -> p dc t", dc=8),
                            x1d.rearrange("(dc p) t -> p dc t", p=128)
                            [:, :, t0:t0 + NT], wait=wt0)

    def emit_SB_a(ti):
        xb1 = (ti % 2) * 8 * NT
        tok_x1 = sb_load.pop(ti)

        def bx(d, xb1=xb1):
            return b_x1[:, xb1 + d * NT:xb1 + (d + 1) * NT]

        tok_stats = ln_stats(bx, b_xb, b_sq,
                             tok_x1 + Bst.get("ln3_stats", ()))
        sb_dat[ti] = (bx, tok_stats)
        Bst["sb_stats"] = tok_stats

    def emit_SB_r(ti):
        bx, tok_stats = sb_dat[ti]
        sb_dat[ti] = (bx, ln_rows(tok_stats, rows=b_rows2, rkey="rows2"))

    def emit_SB_b(ti):
        bx, tok_rows = sb_dat.pop(ti)
        xo = (ti % 2) * 8 * NT
        tok_n = ln_norm(bx, lambda d, xo=xo: b_xn[:, xo + d * NT:xo + (d + 1) * NT],
                        b_abc, b_bbc, b_tmp, tok_rows,
                        norm_wait=xn_free.get(ti % 2, ()),
                        rows=b_rows2, rkey="rows2",
                        neng="vector", tmp_bank3=True)
        b_pre[ti] = (bx, tok_n)

    def emit_gu(ti):
        xn1 = (ti % 2) * 8 * NT
        bx, tok_ln2 = b_pre[ti]
        for hc in range(24):
            buf = hc % 2
            wtw = wgu_free.get(buf, ())
            if ti == 0 and hc < 2:
                wtw = wtw + tok_wcache
            tok_wl = p.dma(b_wg[:, buf * 1024:(buf + 1) * 1024],
                           wgbf[hc], sem=f"wvg{buf}", wait=wtw)
            tok_wl = tok_wl + p.dma(b_wu[:, buf * 1024:(buf + 1) * 1024],
                                    wubf[hc], sem=f"wvg{buf}")
            gb, gbn = (pAb[:, 0:NT], "pA") if buf == 0 else (sc4[:, 0:NT], "sc0")
            ub, ubn = (pBb[:, 0:NT], "pB") if buf == 0 else (pvO[:, 0:NT], "pvO")
            for dc in range(8):
                wt = (tok_ln2 + tok_wl + G[gbn]) if dc == 0 else ()
                p.op("tensor", lambda e, d=dc, b=buf, gb=gb, xn1=xn1: e.matmul(
                    out=gb,
                    lhsT=b_wg[:, b * 1024 + d * 128: b * 1024 + d * 128 + 128],
                    rhs=b_xn[:, xn1 + d * NT:xn1 + (d + 1) * NT],
                    start=(d == 0), stop=(d == 7)), wait=wt)
            for dc in range(8):
                wt = G[ubn] if dc == 0 else ()
                p.op("tensor", lambda e, d=dc, b=buf, ub=ub, xn1=xn1: e.matmul(
                    out=ub,
                    lhsT=b_wu[:, b * 1024 + d * 128: b * 1024 + d * 128 + 128],
                    rhs=b_xn[:, xn1 + d * NT:xn1 + (d + 1) * NT],
                    start=(d == 0), stop=(d == 7)), wait=wt)
            tok_gu = p.mark("tensor", [("mm", 1)])
            wgu_free[buf] = tok_gu
            if hc == 23:
                xn_free[ti % 2] = tok_gu
            sgs = b_sg[:, (hc % 2) * NT:(hc % 2) * NT + NT]
            tok_sg = p.op("scalar", lambda e, gb=gb, s=sgs: e.activation(
                out=s, in_=gb, func=AF.Silu, scale=f32(DSC)),
                wait=tok_gu + sg_free.get(hc % 2, ()), inc=[("ac", 1)])
            tok_h = p.op("vector", lambda e, h=hc, ub=ub, s=sgs: e.tensor_mul(
                out=b_h[:, h * NT:(h + 1) * NT], in0=ub, in1=s),
                wait=tok_sg + (Bst["x2done_mm"] if hc == 0 else ()),
                inc=[("ve", 1)])
            sg_free[hc % 2] = tok_h
            G[gbn] = tok_sg
            G[ubn] = tok_h
            if hc == 1 and ti >= 1:
                emit_ln3_stats(ti - 1)
            if hc == 2 and ti >= 1:
                emit_ln3_rows(ti - 1)
            if hc == 3 and ti + 1 < nb:
                emit_SB_load(ti + 1)
            if hc == 4 and ti >= 1:
                emit_ln3_mid(ti - 1)
            if hc == 9 and ti + 1 < nb:
                emit_SB_a(ti + 1)
            if hc == 10 and ti + 1 < nb:
                emit_SB_r(ti + 1)
            if hc == 12 and ti + 1 < nb:
                emit_SB_b(ti + 1)
            if hc == 13 and ti >= 1:
                emit_ln3b_stats(ti - 1)
            if hc == 14 and ti >= 1:
                emit_ln3b_rows(ti - 1)
            if hc == 16 and ti >= 1:
                emit_ln3b_fin(ti - 1)
        tok_hdone = tok_h

        Bst["hdone"] = tok_hdone

    def emit_down(ti):
        t0 = ti * NT
        xb1 = (ti % 2) * 8 * NT
        tok_hdone = Bst["hdone"]
        for oc in range(8):
            buf = oc % 2
            tok_wd = p.dma(b_wd[:, buf * 3072:buf * 3072 + 1536],
                           wdbf[oc][:, 0:1536], sem=f"wvd{buf}",
                           wait=wd_free.get(buf, ()))
            tok_wd = tok_wd + p.dma(b_wd[:, buf * 3072 + 1536:(buf + 1) * 3072],
                                    wdbf[oc][:, 1536:3072], sem=f"wvd{buf}")
            bank, bname = (pAb, "pA") if oc % 2 == 0 else (pBb, "pB")
            for hc in range(24):
                wt = (tok_hdone + tok_wd + G[bname]) if hc == 0 else ()
                p.op("tensor", lambda e, h=hc, b=bank, bu=buf: e.matmul(
                    out=b[:, 0:NT],
                    lhsT=b_wd[:, bu * 3072 + h * 128: bu * 3072 + h * 128 + 128],
                    rhs=b_h[:, h * NT:(h + 1) * NT],
                    start=(h == 0), stop=(h == 23)), wait=wt)
            tok_mm = p.mark("tensor", [("mm", 1)])
            tok_x2 = p.op("vector", lambda e, b=bank, o_=oc, xb1=xb1: e.scalar_tensor_tensor(
                out=b_x2[:, o_ * NT:(o_ + 1) * NT], in0=b[:, 0:NT],
                scalar=DSC * DSC, in1=b_x1[:, xb1 + o_ * NT:xb1 + (o_ + 1) * NT],
                op0=OP.mult, op1=OP.add),
                wait=tok_mm + tok_x2_dma.get(oc, ()), inc=[("ve", 1)])
            G[bname] = tok_x2
            wd_free[buf] = tok_x2
            x1buf_free[ti % 2] = tok_x2
            tok_x2_dma[oc] = p.dma(x2d[oc * 128:(oc + 1) * 128, t0:t0 + NT],
                                   b_x2[:, oc * NT:(oc + 1) * NT],
                                   sem=f"x2w{oc}", wait=tok_x2, eng="gpsimd")
        tok_x2t = tok_x2
        tok_x1_free = tok_x2t

        Bst["x2t"] = tok_x2t
        Bst["x2done_mm"] = tok_mm

    def b2f(d):
        return b_x2[:, d * NT:(d + 1) * NT]

    def b3f(d):
        return b_xn3[:, d * NT:(d + 1) * NT]

    def emit_ln3_stats(ti):
        Bst["ln3_stats"] = ln_stats(b2f, b_xb, b_sq,
                                    Bst["x2t"] + Bst["sb_stats"])

    def emit_ln3_rows(ti):
        Bst["ln3_rows"] = ln_rows(Bst["ln3_stats"])

    def emit_ln3_mid(ti):
        Bst["ln3"] = ln_norm(b2f, lambda d: b_xn3[:, d * NT:(d + 1) * NT],
                             b_abc, b_bbc, b_tmp, Bst["ln3_rows"],
                             norm_wait=Bst["xn3f"])

    def emit_ln3b_stats(ti):
        Bst["ln3b_stats"] = ln_stats(b3f, b_xb, b_sq,
                                     Bst["ln3"] + Bst["sb_stats"], bf_in=True)

    def emit_ln3b_rows(ti):
        Bst["ln3b_rows"] = ln_rows(Bst["ln3b_stats"])

    def emit_ln3b_fin(ti):
        tok_ln3b = ln_norm(b3f, lambda d: b_xn3b[:, d * NT:(d + 1) * NT],
                           b_abc, b_bbc, b_tmp, Bst["ln3b_rows"],
                           norm_wait=Bst["xn3f"],
                           neng="vector", tmp_bank3=True)
        for dc in range(8):
            wt = tok_ln3b if dc == 0 else ()
            p.op("scalar", lambda e, d=dc: e.activation(
                out=b_xn3b[:, d * NT:(d + 1) * NT],
                in_=b_xn3b[:, d * NT:(d + 1) * NT], func=AF.Identity,
                bias=t_cols[:, 32 + d:33 + d], scale=t_cols[:, 24 + d:25 + d]),
                wait=wt)
        Bst["gb"] = p.mark("scalar", [("ac", 1)])

    def emit_kv(ti):
        t0 = ti * NT
        tok_gb = Bst["gb"]
        for wi, (wofs, dstd) in enumerate(((0, kd), (2 * 8192, qd))):
            for oc in range(8):
                bank, bname = (pAb, "pA") if oc % 2 == 0 else (pBb, "pB")
                for dc in range(8):
                    wt = G[bname] if dc == 0 else ()
                    if dc == 0 and oc == 0 and wi == 0:
                        wt = wt + tok_gb
                    p.op("tensor", lambda e, d=dc, o_=oc, b=bank, w=wofs: e.matmul(
                        out=b[:, 0:NT],
                        lhsT=b_w[:, w + d * 1024 + o_ * 128:
                                 w + d * 1024 + o_ * 128 + 128],
                        rhs=b_xn3b[:, d * NT:(d + 1) * NT],
                        start=(d == 0), stop=(d == 7)), wait=wt)
                tok_mm = p.mark("tensor", [("mm", 1)])
                sl = (wi * 8 + oc) % 4
                tok_ev = p.op("vector", lambda e, b=bank, s=sl: e.tensor_copy(
                    out=b_kv[:, s * NT:(s + 1) * NT], in_=b[:, 0:NT]),
                    wait=tok_mm + tok_kvbuf.get(sl, ()), inc=[("ve", 1)])
                G[bname] = tok_ev
                tok_kvbuf[sl] = p.dma(dstd[oc * 128:(oc + 1) * 128, t0:t0 + NT],
                                      b_kv[:, sl * NT:(sl + 1) * NT],
                                      sem=f"kvw{sl}", wait=tok_ev, eng="gpsimd")
        for tb in range(4):
            for fh in range(2):
                bank, bname = (pAb, "pA") if fh == 0 else (pBb, "pB")
                for dc in range(8):
                    wt = G[bname] if dc == 0 else ()
                    p.op("tensor", lambda e, d=dc, t=tb, f=fh, b=bank: e.matmul(
                        out=b[:, 0:512],
                        lhsT=b_xn3b[:, d * NT + t * 128: d * NT + t * 128 + 128],
                        rhs=b_w[:, 8192 + d * 1024 + f * 512:
                                 8192 + d * 1024 + f * 512 + 512],
                        start=(d == 0), stop=(d == 7)), wait=wt)
                tok_mm = p.mark("tensor", [("mm", 1)])
                sl = (tb * 2 + fh) % 4
                tok_ev = p.op("vector", lambda e, b=bank, s=sl: e.tensor_copy(
                    out=b_kv[:, s * NT:s * NT + 512], in_=b[:, 0:512]),
                    wait=tok_mm + tok_kvbuf.get(sl, ()), inc=[("ve", 1)])
                G[bname] = tok_ev
                tok_kvbuf[sl] = p.dma(vd[t0 + tb * 128:t0 + tb * 128 + 128,
                                         fh * 512:(fh + 1) * 512],
                                      b_kv[:, sl * NT:sl * NT + 512],
                                      sem=f"kvw{sl}", wait=tok_ev, eng="gpsimd")
                if tb == 3 and fh == 1:
                    Bst["xn3f"] = tok_mm


    emit_SB_load(0)
    emit_SB_a(0)
    emit_SB_r(0)
    emit_SB_b(0)
    emit_gu(0)
    for ti in range(nb):
        emit_down(ti)
        if ti + 1 < nb:
            emit_gu(ti + 1)
        else:
            emit_ln3_stats(ti)
            emit_ln3_rows(ti)
            emit_ln3_mid(ti)
            emit_ln3b_stats(ti)
            emit_ln3b_rows(ti)
            emit_ln3b_fin(ti)
        emit_kv(ti)

    tok_kv_all = tuple((f"kvw{i}", p.cnt[f"kvw{i}"]) for i in range(4))
    tok_x2_all = tuple((f"x2w{i}", p.cnt[f"x2w{i}"]) for i in range(8))
    tok_b_done = G["pA"] + G["pB"]

    if "C" not in phases:
        _finish(nc, p)
        return nc

    tok_l = ()
    for h in range(2):
        tok_l = tok_l + p.dma(
            c_fstage[:, h * 4096:(h + 1) * 4096]
            .rearrange("p (dc f) -> p dc f", dc=4),
            wbo.rearrange("(dc p) f -> p dc f", p=128)[:, h * 4:h * 4 + 4, :],
            sem="wv", wait=tok_b_done if h == 0 else ())
    tok_wo = p.op("vector", lambda e: e.tensor_copy(out=c_wo[:], in_=c_fstage[:]),
                  wait=tok_l, inc=[("ve", 1)])

    tok_qsel = ()
    for qt in range(8):
        q0 = qt * 512
        wt = (tok_kv_all + tok_wo) if qt == 0 else tok_qsel
        tok_a = p.dma(c_st0[:].rearrange("p (dc t) -> p dc t", dc=8),
                      qd.rearrange("(dc p) t -> p dc t", p=128)
                      [:, :, q0:q0 + 512], wait=wt)
        tok_b2 = p.dma(c_st1[:].rearrange("p (dc t) -> p dc t", dc=8),
                       qd.rearrange("(dc p) t -> p dc t", p=128)
                       [:, :, HALF + q0:HALF + q0 + 512])
        for dc in range(8):
            wt2 = (tok_a + tok_b2 + tok_cfl) if dc == 0 else ()
            p.op("vector", lambda e, d=dc: e.tensor_scalar_mul(
                out=c_st0[:, d * 512:(d + 1) * 512],
                in0=c_st0[:, d * 512:(d + 1) * 512],
                scalar1=t_flags[:, 0:1]), wait=wt2)
            p.op("vector", lambda e, d=dc, q=q0: e.scalar_tensor_tensor(
                out=c_q[:, d * HALF + q: d * HALF + q + 512],
                in0=c_st1[:, d * 512:(d + 1) * 512],
                scalar=t_flags[:, 1:2],
                in1=c_st0[:, d * 512:(d + 1) * 512],
                op0=OP.mult, op1=OP.add))
        tok_qsel = p.mark("vector", [("ve", 1)])

    G["sc"] = G["sc"] + G["sc0"] + G["sc1"]
    tok_vones = p.op("vector", lambda e: e.memset(
        c_ve[:].rearrange("p (c f) -> p c f", f=65)[:, :, 64:65], 1.0),
        wait=tok_qsel)
    tok_vones = p.op("vector", lambda e: e.memset(
        c_vo[:].rearrange("p (c f) -> p c f", f=65)[:, :, 64:65], 1.0),
        inc=[("ve", 1)])
    tok_kvld = ()
    tok_fin = ()
    tok_pv_last = {}
    for hp in range(8):
        wt_ld = (tok_kv_all + tok_vones) if hp == 0 else tok_kvld2
        tok_k = p.dma(c_k[:], kd[hp * 128:(hp + 1) * 128, :], wait=wt_ld)
        tok_v1 = ()
        for c0 in range(0, 64, 8):
            tok_v1 = tok_v1 + p.dma(
                c_ve[:].rearrange("p (c f) -> p c f", f=65)[:, c0:c0 + 8, 0:64],
                vd.rearrange("(c p) f -> p c f", p=128)
                [:, c0:c0 + 8, (2 * hp) * 64:(2 * hp) * 64 + 64])
            tok_v1 = tok_v1 + p.dma(
                c_vo[:].rearrange("p (c f) -> p c f", f=65)[:, c0:c0 + 8, 0:64],
                vd.rearrange("(c p) f -> p c f", p=128)
                [:, c0:c0 + 8, (2 * hp + 1) * 64:(2 * hp + 1) * 64 + 64])
        tok_kvld = tok_k + tok_v1
        tok_kvld2 = ()
        for qt in range(8):
            q0 = qt * 512
            tok_exp_g = {}
            tok_pv_last = {}

            def emit_qk(g, wt0):
                for eo in range(2):
                    p.op("tensor", lambda e, g=g, eo=eo, hp=hp, q=q0: e.matmul(
                        out=sc4[:, ((g % 2) * 2 + eo) * 512:
                                ((g % 2) * 2 + eo) * 512 + 512],
                        lhsT=c_k[eo * 64:eo * 64 + 64, g * 128:(g + 1) * 128],
                        rhs=c_q[eo * 64:eo * 64 + 64,
                                hp * HALF + q: hp * HALF + q + 512],
                        start=True, stop=True), wait=wt0 if eo == 0 else ())
                return p.mark("tensor", [("mm", 1)])

            def emit_pv_e(g, extra):
                pe = c_p[:, (g % 2) * 1024:(g % 2) * 1024 + 512]
                p.op("tensor", lambda e, g=g, pe=pe: e.matmul(
                    out=pvE[0:65, :], lhsT=c_ve[:, g * 65:g * 65 + 65],
                    rhs=pe, start=(g == 0), stop=(g == 63)),
                    wait=tok_exp_g[g] + extra)

            def emit_pv_o(g):
                po = c_p[:, (g % 2) * 1024 + 512:(g % 2) * 1024 + 1024]
                p.op("tensor", lambda e, g=g, po=po: e.matmul(
                    out=pvO[0:65, :], lhsT=c_vo[:, g * 65:g * 65 + 65],
                    rhs=po, start=(g == 0), stop=(g == 63)))
                return p.mark("tensor", [("mm", 1)])

            for g in range(64):
                wt0 = tok_exp_g.get(g - 2, ())
                if g <= 1:
                    wt0 = wt0 + tok_kvld + G["sc"]
                tok_qkm = emit_qk(g, wt0)
                if g >= 1:
                    extra = ()
                    if g == 1:
                        extra = tok_fin + G["pvE"] + G["pvO"]
                    emit_pv_e(g - 1, extra)
                    tok_pv_last[g - 1] = emit_pv_o(g - 1)
                tok_exp_g[g] = p.op("scalar", lambda e, g=g: e.activation(
                    out=c_p[:, (g % 2) * 1024:(g % 2) * 1024 + 1024],
                    in_=sc4[:, (g % 2) * 1024:(g % 2) * 1024 + 1024],
                    func=AF.Exp, scale=f32(0.125), bias=f32(-C_B)),
                    wait=tok_qkm + tok_pv_last.get(g - 2, ()), inc=[("ac", 1)])
            emit_pv_e(63, ())
            tok_pv_last[63] = emit_pv_o(63)
            G["sc"] = tok_exp_g[63]
            tok_pvm = tok_pv_last[63]
            p.op("vector", lambda e: e.tensor_copy(
                out=t_rows[0:1, 0:512], in_=pvE[64:65, :]), wait=tok_pvm)
            tok_lr = p.op("vector", lambda e: e.tensor_copy(
                out=t_rows[0:1, 512:1024], in_=pvO[64:65, :]), inc=[("ve", 1)])
            tok_ob = p.op("vector", lambda e: e.tensor_copy(
                out=c_ob[0:64, :], in_=pvO[0:64, :]), inc=[("ve", 1)])
            p.op("tensor", lambda e: e.matmul(
                out=pAb[0:64, :], lhsT=t_onesr[0:1, 0:64],
                rhs=t_rows[0:1, 0:512], start=True, stop=True),
                wait=tok_lr + G["pA"])
            tok_bcm = p.op("tensor", lambda e: e.matmul(
                out=pAb[64:128, :], lhsT=t_onesr[0:1, 0:64],
                rhs=t_rows[0:1, 512:1024], start=True, stop=True),
                inc=[("mm", 1)])
            tok_sh = p.op("tensor", lambda e: e.matmul(
                out=pBb[:, :], lhsT=t_shift[0:64, :], rhs=c_ob[0:64, :],
                start=True, stop=True),
                wait=tok_ob + G["pB"], inc=[("mm", 1)])
            p.op("scalar", lambda e: e.activation(
                out=c_lnx[:], in_=pAb[:, :], func=AF.Ln), wait=tok_bcm)
            tok_rb = p.op("scalar", lambda e: e.activation(
                out=c_rb[:], in_=c_lnx[:], func=AF.Exp, scale=f32(-1.0)),
                inc=[("ac", 1)])
            G["pA"] = tok_rb
            p.op("vector", lambda e, hp=hp, q=q0: e.tensor_mul(
                out=c_o[0:64, hp * HALF + q: hp * HALF + q + 512],
                in0=pvE[0:64, :], in1=c_rb[0:64, :]), wait=tok_rb)
            tok_fin = p.op("vector", lambda e, hp=hp, q=q0: e.tensor_mul(
                out=c_o[64:128, hp * HALF + q: hp * HALF + q + 512],
                in0=pBb[64:128, :], in1=c_rb[64:128, :]),
                wait=tok_sh, inc=[("ve", 1)])
            G["pvE"] = tok_fin
            G["pvO"] = tok_lr + tok_ob
            G["pB"] = tok_fin
        tok_kvld2 = tok_pv_last[63] + tok_exp_g[63]

    if taps:
        p.dma(od.rearrange("(dc p) t -> p dc t", p=128),
              c_o[:].rearrange("p (dc t) -> p dc t", dc=8),
              sem="s3", wait=tok_fin)

    if "D" not in phases:
        _finish(nc, p)
        return nc

    tok_out_dma = {}
    tok_stage_free = tok_kvld2 + tok_fin
    wgu_free = {0: tok_kvld2, 1: tok_kvld2}
    wd_free = {0: tok_kvld2, 1: tok_kvld2}
    sg_free = {}
    for ti in range(nd):
        t0 = ti * NT
        tok_a = p.dma(d_xs0[:].rearrange("p (dc t) -> p dc t", dc=8),
                      x2d.rearrange("(dc p) t -> p dc t", p=128)
                      [:, :, t0:t0 + NT], wait=tok_x2_all + tok_stage_free)
        tok_b2 = p.dma(d_xs1a[:].rearrange("p (dc t) -> p dc t", dc=6),
                       x2d.rearrange("(dc p) t -> p dc t", p=128)
                       [:, 0:6, HALF + t0:HALF + t0 + NT])
        tok_b2 = tok_b2 + p.dma(
            d_xs1b[:].rearrange("p (dc t) -> p dc t", dc=2),
            x2d.rearrange("(dc p) t -> p dc t", p=128)
            [:, 6:8, HALF + t0:HALF + t0 + NT])
        for dc in range(8):
            wt = (tok_a + tok_b2) if dc == 0 else ()
            p.op("vector", lambda e, d=dc: e.tensor_scalar_mul(
                out=d_xs0[:, d * NT:(d + 1) * NT],
                in0=d_xs0[:, d * NT:(d + 1) * NT],
                scalar1=t_flags[:, 0:1]), wait=wt)
            p.op("vector", lambda e, d=dc: e.scalar_tensor_tensor(
                out=d_xs0[:, d * NT:(d + 1) * NT],
                in0=dxs1(d), scalar=t_flags[:, 1:2],
                in1=d_xs0[:, d * NT:(d + 1) * NT], op0=OP.mult, op1=OP.add))
        tok_x2sel = p.mark("vector", [("ve", 1)])

        for oc in range(8):
            bank, bname = (pAb, "pA") if oc % 2 == 0 else (pBb, "pB")
            for dc in range(8):
                wt = ()
                if dc == 0:
                    wt = G[bname] + tok_x2sel
                p.op("tensor", lambda e, d=dc, o_=oc, b=bank, t=t0: e.matmul(
                    out=b[:, 0:NT],
                    lhsT=c_wo[:, d * 1024 + o_ * 128: d * 1024 + o_ * 128 + 128],
                    rhs=c_o[:, d * HALF + t: d * HALF + t + NT],
                    start=(d == 0), stop=(d == 7)), wait=wt)
            tok_mm = p.mark("tensor", [("mm", 1)])
            tok_x3 = p.op("vector", lambda e, b=bank, o_=oc: e.tensor_add(
                out=d_x3[:, o_ * NT:(o_ + 1) * NT], in0=b[:, 0:NT],
                in1=d_xs0[:, o_ * NT:(o_ + 1) * NT]),
                wait=tok_mm, inc=[("ve", 1)])
            G[bname] = tok_x3
        tok_x3f = tok_x3

        def dxf(d):
            return d_x3[:, d * NT:(d + 1) * NT]

        tok_ln4, _ = layer_norm(dxf, lambda d: d_xn[:, d * NT:(d + 1) * NT],
                                d_xb, d_sq, d_abc, d_bbc, d_tmp, tok_x3f)
        for hc in range(24):
            buf = hc % 2
            tok_wl = p.dma(d_wg[:, buf * 1024:(buf + 1) * 1024],
                           wgbf[hc], sem=f"wvg{buf}", wait=wgu_free.get(buf, ()))
            tok_wl = tok_wl + p.dma(d_wu[:, buf * 1024:(buf + 1) * 1024],
                                    wubf[hc], sem=f"wvg{buf}")
            gb, gbn = (pAb[:, 0:NT], "pA") if buf == 0 else (sc4[:, 0:NT], "sc0")
            ub, ubn = (pBb[:, 0:NT], "pB") if buf == 0 else (pvO[:, 0:NT], "pvO")
            for dc in range(8):
                wt = (tok_ln4 + tok_wl + G[gbn]) if dc == 0 else ()
                p.op("tensor", lambda e, d=dc, b=buf, gb=gb: e.matmul(
                    out=gb,
                    lhsT=d_wg[:, b * 1024 + d * 128: b * 1024 + d * 128 + 128],
                    rhs=d_xn[:, d * NT:(d + 1) * NT],
                    start=(d == 0), stop=(d == 7)), wait=wt)
            for dc in range(8):
                wt = G[ubn] if dc == 0 else ()
                p.op("tensor", lambda e, d=dc, b=buf, ub=ub: e.matmul(
                    out=ub,
                    lhsT=d_wu[:, b * 1024 + d * 128: b * 1024 + d * 128 + 128],
                    rhs=d_xn[:, d * NT:(d + 1) * NT],
                    start=(d == 0), stop=(d == 7)), wait=wt)
            tok_gu = p.mark("tensor", [("mm", 1)])
            wgu_free[buf] = tok_gu
            sgs = d_sg[:, (hc % 2) * NT:(hc % 2) * NT + NT]
            tok_sg = p.op("scalar", lambda e, gb=gb, s=sgs: e.activation(
                out=s, in_=gb, func=AF.Silu, scale=f32(DSC)),
                wait=tok_gu + sg_free.get(hc % 2, ()), inc=[("ac", 1)])
            tok_h = p.op("vector", lambda e, h=hc, ub=ub, s=sgs: e.tensor_mul(
                out=d_h[:, h * NT:(h + 1) * NT], in0=ub, in1=s),
                wait=tok_sg, inc=[("ve", 1)])
            sg_free[hc % 2] = tok_h
            G[gbn] = tok_sg
            G[ubn] = tok_h
        tok_hdone = tok_h
        for oc in range(8):
            buf = oc % 2
            tok_wd = p.dma(d_wd[:, buf * 3072:buf * 3072 + 1536],
                           wdbf[oc][:, 0:1536], sem=f"wvd{buf}",
                           wait=wd_free.get(buf, ()))
            tok_wd = tok_wd + p.dma(d_wd[:, buf * 3072 + 1536:(buf + 1) * 3072],
                                    wdbf[oc][:, 1536:3072], sem=f"wvd{buf}")
            bank, bname = (pAb, "pA") if oc % 2 == 0 else (pBb, "pB")
            for hc in range(24):
                wt = (tok_hdone + tok_wd + G[bname]) if hc == 0 else ()
                p.op("tensor", lambda e, h=hc, b=bank, bu=buf: e.matmul(
                    out=b[:, 0:NT],
                    lhsT=d_wd[:, bu * 3072 + h * 128: bu * 3072 + h * 128 + 128],
                    rhs=d_h[:, h * NT:(h + 1) * NT],
                    start=(h == 0), stop=(h == 23)), wait=wt)
            tok_mm = p.mark("tensor", [("mm", 1)])
            tok_o = p.op("vector", lambda e, b=bank, o_=oc: e.scalar_tensor_tensor(
                out=d_out[:, (o_ % 2) * NT:(o_ % 2) * NT + NT],
                in0=b[:, 0:NT], scalar=DSC * DSC,
                in1=d_x3[:, o_ * NT:(o_ + 1) * NT], op0=OP.mult, op1=OP.add),
                wait=tok_mm + tok_out_dma.get(oc % 2, ()), inc=[("ve", 1)])
            G[bname] = tok_o
            wd_free[buf] = tok_o
            tok_out_dma[oc % 2] = p.dma(outT[oc * 128:(oc + 1) * 128, t0:t0 + NT],
                                        d_out[:, (oc % 2) * NT:(oc % 2) * NT + NT],
                                        sem=f"ow{oc % 2}", wait=tok_o,
                                        eng="gpsimd")
        tok_stage_free = tok_x3f

    _finish(nc, p)
    return nc


def _finish(nc, p):
    p.op("sync", lambda e: e.nop(),
         wait=[(s, v) for s, v in p.cnt.items() if v > 0])
    import contextlib
    stack = contextlib.ExitStack()
    sems = {s: stack.enter_context(nc.semaphore(s)) for s in p.cnt}
    with nc.Block() as block:
        _replay(block, p, sems)
    stack.close()


def eo_off(ch):
    return 256 + ch * 128


def _host_constants():
    pos = np.arange(S, dtype=np.float64)
    inv = 1.0 / (10000.0 ** (np.arange(0, HD, 2, dtype=np.float64) / HD))
    ang = pos[:, None] * inv[None, :]
    cos64 = np.tile(np.cos(ang), (1, 2)).astype(np.float32)
    sin64 = np.tile(np.sin(ang), (1, 2)).astype(np.float32)
    sgn = np.ones(64, np.float32)
    sgn[:32] = -1.0
    sinp = sin64 * sgn[None, :]
    cosT_ = np.ascontiguousarray(np.concatenate([cos64.T, cos64.T], axis=0))
    sinT_ = np.ascontiguousarray(np.concatenate([sinp.T, sinp.T], axis=0))
    r = np.zeros((128, 128), np.float32)
    for h in range(2):
        for i in range(64):
            r[h * 64 + (i + 32) % 64, h * 64 + i] = 1.0
    r128_ = np.ascontiguousarray(r.astype(bf16))
    sh = np.zeros((128, 128), np.float32)
    for k in range(64):
        sh[k, 64 + k] = 1.0
    shmat_ = np.ascontiguousarray(sh.astype(bf16))
    k_i = np.arange(128)[:, None]
    q_i = np.arange(128)[None, :]
    mprev = (k_i >= q_i).astype(np.float32)
    mcur = (k_i <= q_i).astype(np.float32)
    blockm = np.concatenate([mprev, mcur], axis=1)
    maskA_ = np.ascontiguousarray(np.tile(blockm, (1, 8)).astype(bf16))
    block0 = np.concatenate([0 * mprev, mcur], axis=1)
    half = np.concatenate([block0] + [blockm] * 3, axis=1)
    maskA0_ = np.ascontiguousarray(np.tile(half, (1, 2)).astype(bf16))
    return cosT_, sinT_, r128_, maskA_, maskA0_, shmat_


def make_in_maps(L0_attn_a_hidden_states, L0_attn_a_Wq, L0_attn_a_bq,
                 L0_attn_a_Wk, L0_attn_a_bk, L0_attn_a_Wv, L0_attn_a_bv,
                 L0_attn_a_Wo, L0_attn_a_bo, L0_attn_b_ln_gamma,
                 L0_attn_b_ln_beta, L0_attn_b_Wq, L0_attn_b_Wk, L0_attn_b_Wv,
                 L0_attn_b_Wo, L0_ffn_w_gate, L0_ffn_w_up, L0_ffn_w_down):
    x = np.asarray(L0_attn_a_hidden_states, np.float32)
    cosT_, sinT_, r128_, maskA_, maskA0_, shmat_ = _host_constants()

    cols = np.zeros((128, 48), np.float32)
    cols[:, 0:8] = np.asarray(L0_attn_a_bq, np.float32).reshape(8, 128).T
    cols[:, 8:16] = np.asarray(L0_attn_a_bk, np.float32).reshape(8, 128).T
    cols[:, 16:24] = np.asarray(L0_attn_a_bo, np.float32).reshape(8, 128).T
    cols[:, 24:32] = np.asarray(L0_attn_b_ln_gamma, np.float32).reshape(8, 128).T
    cols[:, 32:40] = np.asarray(L0_attn_b_ln_beta, np.float32).reshape(8, 128).T
    bvrow_ = np.asarray(L0_attn_a_bv, np.float32).reshape(1, DM)

    shared = {
        "waqT": np.ascontiguousarray(np.asarray(L0_attn_a_Wq, np.float32).T),
        "wakT": np.ascontiguousarray(np.asarray(L0_attn_a_Wk, np.float32).T),
        "wavT": np.ascontiguousarray(np.asarray(L0_attn_a_Wv, np.float32).T),
        "waoT": np.ascontiguousarray(np.asarray(L0_attn_a_Wo, np.float32).T),
        "acols": cols, "bvrow": bvrow_, "r128": r128_,
        "cosT": cosT_, "sinT": sinT_, "maskA": maskA_, "maskA0": maskA0_,
        "shmat": shmat_,
        "wbq": np.ascontiguousarray(np.asarray(L0_attn_b_Wq, np.float32)),
        "wbk": np.ascontiguousarray(np.asarray(L0_attn_b_Wk, np.float32)),
        "wbv": np.ascontiguousarray(np.asarray(L0_attn_b_Wv, np.float32)),
        "wbo": np.ascontiguousarray(np.asarray(L0_attn_b_Wo, np.float32)),
        "wgT": np.ascontiguousarray(np.asarray(L0_ffn_w_gate, np.int32).T),
        "wuT": np.ascontiguousarray(np.asarray(L0_ffn_w_up, np.int32).T),
        "wdT": np.ascontiguousarray(np.asarray(L0_ffn_w_down, np.int32).T),
    }

    in_maps = []
    for c in range(8):
        b, h = c // 2, c % 2
        fl = np.zeros((128, 4), np.float32)
        fl[:, 0] = 1.0 if h == 0 else 0.0
        fl[:, 1] = 1.0 if h == 1 else 0.0
        m = dict(shared)
        m["x0T"] = np.ascontiguousarray(x[b].T)
        m["flags"] = fl
        m["flagsb"] = np.ascontiguousarray(fl.astype(bf16))
        in_maps.append(m)

    return in_maps


def kernel(**inputs):
    in_maps = make_in_maps(**inputs)
    nc = _build(taps=False)
    res = run_bass_kernel_spmd(nc, in_maps, core_ids=list(range(8)))
    out = np.empty((BATCH, S, DM), np.float32)
    for c in range(8):
        b, h = c // 2, c % 2
        out[b, h * HALF:(h + 1) * HALF, :] = res.results[c]["outT"].T
    return out


# version 33
# speedup vs baseline: 1.1673x; 1.0090x over previous
import numpy as np
import ml_dtypes
import concourse.bass as bass
import concourse.mybir as mybir
from concourse.bass_utils import run_bass_kernel_spmd

F32 = mybir.dt.float32
BF16 = mybir.dt.bfloat16
I32 = mybir.dt.int32
AF = mybir.ActivationFunctionType
OP = mybir.AluOpType
f32 = np.float32
bf16 = ml_dtypes.bfloat16

BATCH, S, DM, NH, HD = 4, 8192, 1024, 16, 64
FH = 3072
HALF = S // 2
NT = 512
EPS = 1e-5
C_A = 8.0
C_B = 16.0
DSC = 1.0 / 127.0
VSLOT = 130

ENG = ("sync", "tensor", "vector", "scalar", "gpsimd")
KB = 1024


class Plan:
    def __init__(self):
        self.ops = {e: [] for e in ENG}
        self.cnt = {}

    def sem(self, name):
        self.cnt[name] = 0

    def op(self, eng, fn, wait=(), inc=()):
        by = {}
        for s, v in wait:
            by[s] = max(v, by.get(s, 0))
        wait = tuple(sorted(by.items()))
        for s, v in wait:
            assert v <= self.cnt[s], f"wait on future: {s} {v} > {self.cnt[s]}"
        toks = []
        for s, n in inc:
            self.cnt[s] += n
            toks.append((s, self.cnt[s]))
        self.ops[eng].append((fn, wait, tuple(inc)))
        return tuple(toks)

    def mark(self, eng, inc):
        fn, waits, incs = self.ops[eng][-1]
        toks = []
        for s, n in inc:
            self.cnt[s] += n
            toks.append((s, self.cnt[s]))
        self.ops[eng][-1] = (fn, waits, incs + tuple(inc))
        return tuple(toks)

    def dma(self, dst, src, sem="ld", wait=(), eng="sync"):
        return self.op(eng, lambda e, d=dst, s=src: e.dma_start(out=d, in_=s),
                       wait=wait, inc=[(sem, 16)])


def _replay(block, plan, sems):
    def mk(ops):
        def body(eng):
            for fn, waits, incs in ops:
                for s, v in waits:
                    eng.wait_ge(sems[s], v)
        return body

    for e in ENG:
        ops = plan.ops[e]

        def body(eng, ops=ops):
            for fn, waits, incs in ops:
                for s, v in waits:
                    eng.wait_ge(sems[s], v)
                inst = fn(eng)
                if incs:
                    if inst is None:
                        inst = eng.nop()
                    for s, n in incs:
                        inst = inst.then_inc(sems[s], n)
        getattr(block, e)(body)


def _build(taps=False, phases="ABCD", na=16, nb=16, nd=8, asub=7):
    nc = bass.Bass("TRN2", target_bir_lowering=False, debug=False,
                   detect_race_conditions=False)

    def ein(name, shape, dtype):
        return nc.dram_tensor(name, list(shape), dtype, kind="ExternalInput").ap()

    x0T = ein("x0T", (DM, S), F32)
    waqT = ein("waqT", (DM, DM), F32)
    wakT = ein("wakT", (DM, DM), F32)
    wavT = ein("wavT", (DM, DM), F32)
    waoT = ein("waoT", (DM, DM), F32)
    acols = ein("acols", (128, 48), F32)
    bvrow = ein("bvrow", (1, DM), F32)
    r128 = ein("r128", (128, 128), BF16)
    cosT = ein("cosT", (128, S), F32)
    sinT = ein("sinT", (128, S), F32)
    maskA = ein("maskA", (128, 2048), BF16)
    maskA0 = ein("maskA0", (128, 2048), BF16)
    wbq = ein("wbq", (DM, DM), F32)
    wbk = ein("wbk", (DM, DM), F32)
    wbv = ein("wbv", (DM, DM), F32)
    wbo = ein("wbo", (DM, DM), F32)
    wgT = ein("wgT", (DM, FH), I32)
    wuT = ein("wuT", (DM, FH), I32)
    wdT = ein("wdT", (FH, DM), I32)
    flags = ein("flags", (128, 4), F32)
    flagsb = ein("flagsb", (128, 4), BF16)
    shmat = ein("shmat", (128, 128), BF16)

    outT = nc.dram_tensor("outT", [DM, HALF], F32, kind="ExternalOutput").ap()
    ikind = "ExternalOutput" if taps else "Internal"
    x1d = nc.dram_tensor("x1d", [DM, S], F32, kind=ikind).ap()
    x2d = nc.dram_tensor("x2d", [DM, S], F32, kind=ikind).ap()
    kd = nc.dram_tensor("kd", [DM, S], BF16, kind=ikind).ap()
    vd = nc.dram_tensor("vd", [S, DM], BF16, kind=ikind).ap()
    qd = nc.dram_tensor("qd", [DM, S], BF16, kind=ikind).ap()
    od = nc.dram_tensor("od", [DM, HALF], BF16,
                        kind="ExternalOutput" if taps else "Internal").ap()
    wgbf = nc.dram_tensor("wgbf", [24, 128, 1024], BF16, kind="Internal").ap()
    wubf = nc.dram_tensor("wubf", [24, 128, 1024], BF16, kind="Internal").ap()
    wdbf = nc.dram_tensor("wdbf", [8, 128, 3072], BF16, kind="Internal").ap()

    LIMIT = 212832
    SB_BASE = 16512

    def sb(name, shape, dtype, off):
        nb = int(np.prod(shape[1:])) * mybir.dt.size(dtype)
        assert off >= 0 and off + nb <= LIMIT, (name, off, nb)
        assert SB_BASE + off + nb <= nc.sbuf_top, (name, off, nb)
        return nc.alloc_sbuf_tensor_at(name, list(shape), dtype,
                                       offset=SB_BASE + off)

    t_cols = sb("cols", [128, 48], F32, 0)
    t_onesb = sb("onesb", [128, 16], BF16, 192)
    t_onesr = sb("onesr", [1, 128], F32, 224)
    t_r128 = sb("r128s", [128, 128], BF16, 736)
    t_flags = sb("flagss", [128, 4], F32, 992)
    t_flagsb = sb("flagsbs", [128, 4], BF16, 1024)
    t_rows = sb("rowsP", [1, 2048], F32, 1056)
    t_maskA = sb("maskAs", [128, 2048], BF16, 9248)
    t_maskA0 = sb("maskA0s", [128, 2048], BF16, 13344)
    t_bvbc = sb("bvbc", [128, DM], F32, 17440)
    t_shift = sb("shifts", [128, 128], BF16, 21536)
    PH = 22 * KB

    o = PH
    a_w = sb("a_w", [128, 4 * 8192], BF16, o); o += 64 * KB
    a_x0 = sb("a_x0", [128, 8 * NT], F32, o); o += 16 * KB
    a_xb = sb("a_xb", [128, 2 * NT], BF16, o); o += 2 * KB
    a_sq = sb("a_sq", [128, 2 * NT], BF16, o); o += 2 * KB
    a_xn = sb("a_xn", [128, 8 * NT], BF16, o); o += 8 * KB
    a_abc = sb("a_abc", [128, NT], F32, o); o += 2 * KB
    a_bbc = sb("a_bbc", [128, NT], F32, o); o += 2 * KB
    a_tmp = sb("a_tmp", [128, NT], F32, o); o += 2 * KB
    a_qf = sb("a_qf", [128, NT], F32, o); o += 2 * KB
    a_qb = sb("a_qb", [128, NT], BF16, o); o += 1 * KB
    a_qpf = sb("a_qpf", [128, NT], F32, o); o += 2 * KB
    a_t1 = sb("a_t1", [128, NT], F32, o); o += 2 * KB
    a_t2 = sb("a_t2", [128, NT], F32, o); o += 2 * KB
    a_qr = sb("a_qr", [128, 8 * NT], BF16, o); o += 8 * KB
    a_cos = sb("a_cos", [128, NT], F32, o); o += 2 * KB
    a_sin = sb("a_sin", [128, NT], F32, o); o += 2 * KB
    a_kring = sb("a_kring", [128, 8 * 8 * 128], BF16, o); o += 16 * KB
    a_vring = sb("a_vring", [128, 8 * 8 * VSLOT], BF16, o); o += 8 * 8 * VSLOT * 2
    a_p = sb("a_p", [128, 2048], BF16, o); o += 4 * KB
    a_pm = sb("a_pm", [128, 2 * 2048], BF16, o); o += 8 * KB
    a_oa = sb("a_oa", [128, 8 * NT], BF16, o); o += 8 * KB
    a_rb = sb("a_rb", [128, NT], F32, o); o += 2 * KB
    a_x1 = sb("a_x1", [128, 2 * NT], F32, o); o += 4 * KB
    a_lnx = sb("a_lnx", [128, NT], F32, o); o += 2 * KB
    a_xr0 = sb("a_xr0", [128, NT], F32, o); o += 2 * KB
    assert o <= LIMIT, o

    i_wstage = sb("i_wstage", [128, 12288], I32, LIMIT - 48 * KB)
    i_wcvt = sb("i_wcvt", [128, 12288], BF16, LIMIT - 72 * KB)
    i_astage = sb("i_astage", [128, 8192], F32, LIMIT - 104 * KB)

    o = PH
    b_w = sb("b_w", [128, 3 * 8192], BF16, o); o += 48 * KB
    b_x1 = sb("b_x1", [128, 2 * 8 * NT], F32, o); o += 32 * KB
    b_xb = sb("b_xb", [128, 2 * NT], BF16, o); o += 2 * KB
    b_sq = sb("b_sq", [128, 2 * NT], BF16, o); o += 2 * KB
    b_xn = sb("b_xn", [128, 2 * 8 * NT], BF16, o); o += 16 * KB
    b_abc = sb("b_abc", [128, NT], F32, o); o += 2 * KB
    b_bbc = sb("b_bbc", [128, NT], F32, o); o += 2 * KB
    b_tmp = sb("b_tmp", [128, NT], F32, o); o += 2 * KB
    b_wg = sb("b_wg", [128, 2 * 1024], BF16, o); o += 4 * KB
    b_wu = sb("b_wu", [128, 2 * 1024], BF16, o); o += 4 * KB
    b_sg = sb("b_sg", [128, 2 * NT], BF16, o); o += 2 * KB
    b_h = sb("b_h", [128, 24 * NT], BF16, o); o += 24 * KB
    b_wd = sb("b_wd", [128, 2 * 3072], BF16, o); o += 12 * KB
    b_x2 = sb("b_x2", [128, 8 * NT], F32, o); o += 16 * KB
    b_xn3 = sb("b_xn3", [128, 8 * NT], BF16, o); o += 8 * KB
    b_xn3b = sb("b_xn3b", [128, 8 * NT], BF16, o); o += 8 * KB
    assert o <= LIMIT, o
    b_rows2 = sb("b_rows2", [1, 2048], F32, 9248)
    b_kv = sb("b_kv", [128, 4 * NT], BF16, 17440)
    b_fstage = sb("b_fstage", [128, 8192], F32,
                  PH + 48 * KB + 16 * KB + 2 * KB + 2 * KB + 8 * KB
                  + 2 * KB + 2 * KB + 2 * KB + 4 * KB + 4 * KB + 1 * KB)

    o = PH
    c_q = sb("c_q", [128, 8 * HALF], BF16, o); o += 64 * KB
    C_O_OFF = o
    c_o = sb("c_o", [128, 8 * HALF], BF16, o); o += 64 * KB
    c_wo = sb("c_wo", [128, 8192], BF16, o); o += 16 * KB
    C_K_OFF = o
    c_st0 = sb("c_st0", [128, 8 * NT], BF16, C_K_OFF)
    c_st1 = sb("c_st1", [128, 8 * NT], BF16, C_K_OFF + 8 * KB)
    c_k = sb("c_k", [128, S], BF16, C_K_OFF); o += 16 * KB
    c_ve = sb("c_ve", [128, 64 * 65], BF16, o); o += 64 * 65 * 2
    c_vo = sb("c_vo", [128, 64 * 65], BF16, o); o += 64 * 65 * 2
    c_p = sb("c_p", [128, 2 * 2048], BF16, o); o += 8 * KB
    assert o <= LIMIT, o
    c_rb = sb("c_rb", [128, NT], F32, 17440)
    c_lnx = sb("c_lnx", [128, NT], F32, 13344)
    c_ob = sb("c_ob", [128, NT], BF16, 15392)
    c_fstage = sb("c_fstage", [128, 8192], F32, C_O_OFF)

    o = PH
    d_x3 = sb("d_x3", [128, 8 * NT], F32, o); o += 16 * KB
    d_xs0 = sb("d_xs0", [128, 8 * NT], F32, o); o += 16 * KB
    d_xn = sb("d_xn", [128, 8 * NT], BF16, o); o += 8 * KB
    d_xb = sb("d_xb", [128, 2 * NT], BF16, o); o += 2 * KB
    d_sq = sb("d_sq", [128, 2 * NT], BF16, o); o += 2 * KB
    d_abc = sb("d_abc", [128, NT], F32, o); o += 2 * KB
    d_bbc = sb("d_bbc", [128, NT], F32, o); o += 2 * KB
    d_tmp = sb("d_tmp", [128, NT], F32, o); o += 2 * KB
    d_sg = sb("d_sg", [128, 2 * NT], BF16, o); o += 2 * KB
    d_out = sb("d_out", [128, 2 * NT], F32, o); o += 4 * KB
    d_wg = sb("d_wg", [128, 2 * 1024], BF16, o); o += 4 * KB
    d_wu = sb("d_wu", [128, 2 * 1024], BF16, o); o += 4 * KB
    assert o <= PH + 64 * KB, o
    o2 = C_K_OFF
    d_h = sb("d_h", [128, 24 * NT], BF16, o2); o2 += 24 * KB
    d_wd = sb("d_wd", [128, 2 * 3072], BF16, o2); o2 += 12 * KB
    assert o2 <= LIMIT, o2
    d_xs1a = sb("d_xs1a", [128, 6 * NT], F32, 9248)
    d_xs1b = sb("d_xs1b", [128, 2 * NT], F32, C_K_OFF + 36 * KB)

    def dxs1(dd):
        if dd < 6:
            return d_xs1a[:, dd * NT:(dd + 1) * NT]
        return d_xs1b[:, (dd - 6) * NT:(dd - 5) * NT]

    sc4 = nc.alloc_psum_tensor("sc4", [128, 2048], F32)
    pvE = nc.alloc_psum_tensor("pvE", [128, 512], F32)
    pvO = nc.alloc_psum_tensor("pvO", [128, 512], F32)
    pAb = nc.alloc_psum_tensor("pAb", [128, 512], F32)
    pBb = nc.alloc_psum_tensor("pBb", [128, 512], F32)

    p = Plan()
    for s in ("ld", "wv", "mm", "ve", "ac", "gp", "s2", "s3", "fin"):
        p.sem(s)
    for s in ("x1w0", "x1w1"):
        p.sem(s)
    for i in range(8):
        p.sem(f"x2w{i}")
    for i in range(4):
        p.sem(f"kvw{i}")
    p.sem("ow0"); p.sem("ow1")
    p.sem("xr0"); p.sem("xr1")
    for s in ("wvg0", "wvg1", "wvd0", "wvd1"):
        p.sem(s)
    for s in ("wva0", "wva1", "wvf0", "wvf1", "s30", "s31"):
        p.sem(s)

    G = {"sc": (), "sc0": (), "sc1": (), "pvE": (), "pvO": (),
         "pA": (), "pB": (), "lnb": (), "rows": (), "rows2": (), "abc": ()}

    p.dma(t_cols[:], acols)
    p.dma(t_r128[:], r128)
    p.dma(t_maskA[:], maskA)
    p.dma(t_maskA0[:], maskA0)
    p.dma(t_flags[:], flags)
    tok_cfl = p.dma(t_flagsb[:], flagsb)
    p.dma(t_shift[:], shmat)
    p.op("vector", lambda e: e.memset(t_onesb[:], 1.0))
    p.op("vector", lambda e: e.memset(t_onesr[:], 1.0))
    tok_ones = p.mark("vector", [("ve", 1)])

    tok_bvr = p.dma(i_astage[0:1, 0:DM], bvrow)
    for j in range(2):
        p.op("tensor", lambda e, j=j: e.matmul(
            out=sc4[:, j * 512:(j + 1) * 512], lhsT=t_onesr[0:1, :],
            rhs=i_astage[0:1, j * 512:(j + 1) * 512], start=True, stop=True),
            wait=tok_bvr + tok_ones if j == 0 else ())
    tok_bvm = p.mark("tensor", [("mm", 1)])
    tok_bvc = p.op("vector", lambda e: e.tensor_copy(
        out=t_bvbc[:], in_=sc4[:, 0:1024]), wait=tok_bvm, inc=[("ve", 1)])
    G["sc"] = tok_bvc

    cvt_done = {}
    for u in range(8):
        wi, h = u // 2, u % 2
        wsrc = (waqT, wakT, wavT, waoT)[wi]
        sl = (u % 2) * 4096
        tok_l = p.dma(i_astage[:, sl:sl + 4096]
                      .rearrange("p (dc f) -> p dc f", dc=4),
                      wsrc.rearrange("(dc p) f -> p dc f", p=128)
                      [:, h * 4:h * 4 + 4, :],
                      sem=f"wva{u % 2}", wait=(tok_bvc if u == 0 else ())
                      + cvt_done.get(u - 2, ()),
                      eng=("gpsimd", "scalar")[u % 2])
        cvt_done[u] = p.op("vector", lambda e, wi=wi, h=h, sl=sl: e.tensor_copy(
            out=a_w[:, wi * 8192 + h * 4096:wi * 8192 + h * 4096 + 4096],
            in_=i_astage[:, sl:sl + 4096]),
            wait=tok_l, inc=[("ve", 1)])
    tok_aw = cvt_done[6] + cvt_done[7]

    units = []
    for wsrc, wdst, n_in, n_out in ((wgT, wgbf, 8, 24), (wuT, wubf, 8, 24),
                                    (wdT, wdbf, 24, 8)):
        for ic in range(n_in):
            units.append((wsrc, wdst, ic, n_out))
    f_cvt = {}
    f_store = {}
    for u, (wsrc, wdst, ic, n_out) in enumerate(units):
        sl = (u % 2) * 6144
        nh = n_out * 128 // 2
        tok_l = ()
        for h in range(2):
            tok_l = tok_l + p.dma(
                i_wstage[:, sl + h * nh:sl + (h + 1) * nh],
                wsrc[ic * 128:(ic + 1) * 128, h * nh:(h + 1) * nh],
                sem=f"wvf{u % 2}", wait=f_cvt.get(u - 2, ()),
                eng=("gpsimd", "scalar")[h])
        tok_c = p.op("vector", lambda e, sl=sl, n=n_out: e.tensor_copy(
            out=i_wcvt[:, sl:sl + n * 128], in_=i_wstage[:, sl:sl + n * 128]),
            wait=tok_l + f_store.get(u - 2, ()), inc=[("ve", 1)])
        f_cvt[u] = tok_c
        tok_s = ()
        stp = 8
        for o0 in range(0, n_out, stp):
            tok_s = tok_s + p.dma(
                wdst[o0:o0 + stp, :, ic * 128:(ic + 1) * 128]
                .rearrange("oc p f -> p oc f"),
                i_wcvt[:, sl + o0 * 128:sl + (o0 + stp) * 128]
                .rearrange("p (oc f) -> p oc f", f=128),
                sem=f"s3{u % 2}", wait=tok_c if o0 == 0 else ())
        f_store[u] = tok_s
    tok_wcache = f_store[len(units) - 1] + f_store[len(units) - 2]

    p.op("vector", lambda e: e.memset(a_kring[:], 0.0),
         wait=tok_wcache + tok_aw)
    p.op("vector", lambda e: e.memset(a_vring[:], 0.0))
    p.op("vector", lambda e: e.memset(
        a_vring[:].rearrange("p (s f) -> p s f", f=VSLOT)[:, :, 64:65], 1.0))
    p.op("vector", lambda e: e.memset(
        a_vring[:].rearrange("p (s f) -> p s f", f=VSLOT)[:, :, 129:130], 1.0))
    tok_ring = p.mark("vector", [("ve", 1)])

    def ln_stats(xF, xbT, sqT, wait_first, bf_in=False):
        s1 = sc4[:, 512:1024]
        s2 = sc4[:, 1024:1536]
        mm_toks = {}
        for dc in range(8):
            sl = (dc % 2) * NT
            w = wait_first if dc == 0 else ()
            w = w + mm_toks.get(dc - 2, ())
            if bf_in:
                tsq = p.op("gpsimd", lambda e, d=dc, s=sl: e.tensor_mul(
                    out=sqT[:, s:s + NT], in0=xF(d), in1=xF(d)),
                    wait=w, inc=[("gp", 1)])
                srcap = xF(dc)
                tdat = tsq
            else:
                tdat = p.op("vector", lambda e, d=dc, s=sl: e.tensor_copy(
                    out=xbT[:, s:s + NT], in_=xF(d)), wait=w, inc=[("ve", 1)])
                tsq = p.op("gpsimd", lambda e, d=dc, s=sl: e.tensor_mul(
                    out=sqT[:, s:s + NT], in0=xbT[:, s:s + NT],
                    in1=xbT[:, s:s + NT]), wait=tdat, inc=[("gp", 1)])
                srcap = xbT[:, sl:sl + NT]
            wmm = (wait_first + G["lnb"] + G["sc"]) if dc == 0 else ()
            p.op("tensor", lambda e, d=dc, s=srcap: e.matmul(
                out=s1[0:1, 0:NT], lhsT=t_onesb[:, 0:1], rhs=s,
                start=(d == 0), stop=(d == 7)), wait=wmm + tdat)
            tok2 = p.op("tensor", lambda e, d=dc, s=(dc % 2) * NT: e.matmul(
                out=s2[0:1, 0:NT], lhsT=t_onesb[:, 0:1], rhs=sqT[:, s:s + NT],
                start=(d == 0), stop=(d == 7)), wait=tsq, inc=[("mm", 1)])
            mm_toks[dc] = tok2
        return mm_toks[7]

    def ln_rows(tok_stats, rows=None, rkey="rows"):
        rows = t_rows if rows is None else rows
        s1 = sc4[:, 512:1024]
        s2 = sc4[:, 1024:1536]
        rmean = rows[0:1, 0:NT]
        rvar = rows[0:1, NT:2 * NT]
        rsd = rows[0:1, 2 * NT:3 * NT]
        rb_ = rows[0:1, 3 * NT:4 * NT]
        p.op("vector", lambda e: e.tensor_scalar_mul(
            out=rmean, in0=s1[0:1, 0:NT], scalar1=1.0 / DM),
            wait=tok_stats + G[rkey])
        p.op("vector", lambda e: e.tensor_mul(out=rvar, in0=rmean, in1=rmean))
        p.op("vector", lambda e: e.scalar_tensor_tensor(
            out=rvar, in0=s2[0:1, 0:NT], scalar=1.0 / DM, in1=rvar,
            op0=OP.mult, op1=OP.subtract))
        tok_v = p.mark("vector", [("ve", 1)])
        G["lnb"] = tok_v
        p.op("scalar", lambda e: e.activation(
            out=rsd, in_=rvar, func=AF.Ln, bias=f32(EPS)), wait=tok_v)
        tok_sd = p.op("scalar", lambda e: e.activation(
            out=rsd, in_=rsd, func=AF.Exp, scale=f32(-0.5)), inc=[("ac", 1)])
        p.op("vector", lambda e: e.scalar_tensor_tensor(
            out=rb_, in0=rmean, scalar=-1.0, in1=rsd, op0=OP.mult, op1=OP.mult),
            wait=tok_sd)
        return p.mark("vector", [("ve", 1)])

    def ln_norm(xF, xnO, abcT, bbcT, tmpT, tok_rows, norm_wait=(),
                rows=None, rkey="rows", neng="gpsimd", tmp_bank3=False,
                abanks=False):
        rows = t_rows if rows is None else rows
        rsd = rows[0:1, 2 * NT:3 * NT]
        rb_ = rows[0:1, 3 * NT:4 * NT]
        if abanks:
            bA = sc4[:, 0:512]
            bB = sc4[:, 512:1024]
            wA = tok_rows + G["sc"] + G["sc0"] + G["sc1"]
        else:
            bA = sc4[:, 1536:2048]
            bB = pvE[:, 0:NT]
            wA = tok_rows + G["pvE"]
        p.op("tensor", lambda e: e.matmul(
            out=bA, lhsT=t_onesr[0:1, :], rhs=rsd,
            start=True, stop=True), wait=wA)
        tok_bc = p.op("tensor", lambda e: e.matmul(
            out=bB, lhsT=t_onesr[0:1, :], rhs=rb_,
            start=True, stop=True), inc=[("mm", 1)])
        G[rkey] = tok_bc
        p.op("vector", lambda e, bA=bA: e.tensor_copy(out=abcT[:, 0:NT],
                                                      in_=bA),
             wait=tok_bc + G["abc"])
        tok_cp = p.op("vector", lambda e, bB=bB: e.tensor_copy(
            out=bbcT[:, 0:NT], in_=bB), inc=[("ve", 1)])
        if tmp_bank3:
            tmpT = sc4[:, 1536:2048]
        nsem = {"gpsimd": "gp", "vector": "ve"}[neng]
        for dc in range(8):
            p.op(neng, lambda e, d=dc: e.tensor_mul(
                out=tmpT[:, 0:NT], in0=xF(d), in1=abcT[:, 0:NT]),
                wait=(tok_cp + norm_wait) if dc == 0 else ())
            p.op(neng, lambda e, d=dc: e.tensor_add(
                out=xnO(d), in0=tmpT[:, 0:NT], in1=bbcT[:, 0:NT]))
        tok_n = p.mark(neng, [(nsem, 1)])
        G["abc"] = tok_n
        if abanks:
            G["sc0"] = tok_cp
            G["sc1"] = tok_cp
            G["lnb"] = G["lnb"] + tok_cp
        else:
            G["pvE"] = tok_n if tmp_bank3 else tok_cp
        return tok_n

    def layer_norm(xF, xnO, xbT, sqT, abcT, bbcT, tmpT, wait_first,
                   bf_in=False, norm_wait=()):
        tok_stats = ln_stats(xF, xbT, sqT, wait_first, bf_in=bf_in)
        tok_rows = ln_rows(tok_stats)
        tok_n = ln_norm(xF, xnO, abcT, bbcT, tmpT, tok_rows,
                        norm_wait=norm_wait)
        return tok_n, ()

    tok_attn_done = tok_ring
    tok_x1_dma = {}
    tok_qf_free = ()
    tok_qpf_free = ()
    tok_qb_free = ()
    tok_t2_free = ()
    tok_sum = ()
    a_pre = {}
    a_xn_free = ()
    a_qk_last = ()
    a_stt = {}

    def emit_SA(ti):
        t0 = ti * NT
        if ti == 0:
            wt = tok_aw + tok_ring
        else:
            prev = a_pre[ti - 1]
            wt = prev["norm"] + prev["stats"] + a_qk_last
        p.dma(a_cos[:], cosT[:, t0:t0 + NT], wait=wt)
        p.dma(a_sin[:], sinT[:, t0:t0 + NT])
        tok_x0 = p.dma(a_x0[:].rearrange("p (dc t) -> p dc t", dc=8),
                       x0T.rearrange("(dc p) t -> p dc t", p=128)
                       [:, :, t0:t0 + NT])

        def ax(d):
            return a_x0[:, d * NT:(d + 1) * NT]

        wf = tok_x0 + (tok_attn_done if ti == 0 else ())
        tok_stats = ln_stats(ax, a_xb, a_sq, wf)
        tok_rows = ln_rows(tok_stats)
        a_pre[ti] = {"ax": ax, "stats": tok_stats, "rows": tok_rows,
                     "norm": ()}

    emit_SA(0)
    for ti in range(na):
        t0 = ti * NT
        b0 = 4 * ti
        s0 = b0 % 8
        pre = a_pre[ti]
        ax = pre["ax"]
        tok_ln = ln_norm(ax, lambda d: a_xn[:, d * NT:(d + 1) * NT],
                         a_abc, a_bbc, a_tmp, pre["rows"],
                         norm_wait=a_xn_free, abanks=True)
        pre["norm"] = tok_ln
        tok_cs = pre["rows"]

        if asub < 2:
            continue
        pend_rope = None

        def emit_ropetail():
            nonlocal pend_rope, tok_qb_free, tok_qpf_free, tok_qf_free, \
                tok_t2_free, tok_sum
            if pend_rope is None:
                return
            bank2s, bname2, tok_ev, tok_cast, dst, wt_extra = pend_rope
            pend_rope = None
            tok_pm = p.op("tensor", lambda e, b2=bank2s: e.matmul(
                out=b2, lhsT=t_r128[:], rhs=a_qb[:],
                start=True, stop=True),
                wait=tok_cast + G[bname2], inc=[("mm", 1)])
            tok_qb_free = tok_pm
            tok_evp = p.op("scalar", lambda e, b2=bank2s: e.copy(
                out=a_qpf[:], in_=b2),
                wait=tok_pm + tok_qpf_free, inc=[("ac", 1)])
            G[bname2] = tok_evp
            tok_t1 = p.op("vector", lambda e: e.tensor_mul(
                out=a_t1[:], in0=a_qf[:], in1=a_cos[:]),
                wait=tok_ev + tok_cs, inc=[("ve", 1)])
            tok_qf_free = tok_t1
            tok_t2 = p.op("gpsimd", lambda e: e.tensor_mul(
                out=a_t2[:], in0=a_qpf[:], in1=a_sin[:]),
                wait=tok_evp + tok_t2_free, inc=[("gp", 1)])
            tok_qpf_free = tok_t2
            tok_sum = p.op("vector", lambda e, d=dst: e.tensor_add(
                out=d, in0=a_t1[:], in1=a_t2[:]),
                wait=tok_t1 + tok_t2 + wt_extra, inc=[("ve", 1)])
            tok_t2_free = tok_sum

        for which in range(2):
            for oc in range(8):
                bank, bname = (pAb, "pA") if oc % 2 == 0 else (pBb, "pB")
                bank2s = sc4[:, (oc % 2) * 512:(oc % 2) * 512 + 512]
                bname2 = "sc0" if oc % 2 == 0 else "sc1"
                for dc in range(8):
                    wt = ()
                    if dc == 0:
                        wt = G[bname]
                        if oc == 0 and which == 0:
                            wt = wt + tok_ln
                    p.op("tensor", lambda e, d=dc, o_=oc, b=bank, w=which * 8192:
                         e.matmul(out=b[:, 0:NT],
                                  lhsT=a_w[:, w + d * 1024 + o_ * 128:
                                           w + d * 1024 + o_ * 128 + 128],
                                  rhs=a_xn[:, d * NT:(d + 1) * NT],
                                  start=(d == 0), stop=(d == 7)), wait=wt)
                tok_mm = p.mark("tensor", [("mm", 1)])
                emit_ropetail()
                bcol = (0 if which == 0 else 8) + oc
                tok_ev = p.op("scalar", lambda e, b=bank, bc=bcol: e.activation(
                    out=a_qf[:], in_=b[:, 0:NT], func=AF.Identity,
                    bias=t_cols[:, bc:bc + 1], scale=f32(1.0)),
                    wait=tok_mm + tok_qf_free, inc=[("ac", 1)])
                tok_cast = p.op("vector", lambda e: e.tensor_copy(
                    out=a_qb[:], in_=a_qf[:]),
                    wait=tok_ev + tok_qb_free, inc=[("ve", 1)])
                G[bname] = tok_cast
                if which == 0:
                    dst = a_qr[:, oc * NT:(oc + 1) * NT]
                else:
                    dst = a_kring[:, oc * 1024 + s0 * 128:
                                  oc * 1024 + s0 * 128 + 512]
                wt_extra = tok_attn_done if which == 1 and oc == 0 else ()
                pend_rope = (bank2s, bname2, tok_ev, tok_cast, dst, wt_extra)
        emit_ropetail()
        tok_qk = tok_sum
        a_qk_last = tok_qk

        if asub < 3:
            continue
        for tb in range(4):
            slot = (s0 + tb) % 8
            for fh in range(2):
                bank, bname = (pAb, "pA") if fh == 0 else (pBb, "pB")
                for dc in range(8):
                    wt = G[bname] if dc == 0 else ()
                    if dc == 0 and tb == 0 and fh == 0:
                        wt = wt + tok_attn_done
                    p.op("tensor", lambda e, d=dc, t=tb, f=fh, b=bank: e.matmul(
                        out=b[:, 0:512],
                        lhsT=a_xn[:, d * NT + t * 128: d * NT + t * 128 + 128],
                        rhs=a_w[:, 2 * 8192 + d * 1024 + f * 512:
                                 2 * 8192 + d * 1024 + f * 512 + 512],
                        start=(d == 0), stop=(d == 7)), wait=wt)
                tok_mm = p.mark("tensor", [("mm", 1)])
                tok_vv = ()
                for hh in range(8):
                    h = 8 * fh + hh
                    hp = h // 2
                    base = (slot * 8 + hp) * VSLOT
                    off = base + (0 if h % 2 == 0 else 65)
                    wt2 = tok_mm if hh == 0 else ()
                    tok_vv = p.op("vector", lambda e, b=bank, hh=hh, o_=off, f=fh:
                                  e.tensor_add(
                                      out=a_vring[:, o_:o_ + 64],
                                      in0=b[:, hh * 64:(hh + 1) * 64],
                                      in1=t_bvbc[:, f * 512 + hh * 64:
                                                 f * 512 + hh * 64 + 64]),
                                  wait=wt2, inc=[("ve", 1)] if hh == 7 else ())
                G[bname] = tok_vv
        tok_vdone = tok_vv
        a_xn_free = tok_mm
        if ti + 1 < na:
            emit_SA(ti + 1)

        if asub < 4:
            continue
        mask_t = t_maskA0 if ti == 0 else t_maskA
        tok_fin = tok_attn_done
        tok_qkm_h = {}
        tok_exp_h = {}
        tok_msk_h = {}
        tok_pvm_h = {}

        def a_qk_eo(hp, eo, wt0):
            cnt = 0
            for j in range(4):
                for ch in range(2):
                    slot = (b0 + j - 1 + ch) % 8
                    kb = hp * 1024 + slot * 128
                    oo = eo * 1024 + j * 256 + ch * 128
                    wt = wt0 if cnt == 0 else ()
                    cnt += 1
                    p.op("tensor", lambda e, kb=kb, oo=oo, eo=eo, hp=hp, j=j:
                         e.matmul(out=sc4[:, oo:oo + 128],
                                  lhsT=a_kring[eo * 64:eo * 64 + 64,
                                               kb:kb + 128],
                                  rhs=a_qr[eo * 64:eo * 64 + 64,
                                           hp * NT + j * 128:
                                           hp * NT + j * 128 + 128],
                                  start=True, stop=True), wait=wt)
            return p.mark("tensor", [("mm", 1)])

        def a_pv(hp):
            if asub < 5:
                return ()
            pb = (hp % 2) * 2048
            first = True
            for j in range(4):
                for ch in range(2):
                    slot = (b0 + j - 1 + ch) % 8
                    vb = (slot * 8 + hp) * VSLOT
                    wt = ()
                    if first:
                        wt = tok_msk_h[hp] + G["pvE"] + G["pvO"] + tok_fin
                        first = False
                    pe = a_pm[:, pb + j * 256 + ch * 128:
                              pb + j * 256 + ch * 128 + 128]
                    po = a_pm[:, pb + 1024 + j * 256 + ch * 128:
                              pb + 1024 + j * 256 + ch * 128 + 128]
                    p.op("tensor", lambda e, vb=vb, j=j, ch=ch, pe=pe: e.matmul(
                        out=pvE[0:65, j * 128:(j + 1) * 128],
                        lhsT=a_vring[:, vb:vb + 65],
                        rhs=pe, start=(ch == 0), stop=(ch == 1)), wait=wt)
                    p.op("tensor", lambda e, vb=vb, j=j, ch=ch, po=po: e.matmul(
                        out=pvO[0:65, j * 128:(j + 1) * 128],
                        lhsT=a_vring[:, vb + 65:vb + 130],
                        rhs=po, start=(ch == 0), stop=(ch == 1)))
            return p.mark("tensor", [("mm", 1)])

        def a_fin(hp):
            tok_pvm = tok_pvm_h[hp]
            p.op("vector", lambda e: e.tensor_copy(
                out=t_rows[0:1, 0:512], in_=pvE[64:65, :]), wait=tok_pvm)
            tok_r = p.op("vector", lambda e: e.tensor_copy(
                out=t_rows[0:1, 512:1024], in_=pvO[64:65, :]), inc=[("ve", 1)])
            tok_ob = p.op("vector", lambda e: e.tensor_copy(
                out=a_qb[0:64, :], in_=pvO[0:64, :]), inc=[("ve", 1)])
            p.op("tensor", lambda e: e.matmul(
                out=pAb[0:64, :], lhsT=t_onesr[0:1, 0:64],
                rhs=t_rows[0:1, 0:512], start=True, stop=True),
                wait=tok_r + G["pA"])
            tok_bcm = p.op("tensor", lambda e: e.matmul(
                out=pAb[64:128, :], lhsT=t_onesr[0:1, 0:64],
                rhs=t_rows[0:1, 512:1024], start=True, stop=True),
                inc=[("mm", 1)])
            tok_sh = p.op("tensor", lambda e: e.matmul(
                out=pBb[:, :], lhsT=t_shift[0:64, :], rhs=a_qb[0:64, :],
                start=True, stop=True),
                wait=tok_ob + G["pB"], inc=[("mm", 1)])
            p.op("scalar", lambda e: e.activation(
                out=a_lnx[:], in_=pAb[:, :], func=AF.Ln), wait=tok_bcm)
            tok_rbc = p.op("scalar", lambda e: e.activation(
                out=a_rb[:], in_=a_lnx[:], func=AF.Exp, scale=f32(-1.0)),
                inc=[("ac", 1)])
            G["pA"] = tok_rbc
            p.op("vector", lambda e, hp=hp: e.tensor_mul(
                out=a_oa[0:64, hp * NT:(hp + 1) * NT],
                in0=pvE[0:64, :], in1=a_rb[0:64, :]), wait=tok_rbc)
            tok_f = p.op("vector", lambda e, hp=hp: e.tensor_mul(
                out=a_oa[64:128, hp * NT:(hp + 1) * NT],
                in0=pBb[64:128, :], in1=a_rb[64:128, :]),
                wait=tok_sh, inc=[("ve", 1)])
            G["pvE"] = tok_f
            G["pvO"] = tok_ob
            G["pB"] = tok_f
            return tok_f

        tok_exp = {}
        tok_msk = {}
        for hp in range(8):
            for eo in range(2):
                wt0 = tok_exp.get((hp - 1, eo), ())
                if hp == 0:
                    wt0 = wt0 + tok_qk + tok_vdone + G["sc"] + G["lnb"]
                tok_qkm = a_qk_eo(hp, eo, wt0)
                tok_exp[(hp, eo)] = p.op("scalar", lambda e, eo=eo: e.activation(
                    out=a_p[:, eo * 1024:(eo + 1) * 1024],
                    in_=sc4[:, eo * 1024:(eo + 1) * 1024], func=AF.Exp,
                    scale=f32(0.125), bias=f32(-C_A)),
                    wait=tok_qkm + tok_msk.get((hp - 1, eo), ()),
                    inc=[("ac", 1)])
                tok_msk[(hp, eo)] = p.op(
                    "gpsimd", lambda e, m=mask_t, hp=hp, eo=eo: e.tensor_mul(
                        out=a_pm[:, (hp % 2) * 2048 + eo * 1024:
                                 (hp % 2) * 2048 + eo * 1024 + 1024],
                        in0=a_p[:, eo * 1024:(eo + 1) * 1024],
                        in1=m[:, eo * 1024:(eo + 1) * 1024]),
                    wait=tok_exp[(hp, eo)] + tok_pvm_h.get(hp - 2, ()),
                    inc=[("gp", 1)])
            tok_msk_h[hp] = tok_msk[(hp, 0)] + tok_msk[(hp, 1)]
            if hp >= 1 and asub >= 5:
                tok_pvm_h[hp - 1] = a_pv(hp - 1)
                if asub >= 6:
                    tok_fin = a_fin(hp - 1)
        G["sc"] = tok_exp[(7, 0)] + tok_exp[(7, 1)]
        if asub >= 5:
            tok_pvm_h[7] = a_pv(7)
            if asub >= 6:
                tok_fin = a_fin(7)
        if asub < 7:
            continue
        tok_x1_last = ()
        for oc in range(8):
            bank, bname = (pAb, "pA") if oc % 2 == 0 else (pBb, "pB")
            for dc in range(8):
                wt = ()
                if dc == 0:
                    wt = G[bname]
                    if oc == 0:
                        wt = wt + tok_fin
                p.op("tensor", lambda e, d=dc, o_=oc, b=bank: e.matmul(
                    out=b[:, 0:NT],
                    lhsT=a_w[:, 3 * 8192 + d * 1024 + o_ * 128:
                             3 * 8192 + d * 1024 + o_ * 128 + 128],
                    rhs=a_oa[:, d * NT:(d + 1) * NT],
                    start=(d == 0), stop=(d == 7)), wait=wt)
            tok_mm = p.mark("tensor", [("mm", 1)])
            sl = oc % 2
            xrt = a_xr0 if sl == 0 else a_qpf
            wxr = a_stt.get(sl, ())
            if oc < 2:
                wxr = wxr + tok_qk + tok_t2_free
            tok_xr = p.dma(xrt[:], x0T[oc * 128:(oc + 1) * 128, t0:t0 + NT],
                           sem=f"xr{sl}", wait=wxr)
            tok_x1 = p.op("vector", lambda e, b=bank, o_=oc, x=xrt: e.scalar_tensor_tensor(
                out=a_x1[:, (o_ % 2) * NT:(o_ % 2) * NT + NT],
                in0=b[:, 0:NT], scalar=t_cols[:, 16 + o_:17 + o_],
                in1=x[:], op0=OP.add, op1=OP.add),
                wait=tok_mm + tok_x1_dma.get(oc % 2, ()) + tok_xr,
                inc=[("ve", 1)])
            a_stt[sl] = tok_x1
            G[bname] = tok_x1
            tok_x1_dma[oc % 2] = p.dma(x1d[oc * 128:(oc + 1) * 128, t0:t0 + NT],
                                       a_x1[:, (oc % 2) * NT:(oc % 2) * NT + NT],
                                       sem=f"x1w{oc % 2}", wait=tok_x1,
                                       eng="gpsimd")
            tok_x1_last = tok_x1
        tok_qpf_free = tok_qpf_free + a_stt[1]
        tok_attn_done = tok_fin + tok_x1_last

    tok_x1_all = (("x1w0", p.cnt["x1w0"]), ("x1w1", p.cnt["x1w1"]))

    if "B" not in phases:
        _finish(nc, p)
        return nc

    gate = tok_attn_done
    for wi, wsrc in enumerate((wbk, wbv, wbq)):
        tok_l = ()
        for h in range(2):
            tok_l = tok_l + p.dma(
                b_fstage[:, h * 4096:(h + 1) * 4096]
                .rearrange("p (dc f) -> p dc f", dc=4),
                wsrc.rearrange("(dc p) f -> p dc f", p=128)
                [:, h * 4:h * 4 + 4, :],
                sem="wv", wait=gate if h == 0 else ())
        gate = p.op("vector", lambda e, wi=wi: e.tensor_copy(
            out=b_w[:, wi * 8192:(wi + 1) * 8192], in_=b_fstage[:]),
            wait=tok_l, inc=[("ve", 1)])
    tok_bw = gate

    Bst = {"hdone": (), "x2t": (), "gb": (), "x2done_mm": (), "xn3f": (),
           "sb_stats": ()}
    tok_x1_free = ()
    tok_x2_dma = {}
    wgu_free = {}
    wd_free = {}
    sg_free = {}
    tok_kvbuf = {}
    x1buf_free = {}
    xn_free = {}
    b_pre = {}

    sb_dat = {}

    sb_load = {}

    def emit_SB_load(ti):
        t0 = ti * NT
        xb1 = (ti % 2) * 8 * NT
        wt0 = (tok_bw + tok_x1_all) if ti == 0 else x1buf_free.get(ti % 2, ())
        sb_load[ti] = p.dma(b_x1[:, xb1:xb1 + 8 * NT]
                            .rearrange("p (dc t) -> p dc t", dc=8),
                            x1d.rearrange("(dc p) t -> p dc t", p=128)
                            [:, :, t0:t0 + NT], wait=wt0)

    def emit_SB_a(ti):
        xb1 = (ti % 2) * 8 * NT
        tok_x1 = sb_load.pop(ti)

        def bx(d, xb1=xb1):
            return b_x1[:, xb1 + d * NT:xb1 + (d + 1) * NT]

        tok_stats = ln_stats(bx, b_xb, b_sq,
                             tok_x1 + Bst.get("ln3_stats", ()))
        sb_dat[ti] = (bx, tok_stats)
        Bst["sb_stats"] = tok_stats

    def emit_SB_r(ti):
        bx, tok_stats = sb_dat[ti]
        sb_dat[ti] = (bx, ln_rows(tok_stats, rows=b_rows2, rkey="rows2"))

    def emit_SB_b(ti):
        bx, tok_rows = sb_dat.pop(ti)
        xo = (ti % 2) * 8 * NT
        tok_n = ln_norm(bx, lambda d, xo=xo: b_xn[:, xo + d * NT:xo + (d + 1) * NT],
                        b_abc, b_bbc, b_tmp, tok_rows,
                        norm_wait=xn_free.get(ti % 2, ()),
                        rows=b_rows2, rkey="rows2",
                        neng="vector", tmp_bank3=True)
        b_pre[ti] = (bx, tok_n)

    def emit_gu(ti):
        xn1 = (ti % 2) * 8 * NT
        bx, tok_ln2 = b_pre[ti]
        for hc in range(24):
            buf = hc % 2
            wtw = wgu_free.get(buf, ())
            if ti == 0 and hc < 2:
                wtw = wtw + tok_wcache
            tok_wl = p.dma(b_wg[:, buf * 1024:(buf + 1) * 1024],
                           wgbf[hc], sem=f"wvg{buf}", wait=wtw)
            tok_wl = tok_wl + p.dma(b_wu[:, buf * 1024:(buf + 1) * 1024],
                                    wubf[hc], sem=f"wvg{buf}")
            gb, gbn = (pAb[:, 0:NT], "pA") if buf == 0 else (sc4[:, 0:NT], "sc0")
            ub, ubn = (pBb[:, 0:NT], "pB") if buf == 0 else (pvO[:, 0:NT], "pvO")
            for dc in range(8):
                wt = (tok_ln2 + tok_wl + G[gbn]) if dc == 0 else ()
                p.op("tensor", lambda e, d=dc, b=buf, gb=gb, xn1=xn1: e.matmul(
                    out=gb,
                    lhsT=b_wg[:, b * 1024 + d * 128: b * 1024 + d * 128 + 128],
                    rhs=b_xn[:, xn1 + d * NT:xn1 + (d + 1) * NT],
                    start=(d == 0), stop=(d == 7)), wait=wt)
            for dc in range(8):
                wt = G[ubn] if dc == 0 else ()
                p.op("tensor", lambda e, d=dc, b=buf, ub=ub, xn1=xn1: e.matmul(
                    out=ub,
                    lhsT=b_wu[:, b * 1024 + d * 128: b * 1024 + d * 128 + 128],
                    rhs=b_xn[:, xn1 + d * NT:xn1 + (d + 1) * NT],
                    start=(d == 0), stop=(d == 7)), wait=wt)
            tok_gu = p.mark("tensor", [("mm", 1)])
            wgu_free[buf] = tok_gu
            if hc == 23:
                xn_free[ti % 2] = tok_gu
            sgs = b_sg[:, (hc % 2) * NT:(hc % 2) * NT + NT]
            tok_sg = p.op("scalar", lambda e, gb=gb, s=sgs: e.activation(
                out=s, in_=gb, func=AF.Silu, scale=f32(DSC)),
                wait=tok_gu + sg_free.get(hc % 2, ()), inc=[("ac", 1)])
            tok_h = p.op("vector", lambda e, h=hc, ub=ub, s=sgs: e.tensor_mul(
                out=b_h[:, h * NT:(h + 1) * NT], in0=ub, in1=s),
                wait=tok_sg + (Bst["x2done_mm"] if hc == 0 else ()),
                inc=[("ve", 1)])
            sg_free[hc % 2] = tok_h
            G[gbn] = tok_sg
            G[ubn] = tok_h
            if hc == 1 and ti >= 1:
                emit_ln3_stats(ti - 1)
            if hc == 2 and ti >= 1:
                emit_ln3_rows(ti - 1)
            if hc == 3 and ti + 1 < nb:
                emit_SB_load(ti + 1)
            if hc == 4 and ti >= 1:
                emit_ln3_mid(ti - 1)
            if hc == 9 and ti + 1 < nb:
                emit_SB_a(ti + 1)
            if hc == 10 and ti + 1 < nb:
                emit_SB_r(ti + 1)
            if hc == 12 and ti + 1 < nb:
                emit_SB_b(ti + 1)
            if hc == 13 and ti >= 1:
                emit_ln3b_stats(ti - 1)
            if hc == 14 and ti >= 1:
                emit_ln3b_rows(ti - 1)
            if hc == 16 and ti >= 1:
                emit_ln3b_fin(ti - 1)
        tok_hdone = tok_h

        Bst["hdone"] = tok_hdone

    def emit_down(ti):
        t0 = ti * NT
        xb1 = (ti % 2) * 8 * NT
        tok_hdone = Bst["hdone"]
        for oc in range(8):
            buf = oc % 2
            tok_wd = p.dma(b_wd[:, buf * 3072:buf * 3072 + 1536],
                           wdbf[oc][:, 0:1536], sem=f"wvd{buf}",
                           wait=wd_free.get(buf, ()))
            tok_wd = tok_wd + p.dma(b_wd[:, buf * 3072 + 1536:(buf + 1) * 3072],
                                    wdbf[oc][:, 1536:3072], sem=f"wvd{buf}")
            bank, bname = (pAb, "pA") if oc % 2 == 0 else (pBb, "pB")
            for hc in range(24):
                wt = (tok_hdone + tok_wd + G[bname]) if hc == 0 else ()
                p.op("tensor", lambda e, h=hc, b=bank, bu=buf: e.matmul(
                    out=b[:, 0:NT],
                    lhsT=b_wd[:, bu * 3072 + h * 128: bu * 3072 + h * 128 + 128],
                    rhs=b_h[:, h * NT:(h + 1) * NT],
                    start=(h == 0), stop=(h == 23)), wait=wt)
            tok_mm = p.mark("tensor", [("mm", 1)])
            tok_x2 = p.op("vector", lambda e, b=bank, o_=oc, xb1=xb1: e.scalar_tensor_tensor(
                out=b_x2[:, o_ * NT:(o_ + 1) * NT], in0=b[:, 0:NT],
                scalar=DSC * DSC, in1=b_x1[:, xb1 + o_ * NT:xb1 + (o_ + 1) * NT],
                op0=OP.mult, op1=OP.add),
                wait=tok_mm + tok_x2_dma.get(oc, ()), inc=[("ve", 1)])
            G[bname] = tok_x2
            wd_free[buf] = tok_x2
            x1buf_free[ti % 2] = tok_x2
            tok_x2_dma[oc] = p.dma(x2d[oc * 128:(oc + 1) * 128, t0:t0 + NT],
                                   b_x2[:, oc * NT:(oc + 1) * NT],
                                   sem=f"x2w{oc}", wait=tok_x2, eng="gpsimd")
        tok_x2t = tok_x2
        tok_x1_free = tok_x2t

        Bst["x2t"] = tok_x2t
        Bst["x2done_mm"] = tok_mm

    def b2f(d):
        return b_x2[:, d * NT:(d + 1) * NT]

    def b3f(d):
        return b_xn3[:, d * NT:(d + 1) * NT]

    def emit_ln3_stats(ti):
        Bst["ln3_stats"] = ln_stats(b2f, b_xb, b_sq,
                                    Bst["x2t"] + Bst["sb_stats"])

    def emit_ln3_rows(ti):
        Bst["ln3_rows"] = ln_rows(Bst["ln3_stats"])

    def emit_ln3_mid(ti):
        Bst["ln3"] = ln_norm(b2f, lambda d: b_xn3[:, d * NT:(d + 1) * NT],
                             b_abc, b_bbc, b_tmp, Bst["ln3_rows"],
                             norm_wait=Bst["xn3f"])

    def emit_ln3b_stats(ti):
        Bst["ln3b_stats"] = ln_stats(b3f, b_xb, b_sq,
                                     Bst["ln3"] + Bst["sb_stats"], bf_in=True)

    def emit_ln3b_rows(ti):
        Bst["ln3b_rows"] = ln_rows(Bst["ln3b_stats"])

    def emit_ln3b_fin(ti):
        tok_ln3b = ln_norm(b3f, lambda d: b_xn3b[:, d * NT:(d + 1) * NT],
                           b_abc, b_bbc, b_tmp, Bst["ln3b_rows"],
                           norm_wait=Bst["xn3f"],
                           neng="vector", tmp_bank3=True)
        for dc in range(8):
            wt = tok_ln3b if dc == 0 else ()
            p.op("scalar", lambda e, d=dc: e.activation(
                out=b_xn3b[:, d * NT:(d + 1) * NT],
                in_=b_xn3b[:, d * NT:(d + 1) * NT], func=AF.Identity,
                bias=t_cols[:, 32 + d:33 + d], scale=t_cols[:, 24 + d:25 + d]),
                wait=wt)
        Bst["gb"] = p.mark("scalar", [("ac", 1)])

    def emit_kv(ti):
        t0 = ti * NT
        tok_gb = Bst["gb"]
        for wi, (wofs, dstd) in enumerate(((0, kd), (2 * 8192, qd))):
            for oc in range(8):
                bank, bname = (pAb, "pA") if oc % 2 == 0 else (pBb, "pB")
                for dc in range(8):
                    wt = G[bname] if dc == 0 else ()
                    if dc == 0 and oc == 0 and wi == 0:
                        wt = wt + tok_gb
                    p.op("tensor", lambda e, d=dc, o_=oc, b=bank, w=wofs: e.matmul(
                        out=b[:, 0:NT],
                        lhsT=b_w[:, w + d * 1024 + o_ * 128:
                                 w + d * 1024 + o_ * 128 + 128],
                        rhs=b_xn3b[:, d * NT:(d + 1) * NT],
                        start=(d == 0), stop=(d == 7)), wait=wt)
                tok_mm = p.mark("tensor", [("mm", 1)])
                sl = (wi * 8 + oc) % 4
                tok_ev = p.op("vector", lambda e, b=bank, s=sl: e.tensor_copy(
                    out=b_kv[:, s * NT:(s + 1) * NT], in_=b[:, 0:NT]),
                    wait=tok_mm + tok_kvbuf.get(sl, ()), inc=[("ve", 1)])
                G[bname] = tok_ev
                tok_kvbuf[sl] = p.dma(dstd[oc * 128:(oc + 1) * 128, t0:t0 + NT],
                                      b_kv[:, sl * NT:(sl + 1) * NT],
                                      sem=f"kvw{sl}", wait=tok_ev, eng="gpsimd")
        for tb in range(4):
            for fh in range(2):
                bank, bname = (pAb, "pA") if fh == 0 else (pBb, "pB")
                for dc in range(8):
                    wt = G[bname] if dc == 0 else ()
                    p.op("tensor", lambda e, d=dc, t=tb, f=fh, b=bank: e.matmul(
                        out=b[:, 0:512],
                        lhsT=b_xn3b[:, d * NT + t * 128: d * NT + t * 128 + 128],
                        rhs=b_w[:, 8192 + d * 1024 + f * 512:
                                 8192 + d * 1024 + f * 512 + 512],
                        start=(d == 0), stop=(d == 7)), wait=wt)
                tok_mm = p.mark("tensor", [("mm", 1)])
                sl = (tb * 2 + fh) % 4
                tok_ev = p.op("vector", lambda e, b=bank, s=sl: e.tensor_copy(
                    out=b_kv[:, s * NT:s * NT + 512], in_=b[:, 0:512]),
                    wait=tok_mm + tok_kvbuf.get(sl, ()), inc=[("ve", 1)])
                G[bname] = tok_ev
                tok_kvbuf[sl] = p.dma(vd[t0 + tb * 128:t0 + tb * 128 + 128,
                                         fh * 512:(fh + 1) * 512],
                                      b_kv[:, sl * NT:sl * NT + 512],
                                      sem=f"kvw{sl}", wait=tok_ev, eng="gpsimd")
                if tb == 3 and fh == 1:
                    Bst["xn3f"] = tok_mm


    emit_SB_load(0)
    emit_SB_a(0)
    emit_SB_r(0)
    emit_SB_b(0)
    emit_gu(0)
    for ti in range(nb):
        emit_down(ti)
        if ti + 1 < nb:
            emit_gu(ti + 1)
        else:
            emit_ln3_stats(ti)
            emit_ln3_rows(ti)
            emit_ln3_mid(ti)
            emit_ln3b_stats(ti)
            emit_ln3b_rows(ti)
            emit_ln3b_fin(ti)
        emit_kv(ti)

    tok_kv_all = tuple((f"kvw{i}", p.cnt[f"kvw{i}"]) for i in range(4))
    tok_x2_all = tuple((f"x2w{i}", p.cnt[f"x2w{i}"]) for i in range(8))
    tok_b_done = G["pA"] + G["pB"]

    if "C" not in phases:
        _finish(nc, p)
        return nc

    tok_l = ()
    for h in range(2):
        tok_l = tok_l + p.dma(
            c_fstage[:, h * 4096:(h + 1) * 4096]
            .rearrange("p (dc f) -> p dc f", dc=4),
            wbo.rearrange("(dc p) f -> p dc f", p=128)[:, h * 4:h * 4 + 4, :],
            sem="wv", wait=tok_b_done if h == 0 else ())
    tok_wo = p.op("vector", lambda e: e.tensor_copy(out=c_wo[:], in_=c_fstage[:]),
                  wait=tok_l, inc=[("ve", 1)])

    tok_qsel = ()
    for qt in range(8):
        q0 = qt * 512
        wt = (tok_kv_all + tok_wo) if qt == 0 else tok_qsel
        tok_a = p.dma(c_st0[:].rearrange("p (dc t) -> p dc t", dc=8),
                      qd.rearrange("(dc p) t -> p dc t", p=128)
                      [:, :, q0:q0 + 512], wait=wt)
        tok_b2 = p.dma(c_st1[:].rearrange("p (dc t) -> p dc t", dc=8),
                       qd.rearrange("(dc p) t -> p dc t", p=128)
                       [:, :, HALF + q0:HALF + q0 + 512])
        for dc in range(8):
            wt2 = (tok_a + tok_b2 + tok_cfl) if dc == 0 else ()
            p.op("vector", lambda e, d=dc: e.tensor_scalar_mul(
                out=c_st0[:, d * 512:(d + 1) * 512],
                in0=c_st0[:, d * 512:(d + 1) * 512],
                scalar1=t_flags[:, 0:1]), wait=wt2)
            p.op("vector", lambda e, d=dc, q=q0: e.scalar_tensor_tensor(
                out=c_q[:, d * HALF + q: d * HALF + q + 512],
                in0=c_st1[:, d * 512:(d + 1) * 512],
                scalar=t_flags[:, 1:2],
                in1=c_st0[:, d * 512:(d + 1) * 512],
                op0=OP.mult, op1=OP.add))
        tok_qsel = p.mark("vector", [("ve", 1)])

    G["sc"] = G["sc"] + G["sc0"] + G["sc1"]
    tok_vones = p.op("vector", lambda e: e.memset(
        c_ve[:].rearrange("p (c f) -> p c f", f=65)[:, :, 64:65], 1.0),
        wait=tok_qsel)
    tok_vones = p.op("vector", lambda e: e.memset(
        c_vo[:].rearrange("p (c f) -> p c f", f=65)[:, :, 64:65], 1.0),
        inc=[("ve", 1)])
    tok_kvld = ()
    tok_fin = ()
    tok_pv_last = {}
    for hp in range(8):
        wt_ld = (tok_kv_all + tok_vones) if hp == 0 else tok_kvld2
        tok_k = p.dma(c_k[:], kd[hp * 128:(hp + 1) * 128, :], wait=wt_ld)
        tok_v1 = ()
        for c0 in range(0, 64, 8):
            tok_v1 = tok_v1 + p.dma(
                c_ve[:].rearrange("p (c f) -> p c f", f=65)[:, c0:c0 + 8, 0:64],
                vd.rearrange("(c p) f -> p c f", p=128)
                [:, c0:c0 + 8, (2 * hp) * 64:(2 * hp) * 64 + 64])
            tok_v1 = tok_v1 + p.dma(
                c_vo[:].rearrange("p (c f) -> p c f", f=65)[:, c0:c0 + 8, 0:64],
                vd.rearrange("(c p) f -> p c f", p=128)
                [:, c0:c0 + 8, (2 * hp + 1) * 64:(2 * hp + 1) * 64 + 64])
        tok_kvld = tok_k + tok_v1
        tok_kvld2 = ()
        for qt in range(8):
            q0 = qt * 512
            tok_exp_g = {}
            tok_pv_last = {}

            def emit_qk(g, wt0):
                for eo in range(2):
                    p.op("tensor", lambda e, g=g, eo=eo, hp=hp, q=q0: e.matmul(
                        out=sc4[:, ((g % 2) * 2 + eo) * 512:
                                ((g % 2) * 2 + eo) * 512 + 512],
                        lhsT=c_k[eo * 64:eo * 64 + 64, g * 128:(g + 1) * 128],
                        rhs=c_q[eo * 64:eo * 64 + 64,
                                hp * HALF + q: hp * HALF + q + 512],
                        start=True, stop=True), wait=wt0 if eo == 0 else ())
                return p.mark("tensor", [("mm", 1)])

            def emit_pv_e(g, extra):
                pe = c_p[:, (g % 2) * 1024:(g % 2) * 1024 + 512]
                p.op("tensor", lambda e, g=g, pe=pe: e.matmul(
                    out=pvE[0:65, :], lhsT=c_ve[:, g * 65:g * 65 + 65],
                    rhs=pe, start=(g == 0), stop=(g == 63)),
                    wait=tok_exp_g[g] + extra)

            def emit_pv_o(g):
                po = c_p[:, (g % 2) * 1024 + 512:(g % 2) * 1024 + 1024]
                p.op("tensor", lambda e, g=g, po=po: e.matmul(
                    out=pvO[0:65, :], lhsT=c_vo[:, g * 65:g * 65 + 65],
                    rhs=po, start=(g == 0), stop=(g == 63)))
                return p.mark("tensor", [("mm", 1)])

            for g in range(64):
                wt0 = tok_exp_g.get(g - 2, ())
                if g <= 1:
                    wt0 = wt0 + tok_kvld + G["sc"]
                tok_qkm = emit_qk(g, wt0)
                if g >= 1:
                    extra = ()
                    if g == 1:
                        extra = tok_fin + G["pvE"] + G["pvO"]
                    emit_pv_e(g - 1, extra)
                    tok_pv_last[g - 1] = emit_pv_o(g - 1)
                tok_exp_g[g] = p.op("scalar", lambda e, g=g: e.activation(
                    out=c_p[:, (g % 2) * 1024:(g % 2) * 1024 + 1024],
                    in_=sc4[:, (g % 2) * 1024:(g % 2) * 1024 + 1024],
                    func=AF.Exp, scale=f32(0.125), bias=f32(-C_B)),
                    wait=tok_qkm + tok_pv_last.get(g - 2, ()), inc=[("ac", 1)])
            emit_pv_e(63, ())
            tok_pv_last[63] = emit_pv_o(63)
            G["sc"] = tok_exp_g[63]
            tok_pvm = tok_pv_last[63]
            p.op("vector", lambda e: e.tensor_copy(
                out=t_rows[0:1, 0:512], in_=pvE[64:65, :]), wait=tok_pvm)
            tok_lr = p.op("vector", lambda e: e.tensor_copy(
                out=t_rows[0:1, 512:1024], in_=pvO[64:65, :]), inc=[("ve", 1)])
            tok_ob = p.op("vector", lambda e: e.tensor_copy(
                out=c_ob[0:64, :], in_=pvO[0:64, :]), inc=[("ve", 1)])
            p.op("tensor", lambda e: e.matmul(
                out=pAb[0:64, :], lhsT=t_onesr[0:1, 0:64],
                rhs=t_rows[0:1, 0:512], start=True, stop=True),
                wait=tok_lr + G["pA"])
            tok_bcm = p.op("tensor", lambda e: e.matmul(
                out=pAb[64:128, :], lhsT=t_onesr[0:1, 0:64],
                rhs=t_rows[0:1, 512:1024], start=True, stop=True),
                inc=[("mm", 1)])
            tok_sh = p.op("tensor", lambda e: e.matmul(
                out=pBb[:, :], lhsT=t_shift[0:64, :], rhs=c_ob[0:64, :],
                start=True, stop=True),
                wait=tok_ob + G["pB"], inc=[("mm", 1)])
            p.op("scalar", lambda e: e.activation(
                out=c_lnx[:], in_=pAb[:, :], func=AF.Ln), wait=tok_bcm)
            tok_rb = p.op("scalar", lambda e: e.activation(
                out=c_rb[:], in_=c_lnx[:], func=AF.Exp, scale=f32(-1.0)),
                inc=[("ac", 1)])
            G["pA"] = tok_rb
            p.op("vector", lambda e, hp=hp, q=q0: e.tensor_mul(
                out=c_o[0:64, hp * HALF + q: hp * HALF + q + 512],
                in0=pvE[0:64, :], in1=c_rb[0:64, :]), wait=tok_rb)
            tok_fin = p.op("vector", lambda e, hp=hp, q=q0: e.tensor_mul(
                out=c_o[64:128, hp * HALF + q: hp * HALF + q + 512],
                in0=pBb[64:128, :], in1=c_rb[64:128, :]),
                wait=tok_sh, inc=[("ve", 1)])
            G["pvE"] = tok_fin
            G["pvO"] = tok_lr + tok_ob
            G["pB"] = tok_fin
        tok_kvld2 = tok_pv_last[63] + tok_exp_g[63]

    if taps:
        p.dma(od.rearrange("(dc p) t -> p dc t", p=128),
              c_o[:].rearrange("p (dc t) -> p dc t", dc=8),
              sem="s3", wait=tok_fin)

    if "D" not in phases:
        _finish(nc, p)
        return nc

    tok_out_dma = {}
    tok_stage_free = tok_kvld2 + tok_fin
    wgu_free = {0: tok_kvld2, 1: tok_kvld2}
    wd_free = {0: tok_kvld2, 1: tok_kvld2}
    sg_free = {}
    for ti in range(nd):
        t0 = ti * NT
        tok_a = p.dma(d_xs0[:].rearrange("p (dc t) -> p dc t", dc=8),
                      x2d.rearrange("(dc p) t -> p dc t", p=128)
                      [:, :, t0:t0 + NT], wait=tok_x2_all + tok_stage_free)
        tok_b2 = p.dma(d_xs1a[:].rearrange("p (dc t) -> p dc t", dc=6),
                       x2d.rearrange("(dc p) t -> p dc t", p=128)
                       [:, 0:6, HALF + t0:HALF + t0 + NT])
        tok_b2 = tok_b2 + p.dma(
            d_xs1b[:].rearrange("p (dc t) -> p dc t", dc=2),
            x2d.rearrange("(dc p) t -> p dc t", p=128)
            [:, 6:8, HALF + t0:HALF + t0 + NT])
        for dc in range(8):
            wt = (tok_a + tok_b2) if dc == 0 else ()
            p.op("vector", lambda e, d=dc: e.tensor_scalar_mul(
                out=d_xs0[:, d * NT:(d + 1) * NT],
                in0=d_xs0[:, d * NT:(d + 1) * NT],
                scalar1=t_flags[:, 0:1]), wait=wt)
            p.op("vector", lambda e, d=dc: e.scalar_tensor_tensor(
                out=d_xs0[:, d * NT:(d + 1) * NT],
                in0=dxs1(d), scalar=t_flags[:, 1:2],
                in1=d_xs0[:, d * NT:(d + 1) * NT], op0=OP.mult, op1=OP.add))
        tok_x2sel = p.mark("vector", [("ve", 1)])

        for oc in range(8):
            bank, bname = (pAb, "pA") if oc % 2 == 0 else (pBb, "pB")
            for dc in range(8):
                wt = ()
                if dc == 0:
                    wt = G[bname] + tok_x2sel
                p.op("tensor", lambda e, d=dc, o_=oc, b=bank, t=t0: e.matmul(
                    out=b[:, 0:NT],
                    lhsT=c_wo[:, d * 1024 + o_ * 128: d * 1024 + o_ * 128 + 128],
                    rhs=c_o[:, d * HALF + t: d * HALF + t + NT],
                    start=(d == 0), stop=(d == 7)), wait=wt)
            tok_mm = p.mark("tensor", [("mm", 1)])
            tok_x3 = p.op("vector", lambda e, b=bank, o_=oc: e.tensor_add(
                out=d_x3[:, o_ * NT:(o_ + 1) * NT], in0=b[:, 0:NT],
                in1=d_xs0[:, o_ * NT:(o_ + 1) * NT]),
                wait=tok_mm, inc=[("ve", 1)])
            G[bname] = tok_x3
        tok_x3f = tok_x3

        def dxf(d):
            return d_x3[:, d * NT:(d + 1) * NT]

        tok_ln4, _ = layer_norm(dxf, lambda d: d_xn[:, d * NT:(d + 1) * NT],
                                d_xb, d_sq, d_abc, d_bbc, d_tmp, tok_x3f)
        for hc in range(24):
            buf = hc % 2
            tok_wl = p.dma(d_wg[:, buf * 1024:(buf + 1) * 1024],
                           wgbf[hc], sem=f"wvg{buf}", wait=wgu_free.get(buf, ()))
            tok_wl = tok_wl + p.dma(d_wu[:, buf * 1024:(buf + 1) * 1024],
                                    wubf[hc], sem=f"wvg{buf}")
            gb, gbn = (pAb[:, 0:NT], "pA") if buf == 0 else (sc4[:, 0:NT], "sc0")
            ub, ubn = (pBb[:, 0:NT], "pB") if buf == 0 else (pvO[:, 0:NT], "pvO")
            for dc in range(8):
                wt = (tok_ln4 + tok_wl + G[gbn]) if dc == 0 else ()
                p.op("tensor", lambda e, d=dc, b=buf, gb=gb: e.matmul(
                    out=gb,
                    lhsT=d_wg[:, b * 1024 + d * 128: b * 1024 + d * 128 + 128],
                    rhs=d_xn[:, d * NT:(d + 1) * NT],
                    start=(d == 0), stop=(d == 7)), wait=wt)
            for dc in range(8):
                wt = G[ubn] if dc == 0 else ()
                p.op("tensor", lambda e, d=dc, b=buf, ub=ub: e.matmul(
                    out=ub,
                    lhsT=d_wu[:, b * 1024 + d * 128: b * 1024 + d * 128 + 128],
                    rhs=d_xn[:, d * NT:(d + 1) * NT],
                    start=(d == 0), stop=(d == 7)), wait=wt)
            tok_gu = p.mark("tensor", [("mm", 1)])
            wgu_free[buf] = tok_gu
            sgs = d_sg[:, (hc % 2) * NT:(hc % 2) * NT + NT]
            tok_sg = p.op("scalar", lambda e, gb=gb, s=sgs: e.activation(
                out=s, in_=gb, func=AF.Silu, scale=f32(DSC)),
                wait=tok_gu + sg_free.get(hc % 2, ()), inc=[("ac", 1)])
            tok_h = p.op("vector", lambda e, h=hc, ub=ub, s=sgs: e.tensor_mul(
                out=d_h[:, h * NT:(h + 1) * NT], in0=ub, in1=s),
                wait=tok_sg, inc=[("ve", 1)])
            sg_free[hc % 2] = tok_h
            G[gbn] = tok_sg
            G[ubn] = tok_h
        tok_hdone = tok_h
        for oc in range(8):
            buf = oc % 2
            tok_wd = p.dma(d_wd[:, buf * 3072:buf * 3072 + 1536],
                           wdbf[oc][:, 0:1536], sem=f"wvd{buf}",
                           wait=wd_free.get(buf, ()))
            tok_wd = tok_wd + p.dma(d_wd[:, buf * 3072 + 1536:(buf + 1) * 3072],
                                    wdbf[oc][:, 1536:3072], sem=f"wvd{buf}")
            bank, bname = (pAb, "pA") if oc % 2 == 0 else (pBb, "pB")
            for hc in range(24):
                wt = (tok_hdone + tok_wd + G[bname]) if hc == 0 else ()
                p.op("tensor", lambda e, h=hc, b=bank, bu=buf: e.matmul(
                    out=b[:, 0:NT],
                    lhsT=d_wd[:, bu * 3072 + h * 128: bu * 3072 + h * 128 + 128],
                    rhs=d_h[:, h * NT:(h + 1) * NT],
                    start=(h == 0), stop=(h == 23)), wait=wt)
            tok_mm = p.mark("tensor", [("mm", 1)])
            tok_o = p.op("vector", lambda e, b=bank, o_=oc: e.scalar_tensor_tensor(
                out=d_out[:, (o_ % 2) * NT:(o_ % 2) * NT + NT],
                in0=b[:, 0:NT], scalar=DSC * DSC,
                in1=d_x3[:, o_ * NT:(o_ + 1) * NT], op0=OP.mult, op1=OP.add),
                wait=tok_mm + tok_out_dma.get(oc % 2, ()), inc=[("ve", 1)])
            G[bname] = tok_o
            wd_free[buf] = tok_o
            tok_out_dma[oc % 2] = p.dma(outT[oc * 128:(oc + 1) * 128, t0:t0 + NT],
                                        d_out[:, (oc % 2) * NT:(oc % 2) * NT + NT],
                                        sem=f"ow{oc % 2}", wait=tok_o,
                                        eng="gpsimd")
        tok_stage_free = tok_x3f

    _finish(nc, p)
    return nc


def _finish(nc, p):
    p.op("sync", lambda e: e.nop(),
         wait=[(s, v) for s, v in p.cnt.items() if v > 0])
    import contextlib
    stack = contextlib.ExitStack()
    sems = {s: stack.enter_context(nc.semaphore(s)) for s in p.cnt}
    with nc.Block() as block:
        _replay(block, p, sems)
    stack.close()


def eo_off(ch):
    return 256 + ch * 128


def _host_constants():
    pos = np.arange(S, dtype=np.float64)
    inv = 1.0 / (10000.0 ** (np.arange(0, HD, 2, dtype=np.float64) / HD))
    ang = pos[:, None] * inv[None, :]
    cos64 = np.tile(np.cos(ang), (1, 2)).astype(np.float32)
    sin64 = np.tile(np.sin(ang), (1, 2)).astype(np.float32)
    sgn = np.ones(64, np.float32)
    sgn[:32] = -1.0
    sinp = sin64 * sgn[None, :]
    cosT_ = np.ascontiguousarray(np.concatenate([cos64.T, cos64.T], axis=0))
    sinT_ = np.ascontiguousarray(np.concatenate([sinp.T, sinp.T], axis=0))
    r = np.zeros((128, 128), np.float32)
    for h in range(2):
        for i in range(64):
            r[h * 64 + (i + 32) % 64, h * 64 + i] = 1.0
    r128_ = np.ascontiguousarray(r.astype(bf16))
    sh = np.zeros((128, 128), np.float32)
    for k in range(64):
        sh[k, 64 + k] = 1.0
    shmat_ = np.ascontiguousarray(sh.astype(bf16))
    k_i = np.arange(128)[:, None]
    q_i = np.arange(128)[None, :]
    mprev = (k_i >= q_i).astype(np.float32)
    mcur = (k_i <= q_i).astype(np.float32)
    blockm = np.concatenate([mprev, mcur], axis=1)
    maskA_ = np.ascontiguousarray(np.tile(blockm, (1, 8)).astype(bf16))
    block0 = np.concatenate([0 * mprev, mcur], axis=1)
    half = np.concatenate([block0] + [blockm] * 3, axis=1)
    maskA0_ = np.ascontiguousarray(np.tile(half, (1, 2)).astype(bf16))
    return cosT_, sinT_, r128_, maskA_, maskA0_, shmat_


def make_in_maps(L0_attn_a_hidden_states, L0_attn_a_Wq, L0_attn_a_bq,
                 L0_attn_a_Wk, L0_attn_a_bk, L0_attn_a_Wv, L0_attn_a_bv,
                 L0_attn_a_Wo, L0_attn_a_bo, L0_attn_b_ln_gamma,
                 L0_attn_b_ln_beta, L0_attn_b_Wq, L0_attn_b_Wk, L0_attn_b_Wv,
                 L0_attn_b_Wo, L0_ffn_w_gate, L0_ffn_w_up, L0_ffn_w_down):
    x = np.asarray(L0_attn_a_hidden_states, np.float32)
    cosT_, sinT_, r128_, maskA_, maskA0_, shmat_ = _host_constants()

    cols = np.zeros((128, 48), np.float32)
    cols[:, 0:8] = np.asarray(L0_attn_a_bq, np.float32).reshape(8, 128).T
    cols[:, 8:16] = np.asarray(L0_attn_a_bk, np.float32).reshape(8, 128).T
    cols[:, 16:24] = np.asarray(L0_attn_a_bo, np.float32).reshape(8, 128).T
    cols[:, 24:32] = np.asarray(L0_attn_b_ln_gamma, np.float32).reshape(8, 128).T
    cols[:, 32:40] = np.asarray(L0_attn_b_ln_beta, np.float32).reshape(8, 128).T
    bvrow_ = np.asarray(L0_attn_a_bv, np.float32).reshape(1, DM)

    shared = {
        "waqT": np.ascontiguousarray(np.asarray(L0_attn_a_Wq, np.float32).T),
        "wakT": np.ascontiguousarray(np.asarray(L0_attn_a_Wk, np.float32).T),
        "wavT": np.ascontiguousarray(np.asarray(L0_attn_a_Wv, np.float32).T),
        "waoT": np.ascontiguousarray(np.asarray(L0_attn_a_Wo, np.float32).T),
        "acols": cols, "bvrow": bvrow_, "r128": r128_,
        "cosT": cosT_, "sinT": sinT_, "maskA": maskA_, "maskA0": maskA0_,
        "shmat": shmat_,
        "wbq": np.ascontiguousarray(np.asarray(L0_attn_b_Wq, np.float32)),
        "wbk": np.ascontiguousarray(np.asarray(L0_attn_b_Wk, np.float32)),
        "wbv": np.ascontiguousarray(np.asarray(L0_attn_b_Wv, np.float32)),
        "wbo": np.ascontiguousarray(np.asarray(L0_attn_b_Wo, np.float32)),
        "wgT": np.ascontiguousarray(np.asarray(L0_ffn_w_gate, np.int32).T),
        "wuT": np.ascontiguousarray(np.asarray(L0_ffn_w_up, np.int32).T),
        "wdT": np.ascontiguousarray(np.asarray(L0_ffn_w_down, np.int32).T),
    }

    in_maps = []
    for c in range(8):
        b, h = c // 2, c % 2
        fl = np.zeros((128, 4), np.float32)
        fl[:, 0] = 1.0 if h == 0 else 0.0
        fl[:, 1] = 1.0 if h == 1 else 0.0
        m = dict(shared)
        m["x0T"] = np.ascontiguousarray(x[b].T)
        m["flags"] = fl
        m["flagsb"] = np.ascontiguousarray(fl.astype(bf16))
        in_maps.append(m)

    return in_maps


def kernel(**inputs):
    in_maps = make_in_maps(**inputs)
    nc = _build(taps=False)
    res = run_bass_kernel_spmd(nc, in_maps, core_ids=list(range(8)))
    out = np.empty((BATCH, S, DM), np.float32)
    for c in range(8):
        b, h = c // 2, c % 2
        out[b, h * HALF:(h + 1) * HALF, :] = res.results[c]["outT"].T
    return out
